# version 43
# baseline (speedup 1.0000x reference)
import contextlib
import numpy as np
import concourse.bass as bass
import concourse.mybir as mybir
from concourse.bass_utils import run_bass_kernel_spmd

F32 = mybir.dt.float32
BF16 = mybir.dt.bfloat16
F16 = mybir.dt.float16
AF = mybir.ActivationFunctionType
ALU = mybir.AluOpType
AX = mybir.AxisListType

T = 2048
TFULL = 4096
NT = 16
C = 1024
NCC = 8
TEC = 2052
EPS = 1e-5
K_CLAMP = 60.0
K_EPS = 1e-16

BUILD_UPTO = 99
S2_LEVEL = 4
RG_GROUPS = [[0, 1], [2, 3], [4, 5], [6, 7]]

ST_TM1, ST_E, ST_LAM, ST_FW, ST_NEGE, ST_N2047E = 0, 8, 16, 24, 32, 40
ST_MEAN1, ST_VAR1, ST_RSTD1, ST_NMR, ST_SY, ST_BY, ST_W1 = 48, 64, 80, 96, 112, 128, 144
ST_COLS = 160
VP_TD, VP_TF, VP_TM, VP_NW, VP_NB, VP_QB, VP_KB, VP_HM = 0, 8, 16, 24, 32, 40, 44, 48


class Op:
    __slots__ = ("eng", "sem", "emit", "deps", "inc_k", "val", "needed", "sec",
                 "async_cls", "dwaits")

    def __init__(self, eng, sem, emit, deps, inc_k, sec, async_cls):
        self.eng = eng
        self.sem = sem
        self.emit = emit
        self.deps = deps
        self.dwaits = {}
        self.inc_k = inc_k
        self.val = None
        self.needed = False
        self.sec = sec
        self.async_cls = async_cls


class Section:

    ENGS = ("tensor", "vector", "scalar", "gpsimd", "sync")
    COUNTER = [0]

    def __init__(self, nc, sems):
        self.nc = nc
        self.sems = sems
        self.ops = {e: [] for e in self.ENGS}
        Section.COUNTER[0] += 1
        self.sec_id = Section.COUNTER[0]

    def add(self, eng, emit, deps=(), dma=False, coll=False, sem=None):
        async_cls = dma or coll
        if async_cls:
            skey = sem or "io"
            inc_k = 16 if dma else 1
        else:
            skey = eng
            inc_k = 1
        op = Op(eng, skey, emit, [], inc_k, self.sec_id, async_cls)
        for d in deps:
            if d is None:
                continue
            if d.async_cls:
                op.dwaits[d.sem] = self.sems[d.sem].n
            elif d.sec == self.sec_id:
                d.needed = True
                op.deps.append(d)
        if async_cls:
            if self.sems[skey].n > 0:
                op.dwaits[skey] = max(op.dwaits.get(skey, 0), self.sems[skey].n)
            self.sems[skey].n += inc_k
            op.val = self.sems[skey].n
        self.ops[eng].append(op)
        return op

    def emit(self):
        for e in self.ENGS:
            for op in self.ops[e]:
                if not op.async_cls and op.needed:
                    self.sems[e].n += 1
                    op.val = self.sems[e].n
        with self.nc.Block() as blk:
            for ename in self.ENGS:
                if not self.ops[ename]:
                    continue

                def make_body(ename):
                    def body(eng):
                        for op in self.ops[ename]:
                            waits = dict(op.dwaits)
                            for d in op.deps:
                                if d.val is None:
                                    raise RuntimeError("dep without inc")
                                waits[d.sem] = max(waits.get(d.sem, 0), d.val)
                            for de, dv in waits.items():
                                if dv > 0:
                                    eng.wait_ge(self.sems[de].h, dv)
                            ins = op.emit(eng)
                            if op.val is not None:
                                ins.then_inc(self.sems[op.sem].h, op.inc_k)
                    return body

                getattr(blk, ename)(make_body(ename))


class Sem:
    __slots__ = ("h", "n")

    def __init__(self, h):
        self.h = h
        self.n = 0


def build_nc():
    nc = bass.Bass("TRN2", target_bir_lowering=False)

    x_ext = nc.dram_tensor("x_ext", [T + 1, C], F32, kind="ExternalInput").ap()
    wkT = nc.dram_tensor("wkT", [C, C], F32, kind="ExternalInput").ap()
    wvT = nc.dram_tensor("wvT", [C, C], F32, kind="ExternalInput").ap()
    wrT = nc.dram_tensor("wrT", [C, C], F32, kind="ExternalInput").ap()
    woT = nc.dram_tensor("woT", [C, C], F32, kind="ExternalInput").ap()
    qwT = nc.dram_tensor("qwT", [C, 512], F32, kind="ExternalInput").ap()
    kwT = nc.dram_tensor("kwT", [C, 512], F32, kind="ExternalInput").ap()
    vwT = nc.dram_tensor("vwT", [C, 512], F32, kind="ExternalInput").ap()
    owT0 = nc.dram_tensor("owT0", [C, C], F32, kind="ExternalInput").ap()
    owT1 = nc.dram_tensor("owT1", [C, C], F32, kind="ExternalInput").ap()
    veca = nc.dram_tensor("veca", [128, 49], F32, kind="ExternalInput").ap()
    vb_row = nc.dram_tensor("vb_row", [1, 512], F32, kind="ExternalInput").ap()
    ob_cols = nc.dram_tensor("ob_cols", [128, 16], F32, kind="ExternalInput").ap()
    OUT = nc.dram_tensor("OUT", [T, C], F32, kind="ExternalOutput").ap()

    r1_spill = nc.dram_tensor("r1_spill", [T, C], F32).ap()
    bnd_dram = nc.dram_tensor("bnd_dram", [C], BF16).ap()
    cc1_snd = [nc.dram_tensor(f"cc1s{i}", [128, 2], F32) for i in range(NCC)]
    cc1_rcv = [nc.dram_tensor(f"cc1r{i}", [2, 128, 2], F32) for i in range(NCC)]
    cc2_snd = [nc.dram_tensor(f"cc2s{a}", [128, NCC * (T // 2)], BF16)
               for a in range(2)]
    cc2_rcv = [nc.dram_tensor(f"cc2r{a}", [2, 128, NCC * (T // 2)], BF16)
               for a in range(2)]
    cc3_snd_a = nc.dram_tensor("cc3sa", [128, 2 * TFULL], BF16)
    cc3_rcv_a = nc.dram_tensor("cc3ra", [2, 128, 2 * TFULL], BF16)
    cc3_snd_b1 = nc.dram_tensor("cc3sb1", [128, 2 * T], BF16)
    cc3_rcv_b1 = nc.dram_tensor("cc3rb1", [2, 128, 2 * T], BF16)
    cc3_snd_b2 = nc.dram_tensor("cc3sb2", [128, 2 * T], BF16)
    cc3_rcv_b2 = nc.dram_tensor("cc3rb2", [2, 128, 2 * T], BF16)
    rr_dram = [nc.dram_tensor(f"rrd{h}", [1, 512], F32) for h in range(2)]
    RG = RG_GROUPS

    with contextlib.ExitStack() as _top:
        ec = _top.enter_context
        XR = ec(nc.sbuf_tensor("XR", [128, NT * C], F32))
        vp = ec(nc.sbuf_tensor("vp", [128, 52], F32))
        stat = ec(nc.sbuf_tensor("stat", [128, ST_COLS], F32))
        ident_b = ec(nc.sbuf_tensor("ident_b", [128, 128], BF16))
        ident_f = ec(nc.sbuf_tensor("ident_f", [128, 128], F32))
        ones_r1 = ec(nc.sbuf_tensor("ones_r1", [1, 128], BF16))
        ones_r1f = ec(nc.sbuf_tensor("ones_r1f", [1, 128], F32))
        ones_c1 = ec(nc.sbuf_tensor("ones_c1", [128, 1], BF16))
        ones128 = ec(nc.sbuf_tensor("ones128", [128, 128], BF16))
        carry = ec(nc.sbuf_tensor("carry", [128, 24], F32))
        st6 = ec(nc.sbuf_tensor("st6", [128, 12], F32))
        mv = ec(nc.sbuf_tensor("mv", [128, 4], F32))
        cbias = ec(nc.sbuf_tensor("cbias", [128, 2], F32))
        cst = ec(nc.sbuf_tensor("cst", [128, 1], F32))
        nc.const_aps.register(F32, EPS, cst[:, 0:1])
        sems = {n: Sem(ec(nc.semaphore("q_" + n)))
                for n in ("tensor", "vector", "scalar", "gpsimd", "sync",
                          "ld", "ld2", "w", "io", "io2", "cc", "cc1d", "spl", "rb",
                          "x0", "x1", "x2", "x3", "x4", "x5", "x6", "x7")}

        XRb = XR.ap().bitcast(BF16)

        def QT(j):
            return XRb[:, j * TFULL:(j + 1) * TFULL]

        def KT(j):
            return XRb[:, 16384 + j * TFULL:16384 + (j + 1) * TFULL]

        s = Section(nc, sems)
        o_m1 = s.add("gpsimd", lambda e: e.memset(ones_r1[:, :], 1.0))
        o_m2 = s.add("gpsimd", lambda e: e.memset(ones_r1f[:, :], 1.0))
        o_m3 = s.add("gpsimd", lambda e: e.memset(ones_c1[:, :], 1.0))
        o_m4 = s.add("gpsimd", lambda e: e.memset(ones128[:, :], 1.0))
        o_m5 = s.add("gpsimd", lambda e: e.memset(carry[:, :], 0.0))
        o_m6 = s.add("gpsimd", lambda e: e.memset(cst[:, :], EPS))
        o_vp = s.add("sync", lambda e: e.dma_start(out=vp[:, 0:49], in_=veca), dma=True, sem="ld")
        xload = []
        for i in range(NT):
            xload.append(s.add("sync", lambda e, i=i: e.dma_start(
                out=XR[:, i * C:(i + 1) * C],
                in_=x_ext[1 + i * 128:1 + (i + 1) * 128, :]), dma=True,
                sem="ld" if i % 2 == 0 else "ld2"))
        o_id = s.add("gpsimd", lambda e: e.affine_select(
            ident_b[:, :], ones128[:, :], [[1, 128]],
            compare_op=ALU.is_equal, fill=0.0, base=0, channel_multiplier=-1),
            deps=[o_m4])
        o_idf = s.add("vector", lambda e: e.tensor_copy(ident_f[:, :], ident_b[:, :]),
                      deps=[o_id])
        o_tm1 = s.add("vector", lambda e: e.tensor_scalar(
            stat[:, ST_TM1:ST_TM1 + 8], vp[:, VP_TM:VP_TM + 8], -1.0, 1.0,
            op0=ALU.mult, op1=ALU.add), deps=[o_vp])
        o_e = s.add("scalar", lambda e: e.activation(
            stat[:, ST_E:ST_E + 8], vp[:, VP_TD:VP_TD + 8], AF.Exp), deps=[o_vp])
        o_lam = s.add("scalar", lambda e: e.activation(
            stat[:, ST_LAM:ST_LAM + 8], stat[:, ST_E:ST_E + 8], AF.Exp, scale=-1.0),
            deps=[o_e])
        o_fw = s.add("scalar", lambda e: e.activation(
            stat[:, ST_FW:ST_FW + 8], vp[:, VP_TF:VP_TF + 8], AF.Exp), deps=[o_vp])
        o_nege = s.add("vector", lambda e: e.tensor_scalar_mul(
            stat[:, ST_NEGE:ST_NEGE + 8], stat[:, ST_E:ST_E + 8], -1.0), deps=[o_e])
        o_n2047 = s.add("vector", lambda e: e.tensor_scalar_mul(
            stat[:, ST_N2047E:ST_N2047E + 8], stat[:, ST_E:ST_E + 8], -2047.0),
            deps=[o_e])

        last_coll = [None]
        with contextlib.ExitStack() as _sx0:
            xmT = _sx0.enter_context(nc.sbuf_tensor("xmT", [128, NCC * T], BF16))
            rwkvT = _sx0.enter_context(nc.sbuf_tensor("rwkvT", [128, NCC * T], BF16))
            with contextlib.ExitStack() as _s1x:
                ec1 = _s1x.enter_context
                xnT = ec1(nc.sbuf_tensor("xnT", [128, NCC * TEC], BF16))
                xn_tile = ec1(nc.sbuf_tensor("xn_tile", [128, 2 * C], BF16))
                xb_row = ec1(nc.sbuf_tensor("xb_row", [1, C], F32))
                xb_st = ec1(nc.sbuf_tensor("xb_st", [1, 12], F32))
                xb_mv = ec1(nc.sbuf_tensor("xb_mv", [1, 6], F32))
                xb_bf = ec1(nc.sbuf_tensor("xb_bf", [1, C], BF16))
                ptr = ec1(nc.psum_tensor("ptr", [128, 2 * C], BF16))
                _dummy = 1
                stats_done = []
                prev = None
                for i in range(NT):
                    d = [xload[i]] + ([prev] if prev else [])
                    a = s.add("vector", lambda e, i=i: e.bn_stats(
                        st6[:, 0:6], XR[:, i * C:i * C + 512]), deps=d)
                    b = s.add("vector", lambda e, i=i: e.bn_stats(
                        st6[:, 6:12], XR[:, i * C + 512:i * C + 1024]), deps=[a])
                    g = s.add("vector", lambda e: e.bn_aggr(mv[:, 0:2], st6[:, :]),
                              deps=[a, b])
                    cm = s.add("vector", lambda e, i=i: e.tensor_copy(
                        stat[:, ST_MEAN1 + i:ST_MEAN1 + i + 1], mv[:, 0:1]), deps=[g])
                    cv = s.add("vector", lambda e, i=i: e.tensor_copy(
                        stat[:, ST_VAR1 + i:ST_VAR1 + i + 1], mv[:, 1:2]), deps=[g])
                    prev = cv
                    stats_done.extend([cm, cv])
                o_ln = s.add("scalar", lambda e: e.activation(
                    stat[:, ST_RSTD1:ST_RSTD1 + 16], stat[:, ST_VAR1:ST_VAR1 + 16],
                    AF.Ln, bias=EPS), deps=stats_done)
                o_rstd = s.add("scalar", lambda e: e.activation(
                    stat[:, ST_RSTD1:ST_RSTD1 + 16], stat[:, ST_RSTD1:ST_RSTD1 + 16],
                    AF.Exp, scale=-0.5), deps=[o_ln])
                o_nmr0 = s.add("vector", lambda e: e.tensor_mul(
                    stat[:, ST_NMR:ST_NMR + 16], stat[:, ST_MEAN1:ST_MEAN1 + 16],
                    stat[:, ST_RSTD1:ST_RSTD1 + 16]), deps=[o_rstd])
                o_nmr = s.add("vector", lambda e: e.tensor_scalar_mul(
                    stat[:, ST_NMR:ST_NMR + 16], stat[:, ST_NMR:ST_NMR + 16], -1.0),
                    deps=[o_nmr0])
                o_xb = s.add("sync", lambda e: e.dma_start(
                    out=xb_row[:, :], in_=x_ext[0:1, :]), dma=True, sem="ld")
                a = s.add("vector", lambda e: e.bn_stats(xb_st[:, 0:6], xb_row[:, 0:512]),
                          deps=[o_xb])
                b = s.add("vector", lambda e: e.bn_stats(xb_st[:, 6:12], xb_row[:, 512:1024]),
                          deps=[a])
                g = s.add("vector", lambda e: e.bn_aggr(xb_mv[:, 0:2], xb_st[:, :]),
                          deps=[a, b])
                l1 = s.add("scalar", lambda e: e.activation(
                    xb_mv[:, 2:3], xb_mv[:, 1:2], AF.Ln, bias=EPS), deps=[g])
                l2 = s.add("scalar", lambda e: e.activation(
                    xb_mv[:, 3:4], xb_mv[:, 2:3], AF.Exp, scale=-0.5), deps=[l1])
                l3 = s.add("vector", lambda e: e.tensor_mul(
                    xb_mv[:, 4:5], xb_mv[:, 0:1], xb_mv[:, 3:4]), deps=[l2, g])
                l4 = s.add("vector", lambda e: e.tensor_scalar_mul(
                    xb_mv[:, 4:5], xb_mv[:, 4:5], -1.0), deps=[l3])
                l5 = s.add("scalar", lambda e: e.activation(
                    xb_bf[:, :], xb_row[:, :], AF.Identity,
                    bias=xb_mv[:, 4:5], scale=xb_mv[:, 3:4]), deps=[l4])
                l6 = s.add("sync", lambda e: e.dma_start(
                    out=bnd_dram.rearrange("(o c) -> o c", o=1), in_=xb_bf[0:1, :]),
                    deps=[l5], dma=True, sem="ld")
                l7s = []
                for cc in range(NCC):
                    l7s.append(s.add("sync", lambda e, cc=cc: e.dma_start(
                        out=xnT[:, cc * TEC:cc * TEC + 1],
                        in_=bnd_dram[cc * 128:(cc + 1) * 128].rearrange("(p o) -> p o", o=1)),
                        deps=[l6], dma=True, sem="ld"))
                l7 = l7s[-1]
                ap_ops = [None] * NT
                tr_ops = [None] * NT
                cp_ops = [None] * NT
                for i in range(NT):
                    d = [o_rstd, o_nmr, xload[i]]
                    if i >= 2:
                        d.append(tr_ops[i - 2])
                    ap_ops[i] = s.add("scalar", lambda e, i=i: e.activation(
                        xn_tile[:, (i % 2) * C:(i % 2) * C + C],
                        XR[:, i * C:(i + 1) * C], AF.Identity,
                        bias=stat[:, ST_NMR + i:ST_NMR + i + 1],
                        scale=stat[:, ST_RSTD1 + i:ST_RSTD1 + i + 1]), deps=d)

                    def em_tr(e, i=i):
                        for cc in range(NCC):
                            ins = e.transpose(
                                ptr[:, (i % 2) * C + cc * 128:(i % 2) * C + cc * 128 + 128],
                                xn_tile[:, (i % 2) * C + cc * 128:(i % 2) * C + cc * 128 + 128],
                                ident_b[:, :])
                        return ins
                    d = [ap_ops[i], o_id]
                    if i >= 2:
                        d.append(cp_ops[i - 2])
                    tr_ops[i] = s.add("tensor", em_tr, deps=d)
                    cp_ops[i] = s.add("vector", lambda e, i=i: e.tensor_copy(
                        xnT.ap().rearrange("p (cc t) -> p cc t", cc=NCC)[:, :, 1 + i * 128:1 + (i + 1) * 128],
                        ptr.ap().rearrange("p (b cc t) -> p b cc t", b=2, cc=NCC)[:, i % 2]),
                        deps=[tr_ops[i]])
                mix1 = []
                for cc in range(NCC):
                    mix1.append(s.add("vector", lambda e, cc=cc: e.tensor_scalar_mul(
                        xmT[:, cc * T:(cc + 1) * T],
                        xnT[:, cc * TEC:cc * TEC + T],
                        stat[:, ST_TM1 + cc:ST_TM1 + cc + 1]),
                        deps=[o_tm1] + l7s + cp_ops))
                mix_done = []
                for cc in range(NCC):
                    mix_done.append(s.add("vector", lambda e, cc=cc: e.scalar_tensor_tensor(
                        xmT[:, cc * T:(cc + 1) * T],
                        xnT[:, cc * TEC + 1:cc * TEC + 1 + T],
                        vp[:, VP_TM + cc:VP_TM + cc + 1],
                        xmT[:, cc * T:(cc + 1) * T],
                        op0=ALU.mult, op1=ALU.add), deps=[mix1[cc]]))
                s.emit()

            if BUILD_UPTO == 12:
                _tap_out(nc, sems, XR, OUT, [])
                return nc
            with contextlib.ExitStack() as _s2x:
                ec2 = _s2x.enter_context
                wbuf = ec2(nc.sbuf_tensor("wbuf", [128, 2 * 2 * C], BF16))
                sl1 = ec2(nc.sbuf_tensor("sl1", [128, 2 * T], F32))
                sl2 = ec2(nc.sbuf_tensor("sl2", [128, 2 * T], F32))
                sl3 = ec2(nc.sbuf_tensor("sl3", [128, 2 * T], F32))
                sl6 = ec2(nc.sbuf_tensor("sl6", [128, 2 * T], F32))
                lamb = ec2(nc.sbuf_tensor("lamb", [128, T], BF16))
                psA = ec2(nc.psum_tensor("psA", [128, 1024], F32))
                psB = ec2(nc.psum_tensor("psB", [128, 1024], F32))
                psR = ec2(nc.psum_tensor("psR", [128, 1024], F32))
                _dummy = 1
                s = Section(nc, sems)
                rwkv_ops = [None] * NCC
                prev = {}
                E60 = float(np.exp(60.0))

                def emit_cc(oc):
                    if ("ccargs", oc) not in prev:
                        return
                    (snd, CM, xs) = prev.pop(("ccargs", oc))
                    cc = s.add("gpsimd", lambda e, oc=oc: e.collective_compute(
                        "AllGather", ALU.bypass, replica_groups=RG,
                        ins=[cc1_snd[oc].ap().opt()],
                        outs=[cc1_rcv[oc].ap().opt()]),
                        deps=[snd], coll=True, sem=xs)
                    last_coll[0] = cc
                    rcv = s.add("sync", lambda e, oc=oc, CM=CM: e.dma_start(
                        out=carry[:, CM:CM + 2], in_=cc1_rcv[oc].ap()[0]),
                        deps=[cc, prev.get(("cmask", oc - 2))], dma=True, sem=xs)
                    cmask = s.add("vector", lambda e, CM=CM: e.tensor_scalar_mul(
                        carry[:, CM + 2:CM + 4], carry[:, CM:CM + 2],
                        vp[:, VP_HM:VP_HM + 1]),
                        deps=[rcv, o_vp])
                    prev[("cmask", oc)] = cmask

                def emit_tail(oc):
                    if ("tailargs", oc) not in prev:
                        return
                    (s1, s2, s3, s6, lam0, CM,
                     w1b, w2e, er, ck) = prev.pop(("tailargs", oc))
                    cmask = prev[("cmask", oc)]
                    c1 = s.add("vector", lambda e, s2=s2, CM=CM:
                               e.scalar_tensor_tensor(
                                   s2, lamb[:, :], carry[:, CM + 2:CM + 3], s2,
                                   op0=ALU.mult, op1=ALU.add),
                               deps=[w1b, lam0, cmask])
                    c2 = s.add("vector", lambda e, s1=s1, CM=CM:
                               e.scalar_tensor_tensor(
                                   s1, lamb[:, :], carry[:, CM + 3:CM + 4], s1,
                                   op0=ALU.mult, op1=ALU.add),
                               deps=[w2e, lam0, cmask, c1])
                    prev[("lamused", oc)] = c2
                    dd = s.add("vector", lambda e, s1=s1, s3=s3, s6=s6:
                               e.scalar_tensor_tensor(
                                   s3, s6, 1.0, s1, op0=ALU.add, op1=ALU.mult),
                               deps=[er, c2])
                    prev[("dd", oc)] = dd
                    prev[("t2", oc)] = (s2, s3, c1, ck, dd)

                def emit_tail2a(oc):
                    if ("t2", oc) not in prev:
                        return
                    (s2, s3, c1, ck, dd) = prev[("t2", oc)]
                    lnd = s.add("scalar", lambda e, s3=s3: e.activation(s3, s3, AF.Ln),
                                deps=[dd])
                    invd = s.add("scalar", lambda e, s3=s3: e.activation(
                        s3, s3, AF.Exp, scale=-1.0), deps=[lnd])
                    prev[("invd", oc)] = invd

                def emit_tail2b(oc):
                    if ("t2", oc) not in prev:
                        return
                    (s2, s3, c1, ck, dd) = prev.pop(("t2", oc))
                    invd = prev.pop(("invd", oc))
                    rw = s.add("vector", lambda e, s2=s2, s3=s3, oc=oc: e.tensor_mul(
                        rwkvT[:, oc * T:(oc + 1) * T], s2, s3),
                        deps=[invd, c1, ck])
                    prev[("rw", oc)] = rw
                    rwkv_ops[oc] = rw

                def emit_wload(j):
                    if j >= NCC:
                        return
                    b2 = (j % 2) * 2 * C
                    wd = [prev.get(("rmm", j - 2))] if j >= 2 else []
                    prev[("wkd", j)] = s.add("gpsimd", lambda e, j=j, b2=b2: e.dma_start(
                        out=wbuf[:, b2:b2 + C].rearrange("p (ic m) -> p ic m", ic=NCC),
                        in_=wkT[:, j * 128:(j + 1) * 128]
                        .rearrange("(ic p) m -> p ic m", p=128)),
                        deps=wd, dma=True, sem="w")
                    prev[("wvd", j)] = s.add("gpsimd", lambda e, j=j, b2=b2: e.dma_start(
                        out=wbuf[:, b2 + C:b2 + 2 * C].rearrange("p (ic m) -> p ic m", ic=NCC),
                        in_=wvT[:, j * 128:(j + 1) * 128]
                        .rearrange("(ic p) m -> p ic m", p=128)),
                        deps=wd, dma=True, sem="w")

                for oc in range(NCC):
                    par = oc % 2
                    s1 = sl1[:, par * T:par * T + T]
                    s2 = sl2[:, par * T:par * T + T]
                    s3 = sl3[:, par * T:par * T + T]
                    s6 = sl6[:, par * T:par * T + T]
                    trash = rwkvT[:, oc * T:(oc + 1) * T]
                    ecol = stat[:, ST_E + oc:ST_E + oc + 1]
                    negecol = stat[:, ST_NEGE + oc:ST_NEGE + oc + 1]
                    n2047col = stat[:, ST_N2047E + oc:ST_N2047E + oc + 1]
                    fwcol = stat[:, ST_FW + oc:ST_FW + oc + 1]
                    lamcol = stat[:, ST_LAM + oc:ST_LAM + oc + 1]
                    base = (oc % 2) * 2 * C
                    CM = 16 + 4 * par

                    if oc == 0:
                        emit_wload(0)
                    emit_wload(oc + 1)
                    wk_d = prev[("wkd", oc)]
                    wv_d = prev[("wvd", oc)]
                    o_iota = s.add("gpsimd", lambda e, s3=s3: e.iota(
                        s3, [[1, T]], base=0, channel_multiplier=0,
                        allow_small_or_imprecise_dtypes=True),
                        deps=[prev.get(("rw", oc - 2))])


                    def emit_mms_h(e, ps, woff, h, base=base):
                        for qt in range(2 * h, 2 * h + 2):
                            for ic in range(NCC):
                                ins = e.matmul(
                                    ps[:, (qt - 2 * h) * 512:(qt - 2 * h + 1) * 512],
                                    wbuf[:, base + woff + ic * 128:base + woff + ic * 128 + 128],
                                    xmT[:, ic * T + qt * 512:ic * T + (qt + 1) * 512],
                                    start=(ic == 0), stop=(ic == NCC - 1))
                        return ins

                    kmm_h = [None, None]
                    vmm_h = [None, None]
                    rmm_h = [None, None]
                    ek_h = [None, None]
                    vev_h = [None, None]
                    er_h = [None, None]
                    for h in (0, 1):
                        d = [wk_d] + mix_done
                        pk = prev.get(("ek", oc - 1, 1)) if h == 0 else ek_h[0]
                        if pk is not None:
                            d.append(pk)
                        kmm_h[h] = s.add("tensor",
                                         lambda e, h=h, f=emit_mms_h: f(e, psA, 0, h),
                                         deps=d)
                        ek_h[h] = s.add("scalar", lambda e, s1=s1, h=h: e.activation(
                            s1[:, h * 1024:(h + 1) * 1024], psA[:, :], AF.Exp),
                            deps=[kmm_h[h], prev.get(("dd", oc - 2))])
                        prev[("ek", oc, h)] = ek_h[h]
                    wr_d = s.add("gpsimd", lambda e, oc=oc, base=base: e.dma_start(
                        out=wbuf[:, base:base + C].rearrange("p (ic m) -> p ic m", ic=NCC),
                        in_=wrT[:, oc * 128:(oc + 1) * 128]
                        .rearrange("(ic p) m -> p ic m", p=128)),
                        deps=[kmm_h[1]], dma=True, sem="w")
                    emit_cc(oc - 1)
                    emit_tail(oc - 1)
                    for h in (0, 1):
                        d = [wv_d]
                        pv = prev.get(("vev", oc - 1, 1)) if h == 0 else vev_h[0]
                        if pv is not None:
                            d.append(pv)
                        vmm_h[h] = s.add("tensor",
                                         lambda e, h=h, f=emit_mms_h: f(e, psB, C, h),
                                         deps=d)
                        vev_h[h] = s.add("scalar", lambda e, s2=s2, h=h: e.activation(
                            s2[:, h * 1024:(h + 1) * 1024], psB[:, :], AF.Identity),
                            deps=[vmm_h[h], prev.get(("rw", oc - 2))])
                        prev[("vev", oc, h)] = vev_h[h]
                    for h in (0, 1):
                        d = [wr_d]
                        pr = prev.get(("er", oc - 1, 1)) if h == 0 else er_h[0]
                        if pr is not None:
                            d.append(pr)
                        rmm_h[h] = s.add("tensor",
                                         lambda e, h=h, f=emit_mms_h: f(e, psR, 0, h),
                                         deps=d)
                        er_h[h] = s.add("scalar", lambda e, s6=s6, h=h: e.activation(
                            s6[:, h * 1024:(h + 1) * 1024], psR[:, :], AF.Exp, scale=-1.0),
                            deps=[rmm_h[h], prev.get(("dd", oc - 2))])
                        prev[("er", oc, h)] = er_h[h]
                    ek = ek_h[1]
                    vev = vev_h[1]
                    er = er_h[1]
                    prev[("er", oc)] = er
                    prev[("vev", oc)] = vev
                    prev[("rmm", oc)] = rmm_h[1]
                    lam0 = s.add("scalar", lambda e, negecol=negecol, s3=s3: e.activation(
                        lamb[:, :], s3, AF.Exp, scale=negecol),
                        deps=[o_iota, o_nege, prev.get(("lamused", oc - 1))])
                    emit_tail2a(oc - 1)
                    clp = s.add("vector", lambda e, s1=s1: e.tensor_scalar_min(s1, s1, E60),
                                deps=[ek, prev.get(("dd", oc - 2))])
                    kv = s.add("vector", lambda e, s1=s1, s2=s2: e.tensor_mul(s2, s1, s2),
                               deps=[clp, vev])
                    emit_tail2b(oc - 1)
                    lrev = s.add("scalar", lambda e, s3=s3, ecol=ecol, n2047col=n2047col:
                                 e.activation(s3, s3, AF.Exp,
                                              scale=ecol, bias=n2047col),
                                 deps=[o_iota, o_e, o_n2047, lam0])
                    ckv = s.add("vector", lambda e, s2=s2, s3=s3, trash=trash, oc=oc:
                                e.scalar_tensor_tensor(
                                    trash, s2, 1.0, s3, op0=ALU.mult, op1=ALU.mult,
                                    accum_out=carry[:, 2 * oc:2 * oc + 1]),
                                deps=[kv, lrev, o_m5])
                    ck = s.add("vector", lambda e, s1=s1, s3=s3, trash=trash, oc=oc:
                               e.scalar_tensor_tensor(
                                   trash, s1, 1.0, s3, op0=ALU.mult, op1=ALU.mult,
                                   accum_out=carry[:, 2 * oc + 1:2 * oc + 2]),
                               deps=[ckv])
                    xs = f"x{oc}"
                    snd = s.add("sync", lambda e, oc=oc: e.dma_start(
                        out=cc1_snd[oc].ap(), in_=carry[:, 2 * oc:2 * oc + 2]),
                        deps=[ck], dma=True, sem=xs)
                    prev[("ccargs", oc)] = (snd, CM, xs)
                    sckv = s.add("vector", lambda e, s2=s2, s3=s3, lamcol=lamcol:
                                 e.tensor_tensor_scan(s3, lamcol.broadcast_to([128, T]),
                                                      s2, 0.0, op0=ALU.mult, op1=ALU.add),
                                 deps=[kv, ckv, ck, o_lam, lrev])
                    w1 = s.add("vector", lambda e, s2=s2, s3=s3, fwcol=fwcol:
                               e.scalar_tensor_tensor(
                                   s2[:, 1:T], s2[:, 1:T], fwcol, s3[:, 0:T - 1],
                                   op0=ALU.mult, op1=ALU.add), deps=[sckv, ck])
                    w1b = s.add("vector", lambda e, s2=s2, fwcol=fwcol:
                                e.tensor_scalar_mul(s2[:, 0:1], s2[:, 0:1], fwcol),
                                deps=[w1])
                    sck = s.add("vector", lambda e, s1=s1, s3=s3, lamcol=lamcol:
                                e.tensor_tensor_scan(s3, lamcol.broadcast_to([128, T]),
                                                     s1, 0.0, op0=ALU.mult, op1=ALU.add),
                                deps=[w1, ck])
                    w2 = s.add("vector", lambda e, s1=s1, s3=s3, fwcol=fwcol:
                               e.scalar_tensor_tensor(
                                   s1[:, 1:T], s1[:, 1:T], fwcol, s3[:, 0:T - 1],
                                   op0=ALU.mult, op1=ALU.add), deps=[sck])
                    w2b = s.add("vector", lambda e, s1=s1, fwcol=fwcol:
                                e.tensor_scalar_mul(s1[:, 0:1], s1[:, 0:1], fwcol),
                                deps=[w2])
                    w2e = s.add("vector", lambda e, s1=s1: e.tensor_scalar_add(
                        s1, s1, K_EPS), deps=[w2b])
                    prev[("tailargs", oc)] = (s1, s2, s3, s6, lam0, CM,
                                              w1b, w2e, er, ck)
                emit_cc(NCC - 1)
                emit_tail(NCC - 1)
                emit_tail2a(NCC - 1)
                emit_tail2b(NCC - 1)
                s.emit()

            if BUILD_UPTO == 13:
                _tap_out(nc, sems, XR, OUT, rwkv_ops)
                return nc
            with contextlib.ExitStack() as _s3x:
                ec3 = _s3x.enter_context
                wo_sb = ec3(nc.sbuf_tensor("wo_sb", [128, NCC * C], BF16))
                o_ev = ec3(nc.sbuf_tensor("o_ev", [128, 2 * 512], F32))
                psO = ec3(nc.psum_tensor("psO", [128, 2 * 512], F32))
                psT = ec3(nc.psum_tensor("psT", [128, 2 * 512], F32))
                _dummy = 1
                s = Section(nc, sems)
                wo_d = s.add("gpsimd", lambda e: e.dma_start(
                    out=wo_sb.ap().rearrange("p (ic m) -> p ic m", ic=NCC),
                    in_=woT.rearrange("(ic p) m -> p ic m", p=128)), dma=True, sem="w")
                r1_ops = []
                prev_ev = {}
                prev_tr = {}
                prev_add = {}
                k = 0
                for oc in range(NCC):
                    for qt in range(4):
                        sl = k % 2

                        def em_owmm(e, oc=oc, qt=qt, sl=sl):
                            for ic in range(NCC):
                                ins = e.matmul(
                                    psO[:, sl * 512:sl * 512 + 512],
                                    wo_sb[:, ic * C + oc * 128:ic * C + oc * 128 + 128],
                                    rwkvT[:, ic * T + qt * 512:ic * T + (qt + 1) * 512],
                                    start=(ic == 0), stop=(ic == NCC - 1))
                            return ins
                        d = [wo_d] + rwkv_ops
                        if k >= 2:
                            d.append(prev_ev[k - 2])
                        mm = s.add("tensor", em_owmm, deps=d)
                        ev = s.add("vector", lambda e, sl=sl: e.tensor_copy(
                            o_ev[:, sl * 512:sl * 512 + 512], psO[:, sl * 512:sl * 512 + 512]),
                            deps=[mm] + ([prev_tr[k - 2]] if k >= 2 else []))
                        prev_ev[k] = ev

                        def em_tr(e, oc=oc, qt=qt, sl=sl):
                            for j in range(4):
                                ins = e.transpose(
                                    psT[:, sl * 512 + j * 128:sl * 512 + j * 128 + 128],
                                    o_ev[:, sl * 512 + j * 128:sl * 512 + j * 128 + 128],
                                    ident_f[:, :])
                            return ins
                        tr = s.add("tensor", em_tr,
                                   deps=[ev, o_idf] + ([prev_add[k - 2]] if k >= 2 else []))
                        prev_tr[k] = tr

                        def em_add(e, oc=oc, qt=qt, sl=sl):
                            return e.tensor_add(
                                XR.ap().rearrange("p (n c) -> p n c", n=NT)[
                                    :, 4 * qt:4 * qt + 4, oc * 128:oc * 128 + 128],
                                XR.ap().rearrange("p (n c) -> p n c", n=NT)[
                                    :, 4 * qt:4 * qt + 4, oc * 128:oc * 128 + 128],
                                psT.ap().rearrange("p (b j t) -> p (b j) t", b=2, j=4)[
                                    :, sl * 4:sl * 4 + 4, :])
                        ad = s.add("vector", em_add, deps=[tr])
                        prev_add[k] = ad
                        r1_ops.append(ad)
                        k += 1

                xn2o = ec3(nc.sbuf_tensor("xn2o", [128, NCC * T], BF16))
                x2_tile = ec3(nc.sbuf_tensor("x2_tile", [128, 2 * C], BF16))
                ptr2 = ec3(nc.psum_tensor("ptr2", [128, 2 * C], BF16))

                stats2 = []
                prevv = None
                for i in range(NT):
                    d = [r1_ops[7 * 4 + i // 4]] + ([prevv] if prevv else [])
                    a = s.add("vector", lambda e, i=i: e.bn_stats(
                        st6[:, 0:6], XR[:, i * C:i * C + 512]), deps=d)
                    b = s.add("vector", lambda e, i=i: e.bn_stats(
                        st6[:, 6:12], XR[:, i * C + 512:i * C + 1024]), deps=[a])
                    g = s.add("vector", lambda e: e.bn_aggr(mv[:, 0:2], st6[:, :]), deps=[a, b])
                    cm = s.add("vector", lambda e, i=i: e.tensor_copy(
                        stat[:, ST_MEAN1 + i:ST_MEAN1 + i + 1], mv[:, 0:1]), deps=[g])
                    cv = s.add("vector", lambda e, i=i: e.tensor_copy(
                        stat[:, ST_VAR1 + i:ST_VAR1 + i + 1], mv[:, 1:2]), deps=[g])
                    prevv = cv
                    stats2.extend([cm, cv])
                LNU, IU, V2 = ST_SY, ST_BY, ST_W1
                o1 = s.add("scalar", lambda e: e.activation(
                    stat[:, LNU:LNU + 16], stat[:, ST_VAR1:ST_VAR1 + 16], AF.Ln, bias=EPS),
                    deps=stats2)
                o2 = s.add("scalar", lambda e: e.activation(
                    stat[:, ST_RSTD1:ST_RSTD1 + 16], stat[:, LNU:LNU + 16], AF.Exp, scale=-0.5),
                    deps=[o1])
                o3 = s.add("scalar", lambda e: e.activation(
                    stat[:, IU:IU + 16], stat[:, LNU:LNU + 16], AF.Exp, scale=-1.0), deps=[o1])
                o4 = s.add("vector", lambda e: e.tensor_mul(
                    stat[:, V2:V2 + 16], stat[:, ST_VAR1:ST_VAR1 + 16], stat[:, IU:IU + 16]),
                    deps=[o3])
                o5 = s.add("scalar", lambda e: e.activation(
                    stat[:, V2:V2 + 16], stat[:, V2:V2 + 16], AF.Ln, bias=EPS), deps=[o4])
                o6 = s.add("scalar", lambda e: e.activation(
                    stat[:, V2:V2 + 16], stat[:, V2:V2 + 16], AF.Exp, scale=-0.5), deps=[o5])
                o7 = s.add("vector", lambda e: e.tensor_mul(
                    stat[:, ST_SY:ST_SY + 16], stat[:, ST_RSTD1:ST_RSTD1 + 16],
                    stat[:, V2:V2 + 16]), deps=[o6, o2, o3])
                o8 = s.add("vector", lambda e: e.tensor_mul(
                    stat[:, ST_BY:ST_BY + 16], stat[:, ST_MEAN1:ST_MEAN1 + 16],
                    stat[:, ST_SY:ST_SY + 16]), deps=[o7])
                o9 = s.add("vector", lambda e: e.tensor_scalar_mul(
                    stat[:, ST_BY:ST_BY + 16], stat[:, ST_BY:ST_BY + 16], -1.0), deps=[o8])
                oW = s.add("vector", lambda e: e.tensor_scalar(
                    stat[:, ST_W1:ST_W1 + 16], stat[:, ST_RSTD1:ST_RSTD1 + 16], 1.0, 1.0,
                    op0=ALU.mult, op1=ALU.add), deps=[o8, o6])
                oN = s.add("vector", lambda e: e.tensor_mul(
                    stat[:, ST_NMR:ST_NMR + 16], stat[:, ST_MEAN1:ST_MEAN1 + 16],
                    stat[:, ST_RSTD1:ST_RSTD1 + 16]), deps=[o7])
                oN2 = s.add("vector", lambda e: e.tensor_scalar_mul(
                    stat[:, ST_NMR:ST_NMR + 16], stat[:, ST_NMR:ST_NMR + 16], -1.0), deps=[oN])
                tr2 = [None] * NT
                cp2 = [None] * NT
                ap2 = [None] * NT
                snd2h = {}
                cc2h = {}
                for i in range(NT):
                    d = [o7, o9]
                    if i >= 2:
                        d.append(tr2[i - 2])
                    ap2[i] = s.add("scalar", lambda e, i=i: e.activation(
                        x2_tile[:, (i % 2) * C:(i % 2) * C + C],
                        XR[:, i * C:(i + 1) * C], AF.Identity,
                        bias=stat[:, ST_BY + i:ST_BY + i + 1],
                        scale=stat[:, ST_SY + i:ST_SY + i + 1]), deps=d)

                    def em_tr2(e, i=i):
                        for cc in range(NCC):
                            ins = e.transpose(
                                ptr2[:, (i % 2) * C + cc * 128:(i % 2) * C + cc * 128 + 128],
                                x2_tile[:, (i % 2) * C + cc * 128:(i % 2) * C + cc * 128 + 128],
                                ident_b[:, :])
                        return ins
                    d = [ap2[i]]
                    if i >= 2:
                        d.append(cp2[i - 2])
                    tr2[i] = s.add("tensor", em_tr2, deps=d)
                    cp2[i] = s.add("vector", lambda e, i=i: e.tensor_copy(
                        xn2o.ap().rearrange("p (cc t) -> p cc t", cc=NCC)[:, :, i * 128:(i + 1) * 128],
                        ptr2.ap().rearrange("p (b cc t) -> p b cc t", b=2, cc=NCC)[:, i % 2]),
                        deps=[tr2[i]])
                    if i in (7, 15):
                        hh = i // 8
                        affh = []
                        for cc in range(NCC):
                            affh.append(s.add("vector", lambda e, cc=cc, hh=hh: e.tensor_scalar(
                                xn2o[:, cc * T + hh * 1024:cc * T + hh * 1024 + 1024],
                                xn2o[:, cc * T + hh * 1024:cc * T + hh * 1024 + 1024],
                                vp[:, VP_NW + cc:VP_NW + cc + 1],
                                vp[:, VP_NB + cc:VP_NB + cc + 1],
                                op0=ALU.mult, op1=ALU.add),
                                deps=[o_vp] + cp2[hh * 8:hh * 8 + 8]))
                        snd2h[hh] = s.add("sync", lambda e, hh=hh: e.dma_start(
                            out=cc2_snd[hh].ap().rearrange("p (cc t) -> p cc t", cc=NCC),
                            in_=xn2o.ap().rearrange("p (cc t) -> p cc t", cc=NCC)[
                                :, :, hh * 1024:hh * 1024 + 1024]),
                            deps=affh, dma=True, sem="io")
                        cc2h[hh] = s.add("gpsimd", lambda e, hh=hh: e.collective_compute(
                            "AllGather", ALU.bypass, replica_groups=RG,
                            ins=[cc2_snd[hh].ap().opt()], outs=[cc2_rcv[hh].ap().opt()]),
                            deps=[snd2h[hh], last_coll[0] if hh == 0 else cc2h[0]],
                            coll=True, sem="cc")
                cc2 = cc2h[1]
                spill = s.add("sync", lambda e: e.dma_start(
                    out=r1_spill.rearrange("(n p) c -> p n c", p=128),
                    in_=XR.ap().rearrange("p (n c) -> p n c", c=C)),
                    deps=[ap2[NT - 1]] + stats2, dma=True, sem="spl")
                s.emit()


        if BUILD_UPTO == 1:
            _tap_out(nc, sems, XR, OUT, r1_ops)
            return nc

        qk_ops = []
        v_ops = []
        with contextlib.ExitStack() as _s5x:
            ec5 = _s5x.enter_context
            Vt = ec5(nc.sbuf_tensor("Vt", [128, 32 * 512], BF16))
            qmkm = ec5(nc.sbuf_tensor("qmkm", [128, 20], F32))
            stt_row = ec5(nc.sbuf_tensor("stt_row", [1, 4], F32))
            _dummy = 1
            with contextlib.ExitStack() as _s5y:
                ec5y = _s5y.enter_context
                xn2f = ec5y(nc.sbuf_tensor("xn2f", [128, NCC * TFULL], BF16))
                pw_sb = ec5y(nc.sbuf_tensor("pw_sb", [128, 2 * 8 * 512], BF16))
                vw_sb = ec5y(nc.sbuf_tensor("vw_sb", [128, 8 * 512], BF16))
                vb_sb = ec5y(nc.sbuf_tensor("vb_sb", [1, 512], BF16))
                _dummy = 1
                s = Section(nc, sems)
                wq_d = s.add("gpsimd", lambda e: e.dma_start(
                    out=pw_sb[:, 0:8 * 512].rearrange("p (ic m) -> p ic m", ic=NCC),
                    in_=qwT.rearrange("(ic p) m -> p ic m", p=128)), dma=True, sem="w")
                wk_d = s.add("gpsimd", lambda e: e.dma_start(
                    out=pw_sb[:, 8 * 512:16 * 512].rearrange("p (ic m) -> p ic m", ic=NCC),
                    in_=kwT.rearrange("(ic p) m -> p ic m", p=128)), dma=True, sem="w")
                wv_d = s.add("gpsimd", lambda e: e.dma_start(
                    out=vw_sb.ap().rearrange("p (ic m) -> p ic m", ic=NCC),
                    in_=vwT.rearrange("(ic p) m -> p ic m", p=128)), dma=True, sem="w")
                vb_d = s.add("gpsimd", lambda e: e.dma_start(
                    out=vb_sb[:, :], in_=vb_row), dma=True, sem="w")
                rcvh = {0: [], 1: []}
                for a in range(2):
                    for r in range(2):
                        dd = [cc2h[a], spill]
                        if a == 0:
                            dd.append(snd2h[1])
                        rcvh[a].append(s.add("sync", lambda e, a=a, r=r: e.dma_start(
                            out=xn2f.ap().rearrange("p (cc t) -> p cc t", cc=NCC)[
                                :, :, r * T + a * 1024:r * T + a * 1024 + 1024],
                            in_=cc2_rcv[a].ap()[r].rearrange("p (cc t) -> p cc t", cc=NCC)),
                            deps=dd, dma=True, sem="io" if r == 0 else "io2"))

                with contextlib.ExitStack() as _s5p:
                    psP = _s5p.enter_context(nc.psum_tensor("psP", [128, 2 * 512], F32))
                    prev_ev = {}
                    red_ops = []
                    k5 = 0
                    units = []
                    for half in range(2):
                        qts = (0, 1, 4, 5) if half == 0 else (2, 3, 6, 7)
                        tcs = (list(range(0, 8)) + list(range(16, 24)) if half == 0
                               else list(range(8, 16)) + list(range(24, 32)))
                        for which, woff, dst, bcol in (("q", 0, QT, VP_QB),
                                                       ("k", 8 * 512, KT, VP_KB)):
                            for j in range(4):
                                for qt in qts:
                                    units.append(("qk", half, which, woff, dst, bcol, j, qt))
                                if half == 1:
                                    units.append(("red", which, j))
                        for tc in tcs:
                            units.append(("v", half, tc))
                    qk_by = {}
                    for u in units:
                        if u[0] == "red":
                            _, which, j = u
                            rbase = 4 if which == "q" else 12
                            xbase = (0 if which == "q" else 16384) + j * 4096
                            red_ops.append(s.add("vector", lambda e, rbase=rbase, j=j, xbase=xbase:
                                e.tensor_reduce(
                                    qmkm[:, rbase + j:rbase + j + 1],
                                    XRb[:, xbase:xbase + 4096],
                                    axis=AX.X, op=ALU.max, apply_absolute_value=True),
                                deps=qk_by[(which, j)]))
                            continue
                        sl = k5 % 2
                        if u[0] == "qk":
                            _, half, which, woff, dst, bcol, j, qt = u
                            nic = NCC >> (qt // 2)

                            def em_p(e, j=j, qt=qt, nic=nic, sl=sl, woff=woff):
                                for ic in range(nic):
                                    ins = e.matmul(
                                        psP[:, sl * 512:sl * 512 + 512],
                                        pw_sb[:, woff + ic * 512 + j * 128:woff + ic * 512 + j * 128 + 128],
                                        xn2f[:, ic * TFULL + qt * 512:ic * TFULL + (qt + 1) * 512],
                                        start=(ic == 0), stop=(ic == nic - 1))
                                return ins
                            d = rcvh[half] + [wq_d if which == "q" else wk_d, spill]
                            if k5 >= 2:
                                d.append(prev_ev[k5 - 2])
                            mm = s.add("tensor", em_p, deps=d)
                            ev = s.add("vector", lambda e, j=j, qt=qt, sl=sl, dst=dst, bcol=bcol: e.tensor_scalar(
                                dst(j)[:, qt * 512:(qt + 1) * 512],
                                psP[:, sl * 512:sl * 512 + 512],
                                vp[:, bcol + j:bcol + j + 1], None, op0=ALU.add),
                                deps=[mm, o_vp])
                            qk_ops.append(ev)
                            qk_by.setdefault((which, j), []).append(ev)
                        else:
                            _, half, tc = u
                            nic = NCC >> (tc // 8)

                            def em_v(e, tc=tc, nic=nic, sl=sl):
                                for ic in range(nic):
                                    e.matmul(
                                        psP[:, sl * 512:sl * 512 + 512],
                                        xn2f[:, ic * TFULL + tc * 128:ic * TFULL + tc * 128 + 128],
                                        vw_sb[:, ic * 512:(ic + 1) * 512],
                                        start=(ic == 0), stop=False)
                                ins = e.matmul(psP[:, sl * 512:sl * 512 + 512],
                                               ones_r1[:, :], vb_sb[:, :],
                                               start=False, stop=True)
                                return ins
                            d = rcvh[half] + [wv_d, vb_d, o_m1]
                            if k5 >= 2:
                                d.append(prev_ev[k5 - 2])
                            mm = s.add("tensor", em_v, deps=d)
                            ev = s.add("vector", lambda e, tc=tc, sl=sl: e.tensor_copy(
                                Vt[:, tc * 512:(tc + 1) * 512], psP[:, sl * 512:sl * 512 + 512]),
                                deps=[mm])
                            v_ops.append(ev)
                        prev_ev[k5] = ev
                        k5 += 1
                    g1 = s.add("vector", lambda e: e.tensor_reduce(
                        qmkm[:, 0:1], qmkm[:, 4:8], axis=AX.X, op=ALU.max),
                        deps=qk_ops + red_ops)
                    g2 = s.add("vector", lambda e: e.tensor_reduce(
                        qmkm[:, 1:2], qmkm[:, 12:16], axis=AX.X, op=ALU.max),
                        deps=qk_ops + red_ops + [g1])
                    with nc.psum_tensor("psst", [128, 132], F32) as psst:
                        t1_ = s.add("tensor", lambda e: e.transpose(
                            psst[0:2, 0:128], qmkm[:, 0:2], ident_f[:, :]),
                            deps=[g1, g2, o_idf])
                        r1_ = s.add("vector", lambda e: e.tensor_reduce(
                            qmkm[0:2, 0:1], psst[0:2, 0:128], axis=AX.X, op=ALU.max),
                            deps=[t1_, g2])
                        t2_ = s.add("tensor", lambda e: e.transpose(
                            psst[0:1, 128:130], qmkm[0:2, 0:1], ident_f[0:2, 0:2]),
                            deps=[r1_])
                        m0_ = s.add("vector", lambda e: e.tensor_copy(
                            stt_row[0:1, 2:4], psst[0:1, 128:130]), deps=[t2_])
                        m1_ = s.add("vector", lambda e: e.tensor_mul(
                            stt_row[0:1, 0:1], stt_row[0:1, 2:3], stt_row[0:1, 3:4]),
                            deps=[m0_])
                        m2_ = s.add("vector", lambda e: e.tensor_scalar_mul(
                            stt_row[0:1, 0:1], stt_row[0:1, 0:1], -4.0), deps=[m1_])
                        b1_ = s.add("tensor", lambda e: e.matmul(
                            psst[:, 130:131], ones_r1f[:, :], stt_row[0:1, 0:1],
                            start=True, stop=True), deps=[m2_, o_m2])
                        b2_ = s.add("vector", lambda e: e.tensor_copy(
                            cbias[:, 0:1], psst[:, 130:131]), deps=[b1_])
                    s.emit()

            attn_oT = ec5(nc.sbuf_tensor("attn_oT", [128, 4 * TFULL], BF16))

            if BUILD_UPTO == 2:
                _tap_out(nc, sems, XR, OUT, qk_ops + v_ops + [b2_])
                return nc

            with contextlib.ExitStack() as _s6x:
                ec6 = _s6x.enter_context
                recipb = ec6(nc.sbuf_tensor("recipb", [128, 2 * 512], F32))
                rrow = ec6(nc.sbuf_tensor("rrow", [1, 2048], F32))
                ptb = ec6(nc.sbuf_tensor("ptb", [128, 2 * 2048], BF16))
                praw = ec6(nc.sbuf_tensor("praw", [128, 2 * 512], F32))
                pssc = ec6(nc.psum_tensor("pssc", [128, 2048], F32))
                psPV = ec6(nc.psum_tensor("psPV", [128, 2 * 512], F32))
                psDB = ec6(nc.psum_tensor("psDB", [128, 2 * 512], F32))
                _dummy = 1
                s = Section(nc, sems)
                attn_ops = []
                hist = {}
                NQ = 16

                def emit_ep_early(p, qt):
                    den_last = hist["last_den"]
                    pv_last = hist["last_pv"]
                    cr = s.add("vector", lambda e: e.tensor_copy(
                        praw[:, :], psPV[:, :]), deps=[pv_last, hist.get("raw_used")])
                    hist["raw_copy"] = cr
                    cps = []
                    for half, r0, r1_ in ((0, 0, 64), (1, 32, 96)):
                        ca = s.add("vector", lambda e, r0=r0, half=half: e.tensor_copy(
                            rrow[0:1, half * 1024:half * 1024 + 512],
                            psDB[r0:r0 + 1, 0:512]), deps=[den_last, hist.get("rr_used")])
                        cb = s.add("vector", lambda e, r1_=r1_, half=half: e.tensor_add(
                            rrow[0:1, half * 1024:half * 1024 + 512],
                            rrow[0:1, half * 1024:half * 1024 + 512],
                            psDB[r1_:r1_ + 1, 0:512]), deps=[ca])
                        cps.append(cb)
                    hist["last_rr"] = cps[-1]
                    hist["den_cp"] = cps

                def emit_ep_late_half(p, qt, half):
                    jj = 2 * p + half
                    rr1 = s.add("scalar", lambda e, half=half: e.activation(
                        rrow[0:1, half * 1024:half * 1024 + 512],
                        rrow[0:1, half * 1024:half * 1024 + 512], AF.Ln),
                        deps=[hist["den_cp"][half]])
                    rr2 = s.add("scalar", lambda e, half=half: e.activation(
                        rrow[0:1, half * 1024:half * 1024 + 512],
                        rrow[0:1, half * 1024:half * 1024 + 512],
                        AF.Exp, scale=-1.0), deps=[rr1])
                    d1 = s.add("sync", lambda e, half=half: e.dma_start(
                        out=rr_dram[half].ap(),
                        in_=rrow[0:1, half * 1024:half * 1024 + 512]),
                        deps=[rr2], dma=True, sem="rb")
                    d2 = s.add("sync", lambda e, half=half: e.dma_start(
                        out=recipb[:, half * 512:half * 512 + 512],
                        in_=rr_dram[half].ap().broadcast_to([128, 512])),
                        deps=[d1, hist.get(("rb_used", half))], dma=True, sem="rb")
                    ev = s.add("vector", lambda e, jj=jj, qt=qt, half=half: e.tensor_mul(
                        attn_oT[:, jj * TFULL + qt * 512:jj * TFULL + (qt + 1) * 512],
                        praw[:, half * 512:half * 512 + 512],
                        recipb[:, half * 512:half * 512 + 512]),
                        deps=[d2, hist["raw_copy"]])
                    attn_ops.append(ev)
                    hist[("rb_used", half)] = ev
                    if half == 1:
                        hist["raw_used"] = ev
                        hist["rr_used"] = d1

                def emit_ep_late(p, qt):
                    emit_ep_late_half(p, qt, 0)
                    emit_ep_late_half(p, qt, 1)

                pending = None
                NQ2 = 32
                units = [(p, qt) for p in range(2) for qt in range(8)]
                for p, qt in units:
                    j0, j1 = 2 * p, 2 * p + 1
                    exp_ops = [None] * NQ2
                    qk_mms = [None] * NQ2
                    pv_mms = [None] * NQ2
                    den_mms = {}

                    def em_qk(e, q, j0=j0, j1=j1, qt=qt):
                        for i, jj in enumerate((j0, j1)):
                            ins = e.matmul(
                                pssc[:, (q % 2) * 1024 + i * 512:
                                     (q % 2) * 1024 + (i + 1) * 512],
                                KT(jj)[:, q * 128:(q + 1) * 128],
                                QT(jj)[:, qt * 512:(qt + 1) * 512],
                                start=True, stop=True)
                        return ins

                    def em_pv(e, q, j0=j0, j1=j1):
                        for i, jj in enumerate((j0, j1)):
                            ins = e.matmul(
                                psPV[:, i * 512:(i + 1) * 512],
                                Vt[:, q * 512 + jj * 128:q * 512 + jj * 128 + 128],
                                ptb[:, (q % 4) * 1024 + i * 512:
                                    (q % 4) * 1024 + (i + 1) * 512],
                                start=(q == 0), stop=(q == NQ2 - 1))
                        return ins

                    def em_den(e, qa, qb):
                        first = (qa == 0)
                        last = (qb == NQ2 - 1)
                        for row, kcx, i in ((0, qa, 0), (32, qa, 1),
                                            (64, qb, 0), (96, qb, 1)):
                            ins = e.matmul(
                                psDB[row:row + 1, 0:512], ones_c1[:, :],
                                ptb[:, (kcx % 4) * 1024 + i * 512:
                                    (kcx % 4) * 1024 + (i + 1) * 512],
                                start=first, stop=last, tile_position=(0, row))
                        return ins

                    for q_ in range(NQ2):
                        if q_ >= 2:
                            d = [exp_ops[q_ - 2]]
                        elif hist.get("pexp") is not None:
                            d = [hist["pexp"][q_]]
                        else:
                            d = qk_ops + v_ops + [b2_]
                        qk_mms[q_] = s.add("tensor",
                                           lambda e, q=q_, f=em_qk: f(e, q), deps=d)

                        ex_d = [qk_mms[q_], b2_]
                        if q_ >= 4:
                            ex_d += [pv_mms[q_ - 4], den_mms[(q_ - 4) // 2]]
                        elif hist.get("ppv") is not None:
                            ex_d += [hist["ppv"][q_], hist["pdb"][q_ // 2]]
                        exp_ops[q_] = s.add("scalar", lambda e, q=q_: e.activation(
                            ptb[:, (q % 4) * 1024:(q % 4) * 1024 + 1024],
                            pssc[:, (q % 2) * 1024:(q % 2) * 1024 + 1024],
                            AF.Exp, scale=1.0 / 32.0, bias=cbias[:, 0:1]), deps=ex_d)

                        if q_ >= 1:
                            d = [exp_ops[q_ - 1]]
                            if q_ - 1 == 0 and hist.get("raw_copy") is not None:
                                d.append(hist["raw_copy"])
                            pv_mms[q_ - 1] = s.add("tensor",
                                lambda e, q=q_ - 1, f=em_pv: f(e, q), deps=d)
                            if (q_ - 1) % 2 == 1:
                                bi = (q_ - 1) // 2
                                dd_ = [exp_ops[q_ - 1], exp_ops[q_ - 2], o_m3]
                                if bi == 0 and hist.get("last_rr") is not None:
                                    dd_.append(hist["last_rr"])
                                if bi == 0 and hist.get("rr_used") is not None:
                                    dd_.append(hist.get("rr_used"))
                                den_mms[bi] = s.add("tensor",
                                    lambda e, qa=q_ - 2, qb=q_ - 1, f=em_den:
                                    f(e, qa, qb), deps=dd_)
                        if q_ == 8 and pending is not None:
                            emit_ep_late_half(pending[0], pending[1], 0)
                        if q_ == 20 and pending is not None:
                            done_u = pending
                            emit_ep_late_half(pending[0], pending[1], 1)
                            pending = None
                            if done_u == (0, 7):
                                snd3a = s.add("sync", lambda e: e.dma_start(
                                    out=cc3_snd_a.ap(),
                                    in_=attn_oT[:, 0:2 * TFULL]),
                                    deps=list(attn_ops), dma=True, sem="io")
                                cc3a = s.add("gpsimd", lambda e: e.collective_compute(
                                    "AllGather", ALU.bypass, replica_groups=RG,
                                    ins=[cc3_snd_a.ap().opt()],
                                    outs=[cc3_rcv_a.ap().opt()]),
                                    deps=[snd3a, cc2], coll=True, sem="cc")
                            elif done_u == (1, 3):
                                snd3b1 = s.add("sync", lambda e: e.dma_start(
                                    out=cc3_snd_b1.ap().rearrange(
                                        "p (j t) -> p j t", j=2),
                                    in_=attn_oT.ap().rearrange(
                                        "p (j t) -> p j t", j=4)[:, 2:4, 0:T]),
                                    deps=attn_ops[16:24], dma=True, sem="io")
                                cc3b1 = s.add("gpsimd", lambda e: e.collective_compute(
                                    "AllGather", ALU.bypass, replica_groups=RG,
                                    ins=[cc3_snd_b1.ap().opt()],
                                    outs=[cc3_rcv_b1.ap().opt()]),
                                    deps=[snd3b1, cc3a], coll=True, sem="cc")
                    pv_mms[NQ2 - 1] = s.add("tensor",
                        lambda e, q=NQ2 - 1, f=em_pv: f(e, q),
                        deps=[exp_ops[NQ2 - 1]])
                    den_mms[(NQ2 - 1) // 2] = s.add("tensor",
                        lambda e, qa=NQ2 - 2, qb=NQ2 - 1, f=em_den: f(e, qa, qb),
                        deps=[exp_ops[NQ2 - 1], exp_ops[NQ2 - 2], o_m3])
                    hist["pexp"] = (exp_ops[NQ2 - 2], exp_ops[NQ2 - 1])
                    hist["ppv"] = [pv_mms[NQ2 - 4 + i] for i in range(4)]
                    hist["pdb"] = [den_mms[(NQ2 - 4) // 2], den_mms[(NQ2 - 2) // 2]]
                    hist["last_pv"] = pv_mms[NQ2 - 1]
                    hist["last_den"] = den_mms[(NQ2 - 1) // 2]
                    emit_ep_early(p, qt)
                    pending = (p, qt)
                emit_ep_late(*pending)
                s.emit()

            if BUILD_UPTO == 3:
                _tap_out(nc, sems, attn_oT.ap().bitcast(F32), OUT, attn_ops, ntiles=8)
                return nc

            s = Section(nc, sems)
            snd3b2 = s.add("sync", lambda e: e.dma_start(
                out=cc3_snd_b2.ap().rearrange("p (j t) -> p j t", j=2),
                in_=attn_oT.ap().rearrange("p (j t) -> p j t", j=4)[:, 2:4, T:2 * T]),
                deps=attn_ops, dma=True, sem="io")
            cc3b2 = s.add("gpsimd", lambda e: e.collective_compute(
                "AllGather", ALU.bypass, replica_groups=RG,
                ins=[cc3_snd_b2.ap().opt()], outs=[cc3_rcv_b2.ap().opt()]),
                deps=[snd3b2, cc3b1], coll=True, sem="cc")
            s.emit()

        with contextlib.ExitStack() as _s7x:
                ec7 = _s7x.enter_context
                attn_all = ec7(nc.sbuf_tensor("attn_all", [128, NCC * T], BF16))
                mbufL = ec7(nc.sbuf_tensor("mbufL", [128, 2 * T], BF16))
                mbufR = ec7(nc.sbuf_tensor("mbufR", [128, 2 * T], BF16))
                ow_sb = ec7(nc.sbuf_tensor("ow_sb", [128, 2 * NCC * C], BF16))
                ob_sb = ec7(nc.sbuf_tensor("ob_sb", [128, 16], F32))
                o_ev2 = ec7(nc.sbuf_tensor("o_ev2", [128, 8 * 512], F32))
                wsb = ec7(nc.sbuf_tensor("wsb", [128, C], F32))
                fin = ec7(nc.sbuf_tensor("fin", [128, 2 * C], F32))
                psO2 = ec7(nc.psum_tensor("psO2", [128, 2 * 512], F32))
                psTk = ec7(nc.psum_tensor("psTk", [128, 2 * C], F32))
                _dummy = 1
                s = Section(nc, sems)
                ow_d0 = s.add("gpsimd", lambda e: e.dma_start(
                    out=ow_sb[:, 0:NCC * C].rearrange("p (ic m) -> p ic m", ic=NCC),
                    in_=owT0.rearrange("(ic p) m -> p ic m", p=128)),
                    deps=[snd3b2], dma=True, sem="w")
                ow_d1 = s.add("gpsimd", lambda e: e.dma_start(
                    out=ow_sb[:, NCC * C:2 * NCC * C].rearrange("p (ic m) -> p ic m", ic=NCC),
                    in_=owT1.rearrange("(ic p) m -> p ic m", p=128)),
                    deps=[snd3b2], dma=True, sem="w")
                ob_d = s.add("gpsimd", lambda e: e.dma_start(
                    out=ob_sb[:, :], in_=ob_cols), deps=[snd3b2], dma=True, sem="w")
                rel = s.add("sync", lambda e: e.dma_start(
                    out=XR.ap().rearrange("p (n c) -> p n c", c=C),
                    in_=r1_spill.rearrange("(n p) c -> p n c", p=128)),
                    deps=attn_ops, dma=True, sem="spl")
                mrg = []
                mrg_by = {}
                for idx, hh in enumerate((0, 1, 4, 5, 2, 3, 6, 7)):
                    r, j = hh // 4, hh % 4
                    pp = idx % 2
                    ms = "io" if pp == 0 else "io2"
                    if j < 2:
                        src_l = cc3_rcv_a.ap()[r][:, j * TFULL:j * TFULL + T]
                        src_r = cc3_rcv_a.ap()[r][:, j * TFULL + T:j * TFULL + 2 * T]
                        dep_l, dep_r = cc3a, cc3a
                    else:
                        src_l = cc3_rcv_b1.ap()[r][:, (j - 2) * T:(j - 1) * T]
                        src_r = cc3_rcv_b2.ap()[r][:, (j - 2) * T:(j - 1) * T]
                        dep_l, dep_r = cc3b1, cc3b2
                    prevm = mrg[idx - 2] if idx >= 2 else None
                    dl = s.add("sync", lambda e, src_l=src_l, pp=pp: e.dma_start(
                        out=mbufL[:, pp * T:(pp + 1) * T], in_=src_l),
                        deps=[dep_l, snd3b2, prevm], dma=True, sem=ms)
                    dr = s.add("sync", lambda e, src_r=src_r, pp=pp: e.dma_start(
                        out=mbufR[:, pp * T:(pp + 1) * T], in_=src_r),
                        deps=[dep_r, snd3b2, prevm], dma=True, sem=ms)
                    m1 = s.add("vector", lambda e, hh=hh, pp=pp: e.tensor_sub(
                        attn_all[:, hh * T:(hh + 1) * T],
                        mbufR[:, pp * T:(pp + 1) * T], mbufL[:, pp * T:(pp + 1) * T]),
                        deps=[dl, dr])
                    m2 = s.add("vector", lambda e, hh=hh, pp=pp: e.scalar_tensor_tensor(
                        attn_all[:, hh * T:(hh + 1) * T],
                        attn_all[:, hh * T:(hh + 1) * T],
                        vp[:, VP_HM:VP_HM + 1], mbufL[:, pp * T:(pp + 1) * T],
                        op0=ALU.mult, op1=ALU.add), deps=[m1, o_vp])
                    mrg.append(m2)
                    mrg_by[hh] = m2
                prev_ev = {}
                prev_f = {}
                last_round_trbs = []
                out_ops = []
                ko = 0
                for eo in range(2):
                    for qt2 in range(2):
                        ev_blocks = []
                        round_trbs = []
                        for oc in range(NCC):
                            sl = ko % 2

                            def em_omm(ics, first, last, e=None, eo=eo, qt2=qt2,
                                       oc=oc, sl=sl):
                                for k_, ic in enumerate(ics):
                                    ins = e.matmul(
                                        psO2[:, sl * 512:sl * 512 + 512],
                                        ow_sb[:, eo * NCC * C + ic * C + oc * 128:
                                              eo * NCC * C + ic * C + oc * 128 + 128],
                                        attn_all[:, ic * T + eo * C + qt2 * 512:
                                                 ic * T + eo * C + (qt2 + 1) * 512],
                                        start=(first and k_ == 0),
                                        stop=(last and k_ == len(ics) - 1))
                                return ins
                            ow_d = ow_d0 if eo == 0 else ow_d1
                            d = [mrg_by[0], mrg_by[1], mrg_by[4], mrg_by[5], ow_d]
                            if ko >= 2:
                                d.append(prev_ev[ko - 2])
                            mma = s.add("tensor", lambda e, f=em_omm: f(
                                (0, 1, 4, 5), True, False, e), deps=d)
                            mmb = s.add("tensor", lambda e, f=em_omm: f(
                                (2, 3, 6, 7), False, True, e),
                                deps=[mma, mrg_by[2], mrg_by[3], mrg_by[6], mrg_by[7]])
                            ev = s.add("vector", lambda e, oc=oc, eo=eo, sl=sl: e.tensor_scalar(
                                o_ev2[:, oc * 512:(oc + 1) * 512],
                                psO2[:, sl * 512:sl * 512 + 512],
                                ob_sb[:, eo * 8 + oc:eo * 8 + oc + 1], None, op0=ALU.add),
                                deps=[mmb, ob_d] + last_round_trbs)
                            prev_ev[ko] = ev
                            ev_blocks.append(ev)
                            ko += 1
                        for blk_ in range(4):
                            ti = eo * 8 + qt2 * 4 + blk_
                            slt = ti % 2

                            def em_trb(e, blk_=blk_, slt=slt):
                                for oc in range(NCC):
                                    ins = e.transpose(
                                        psTk[:, slt * C + oc * 128:slt * C + oc * 128 + 128],
                                        o_ev2[:, oc * 512 + blk_ * 128:oc * 512 + blk_ * 128 + 128],
                                        ident_f[:, :])
                                return ins
                            d = ev_blocks + [o_idf]
                            if ti >= 2:
                                d.append(prev_f[ti - 2])
                            trb = s.add("tensor", em_trb, deps=d)
                            round_trbs.append(trb)
                            wop = s.add("scalar", lambda e, ti=ti: e.activation(
                                wsb[:, :], XR[:, ti * C:(ti + 1) * C], AF.Identity,
                                bias=stat[:, ST_NMR + ti:ST_NMR + ti + 1],
                                scale=stat[:, ST_W1 + ti:ST_W1 + ti + 1]),
                                deps=[rel, oW, oN2, prev_f.get(ti - 1)])
                            fo = s.add("vector", lambda e, ti=ti, slt=slt: e.tensor_add(
                                fin[:, (ti % 2) * C:(ti % 2) * C + C],
                                wsb[:, :], psTk[:, slt * C:slt * C + C]),
                                deps=[wop, trb] +
                                ([out_ops[ti - 2]] if ti >= 2 else []))
                            prev_f[ti] = fo
                            oo = s.add("sync", lambda e, ti=ti: e.dma_start(
                                out=OUT[ti * 128:(ti + 1) * 128, :],
                                in_=fin[:, (ti % 2) * C:(ti % 2) * C + C]),
                                deps=[fo], dma=True, sem="io")
                            out_ops.append(oo)
                        last_round_trbs = round_trbs
                s.add("sync", lambda e: e.nop(), deps=out_ops)
                s.emit()
    return nc


def _tap_out(nc, sems, XR, OUT, deps, ntiles=NT):
    s = Section(nc, sems)
    outs = []
    for i in range(ntiles):
        outs.append(s.add("sync", lambda e, i=i: e.dma_start(
            out=OUT[i * 128:(i + 1) * 128, :], in_=XR[:, i * C:(i + 1) * C]),
            deps=deps, dma=True, sem="io"))
    s.add("sync", lambda e: e.nop(), deps=outs)
    s.emit()


_NC_CACHE = {}


def kernel(L0_attn_x, L0_attn_time_decay, L0_attn_time_first, L0_attn_time_mix,
           L0_attn_Wk, L0_attn_Wv, L0_attn_Wr, L0_attn_Wo,
           L0_ffn_router_prob, L0_ffn_norm_w, L0_ffn_norm_b,
           L0_ffn_q_w, L0_ffn_q_b, L0_ffn_k_w, L0_ffn_k_b,
           L0_ffn_v_w, L0_ffn_v_b, L0_ffn_o_w, L0_ffn_o_b):
    x = np.ascontiguousarray(np.asarray(L0_attn_x, dtype=np.float32))
    B, Tf, D = x.shape
    f32 = np.float32

    def packcols(v):
        return np.ascontiguousarray(np.asarray(v, f32).reshape(NCC, 128).T)

    wkT = np.ascontiguousarray(np.asarray(L0_attn_Wk, f32).T)
    wvT = np.ascontiguousarray(np.asarray(L0_attn_Wv, f32).T)
    wrT = np.ascontiguousarray(np.asarray(L0_attn_Wr, f32).T)
    woT = np.ascontiguousarray(np.asarray(L0_attn_Wo, f32).T)
    qwTf = np.asarray(L0_ffn_q_w, f32).T
    kwTf = np.asarray(L0_ffn_k_w, f32).T
    vwTf = np.asarray(L0_ffn_v_w, f32).T
    owTf = np.asarray(L0_ffn_o_w, f32).T

    key = (BUILD_UPTO, len(RG_GROUPS))
    if key not in _NC_CACHE:
        _NC_CACHE[key] = build_nc()
    nc = _NC_CACHE[key]

    in_maps = []
    for c in range(8):
        b, h = c // 2, c % 2
        xe = np.zeros((T + 1, C), f32)
        if h == 1:
            xe[0] = x[b, T - 1]
        xe[1:] = x[b, h * T:(h + 1) * T]
        ow0 = np.zeros((C, C), f32)
        ow1 = np.zeros((C, C), f32)
        m0 = C >> (2 * h)
        m1 = C >> (2 * h + 1)
        ow0[:m0, :m0] = owTf[:m0, :m0]
        ow1[:m1, :m1] = owTf[:m1, :m1]
        obc = np.zeros((2, C), f32)
        obc[0, :m0] = np.asarray(L0_ffn_o_b, f32)[:m0]
        obc[1, :m1] = np.asarray(L0_ffn_o_b, f32)[:m1]
        ob_cols = np.zeros((128, 16), f32)
        ob_cols[:, 0:8] = obc[0].reshape(NCC, 128).T
        ob_cols[:, 8:16] = obc[1].reshape(NCC, 128).T
        veca = np.zeros((128, 49), f32)
        veca[:, VP_TD:VP_TD + 8] = packcols(L0_attn_time_decay)
        veca[:, VP_TF:VP_TF + 8] = packcols(L0_attn_time_first)
        veca[:, VP_TM:VP_TM + 8] = packcols(L0_attn_time_mix)
        veca[:, VP_NW:VP_NW + 8] = packcols(L0_ffn_norm_w)
        veca[:, VP_NB:VP_NB + 8] = packcols(L0_ffn_norm_b)
        qb = np.asarray(L0_ffn_q_b, f32)[h * 512:(h + 1) * 512]
        kb = np.asarray(L0_ffn_k_b, f32)[h * 512:(h + 1) * 512]
        veca[:, VP_QB:VP_QB + 4] = qb.reshape(4, 128).T
        veca[:, VP_KB:VP_KB + 4] = kb.reshape(4, 128).T
        veca[:, VP_HM] = float(h)
        in_maps.append(dict(
            x_ext=xe,
            wkT=wkT, wvT=wvT, wrT=wrT, woT=woT,
            qwT=np.ascontiguousarray(qwTf[:, h * 512:(h + 1) * 512]),
            kwT=np.ascontiguousarray(kwTf[:, h * 512:(h + 1) * 512]),
            vwT=np.ascontiguousarray(vwTf[:, h * 512:(h + 1) * 512]),
            owT0=ow0, owT1=ow1,
            veca=veca,
            vb_row=np.ascontiguousarray(
                np.asarray(L0_ffn_v_b, f32)[h * 512:(h + 1) * 512].reshape(1, 512)),
            ob_cols=ob_cols,
        ))

    res = run_bass_kernel_spmd(nc, in_maps, core_ids=list(range(8)))
    out = np.empty((B, Tf, D), np.float32)
    for c in range(8):
        b, h = c // 2, c % 2
        out[b, h * T:(h + 1) * T] = res.results[c]["OUT"]
    return out


# version 44
# speedup vs baseline: 1.0308x; 1.0308x over previous
import contextlib
import numpy as np
import concourse.bass as bass
import concourse.mybir as mybir
from concourse.bass_utils import run_bass_kernel_spmd

F32 = mybir.dt.float32
BF16 = mybir.dt.bfloat16
F16 = mybir.dt.float16
AF = mybir.ActivationFunctionType
ALU = mybir.AluOpType
AX = mybir.AxisListType

T = 2048
TFULL = 4096
NT = 16
C = 1024
NCC = 8
TEC = 2052
EPS = 1e-5
K_CLAMP = 60.0
K_EPS = 1e-16

BUILD_UPTO = 99
S2_LEVEL = 4
RG_GROUPS = [[0, 1], [2, 3], [4, 5], [6, 7]]

ST_TM1, ST_E, ST_LAM, ST_FW, ST_NEGE, ST_N2047E = 0, 8, 16, 24, 32, 40
ST_MEAN1, ST_VAR1, ST_RSTD1, ST_NMR, ST_SY, ST_BY, ST_W1 = 48, 64, 80, 96, 112, 128, 144
ST_COLS = 160
VP_TD, VP_TF, VP_TM, VP_NW, VP_NB, VP_QB, VP_KB, VP_HM = 0, 8, 16, 24, 32, 40, 44, 48


class Op:
    __slots__ = ("eng", "sem", "emit", "deps", "inc_k", "val", "needed", "sec",
                 "async_cls", "dwaits")

    def __init__(self, eng, sem, emit, deps, inc_k, sec, async_cls):
        self.eng = eng
        self.sem = sem
        self.emit = emit
        self.deps = deps
        self.dwaits = {}
        self.inc_k = inc_k
        self.val = None
        self.needed = False
        self.sec = sec
        self.async_cls = async_cls


class Section:

    ENGS = ("tensor", "vector", "scalar", "gpsimd", "sync")
    COUNTER = [0]

    def __init__(self, nc, sems):
        self.nc = nc
        self.sems = sems
        self.ops = {e: [] for e in self.ENGS}
        Section.COUNTER[0] += 1
        self.sec_id = Section.COUNTER[0]

    def add(self, eng, emit, deps=(), dma=False, coll=False, sem=None):
        async_cls = dma or coll
        if async_cls:
            skey = sem or "io"
            inc_k = 16 if dma else 1
        else:
            skey = eng
            inc_k = 1
        op = Op(eng, skey, emit, [], inc_k, self.sec_id, async_cls)
        for d in deps:
            if d is None:
                continue
            if d.async_cls:
                op.dwaits[d.sem] = self.sems[d.sem].n
            elif d.sec == self.sec_id:
                d.needed = True
                op.deps.append(d)
        if async_cls:
            if self.sems[skey].n > 0:
                op.dwaits[skey] = max(op.dwaits.get(skey, 0), self.sems[skey].n)
            self.sems[skey].n += inc_k
            op.val = self.sems[skey].n
        self.ops[eng].append(op)
        return op

    def emit(self):
        for e in self.ENGS:
            for op in self.ops[e]:
                if not op.async_cls and op.needed:
                    self.sems[e].n += 1
                    op.val = self.sems[e].n
        with self.nc.Block() as blk:
            for ename in self.ENGS:
                if not self.ops[ename]:
                    continue

                def make_body(ename):
                    def body(eng):
                        for op in self.ops[ename]:
                            waits = dict(op.dwaits)
                            for d in op.deps:
                                if d.val is None:
                                    raise RuntimeError("dep without inc")
                                waits[d.sem] = max(waits.get(d.sem, 0), d.val)
                            for de, dv in waits.items():
                                if dv > 0:
                                    eng.wait_ge(self.sems[de].h, dv)
                            ins = op.emit(eng)
                            if op.val is not None:
                                ins.then_inc(self.sems[op.sem].h, op.inc_k)
                    return body

                getattr(blk, ename)(make_body(ename))


class Sem:
    __slots__ = ("h", "n")

    def __init__(self, h):
        self.h = h
        self.n = 0


def build_nc():
    nc = bass.Bass("TRN2", target_bir_lowering=False)

    x_ext = nc.dram_tensor("x_ext", [T + 1, C], F32, kind="ExternalInput").ap()
    wkT = nc.dram_tensor("wkT", [C, C], F32, kind="ExternalInput").ap()
    wvT = nc.dram_tensor("wvT", [C, C], F32, kind="ExternalInput").ap()
    wrT = nc.dram_tensor("wrT", [C, C], F32, kind="ExternalInput").ap()
    woT = nc.dram_tensor("woT", [C, C], F32, kind="ExternalInput").ap()
    qwT = nc.dram_tensor("qwT", [C, 512], F32, kind="ExternalInput").ap()
    kwT = nc.dram_tensor("kwT", [C, 512], F32, kind="ExternalInput").ap()
    vwT = nc.dram_tensor("vwT", [C, 512], F32, kind="ExternalInput").ap()
    owT0 = nc.dram_tensor("owT0", [C, C], F32, kind="ExternalInput").ap()
    owT1 = nc.dram_tensor("owT1", [C, C], F32, kind="ExternalInput").ap()
    veca = nc.dram_tensor("veca", [128, 49], F32, kind="ExternalInput").ap()
    vb_row = nc.dram_tensor("vb_row", [1, 512], F32, kind="ExternalInput").ap()
    ob_cols = nc.dram_tensor("ob_cols", [128, 16], F32, kind="ExternalInput").ap()
    OUT = nc.dram_tensor("OUT", [T, C], F32, kind="ExternalOutput").ap()

    r1_spill = nc.dram_tensor("r1_spill", [T, C], F32).ap()
    bnd_dram = nc.dram_tensor("bnd_dram", [C], BF16).ap()
    cc1_snd = [nc.dram_tensor(f"cc1s{i}", [128, 2], F32) for i in range(NCC)]
    cc1_rcv = [nc.dram_tensor(f"cc1r{i}", [2, 128, 2], F32) for i in range(NCC)]
    cc2_snd = [nc.dram_tensor(f"cc2s{a}", [128, NCC * (T // 2)], BF16)
               for a in range(2)]
    cc2_rcv = [nc.dram_tensor(f"cc2r{a}", [2, 128, NCC * (T // 2)], BF16)
               for a in range(2)]
    cc3_snd_a = nc.dram_tensor("cc3sa", [128, 2 * TFULL], BF16)
    cc3_rcv_a = nc.dram_tensor("cc3ra", [2, 128, 2 * TFULL], BF16)
    cc3_snd_b1 = nc.dram_tensor("cc3sb1", [128, 2 * T], BF16)
    cc3_rcv_b1 = nc.dram_tensor("cc3rb1", [2, 128, 2 * T], BF16)
    cc3_snd_b2 = nc.dram_tensor("cc3sb2", [128, 2 * T], BF16)
    cc3_rcv_b2 = nc.dram_tensor("cc3rb2", [2, 128, 2 * T], BF16)
    rr_dram = [nc.dram_tensor(f"rrd{h}", [1, 512], F32) for h in range(2)]
    RG = RG_GROUPS

    with contextlib.ExitStack() as _top:
        ec = _top.enter_context
        XR = ec(nc.sbuf_tensor("XR", [128, NT * C], F32))
        vp = ec(nc.sbuf_tensor("vp", [128, 52], F32))
        stat = ec(nc.sbuf_tensor("stat", [128, ST_COLS], F32))
        ident_b = ec(nc.sbuf_tensor("ident_b", [128, 128], BF16))
        ident_f = ec(nc.sbuf_tensor("ident_f", [128, 128], F32))
        ones_r1 = ec(nc.sbuf_tensor("ones_r1", [1, 128], BF16))
        ones_r1f = ec(nc.sbuf_tensor("ones_r1f", [1, 128], F32))
        ones_c1 = ec(nc.sbuf_tensor("ones_c1", [128, 1], BF16))
        ones128 = ec(nc.sbuf_tensor("ones128", [128, 128], BF16))
        carry = ec(nc.sbuf_tensor("carry", [128, 24], F32))
        st6 = ec(nc.sbuf_tensor("st6", [128, 12], F32))
        mv = ec(nc.sbuf_tensor("mv", [128, 4], F32))
        cbias = ec(nc.sbuf_tensor("cbias", [128, 2], F32))
        cst = ec(nc.sbuf_tensor("cst", [128, 1], F32))
        nc.const_aps.register(F32, EPS, cst[:, 0:1])
        sems = {n: Sem(ec(nc.semaphore("q_" + n)))
                for n in ("tensor", "vector", "scalar", "gpsimd", "sync",
                          "ld", "ld2", "w", "io", "io2", "cc", "cc1d", "spl", "rb",
                          "x0", "x1", "x2", "x3", "x4", "x5", "x6", "x7")}

        XRb = XR.ap().bitcast(BF16)

        def QT(j):
            return XRb[:, j * TFULL:(j + 1) * TFULL]

        def KT(j):
            return XRb[:, 16384 + j * TFULL:16384 + (j + 1) * TFULL]

        s = Section(nc, sems)
        o_m1 = s.add("gpsimd", lambda e: e.memset(ones_r1[:, :], 1.0))
        o_m2 = s.add("gpsimd", lambda e: e.memset(ones_r1f[:, :], 1.0))
        o_m3 = s.add("gpsimd", lambda e: e.memset(ones_c1[:, :], 1.0))
        o_m4 = s.add("gpsimd", lambda e: e.memset(ones128[:, :], 1.0))
        o_m5 = s.add("gpsimd", lambda e: e.memset(carry[:, :], 0.0))
        o_m6 = s.add("gpsimd", lambda e: e.memset(cst[:, :], EPS))
        o_vp = s.add("sync", lambda e: e.dma_start(out=vp[:, 0:49], in_=veca), dma=True, sem="ld")
        xload = []
        for i in range(NT):
            xload.append(s.add("sync", lambda e, i=i: e.dma_start(
                out=XR[:, i * C:(i + 1) * C],
                in_=x_ext[1 + i * 128:1 + (i + 1) * 128, :]), dma=True,
                sem="ld" if i % 2 == 0 else "ld2"))
        o_id = s.add("gpsimd", lambda e: e.affine_select(
            ident_b[:, :], ones128[:, :], [[1, 128]],
            compare_op=ALU.is_equal, fill=0.0, base=0, channel_multiplier=-1),
            deps=[o_m4])
        o_idf = s.add("vector", lambda e: e.tensor_copy(ident_f[:, :], ident_b[:, :]),
                      deps=[o_id])
        o_tm1 = s.add("vector", lambda e: e.tensor_scalar(
            stat[:, ST_TM1:ST_TM1 + 8], vp[:, VP_TM:VP_TM + 8], -1.0, 1.0,
            op0=ALU.mult, op1=ALU.add), deps=[o_vp])
        o_e = s.add("scalar", lambda e: e.activation(
            stat[:, ST_E:ST_E + 8], vp[:, VP_TD:VP_TD + 8], AF.Exp), deps=[o_vp])
        o_lam = s.add("scalar", lambda e: e.activation(
            stat[:, ST_LAM:ST_LAM + 8], stat[:, ST_E:ST_E + 8], AF.Exp, scale=-1.0),
            deps=[o_e])
        o_fw = s.add("scalar", lambda e: e.activation(
            stat[:, ST_FW:ST_FW + 8], vp[:, VP_TF:VP_TF + 8], AF.Exp), deps=[o_vp])
        o_nege = s.add("vector", lambda e: e.tensor_scalar_mul(
            stat[:, ST_NEGE:ST_NEGE + 8], stat[:, ST_E:ST_E + 8], -1.0), deps=[o_e])
        o_n2047 = s.add("vector", lambda e: e.tensor_scalar_mul(
            stat[:, ST_N2047E:ST_N2047E + 8], stat[:, ST_E:ST_E + 8], -2047.0),
            deps=[o_e])

        last_coll = [None]
        with contextlib.ExitStack() as _sx0:
            xmT = _sx0.enter_context(nc.sbuf_tensor("xmT", [128, NCC * T], BF16))
            rwkvT = _sx0.enter_context(nc.sbuf_tensor("rwkvT", [128, NCC * T], BF16))
            with contextlib.ExitStack() as _s1x:
                ec1 = _s1x.enter_context
                xnT = ec1(nc.sbuf_tensor("xnT", [128, NCC * TEC], BF16))
                xn_tile = ec1(nc.sbuf_tensor("xn_tile", [128, 2 * C], BF16))
                xb_row = ec1(nc.sbuf_tensor("xb_row", [1, C], F32))
                xb_st = ec1(nc.sbuf_tensor("xb_st", [1, 12], F32))
                xb_mv = ec1(nc.sbuf_tensor("xb_mv", [1, 6], F32))
                xb_bf = ec1(nc.sbuf_tensor("xb_bf", [1, C], BF16))
                ptr = ec1(nc.psum_tensor("ptr", [128, 2 * C], BF16))
                _dummy = 1
                stats_done = []
                prev = None
                for i in range(NT):
                    d = [xload[i]] + ([prev] if prev else [])
                    a = s.add("vector", lambda e, i=i: e.bn_stats(
                        st6[:, 0:6], XR[:, i * C:i * C + 512]), deps=d)
                    b = s.add("vector", lambda e, i=i: e.bn_stats(
                        st6[:, 6:12], XR[:, i * C + 512:i * C + 1024]), deps=[a])
                    g = s.add("vector", lambda e: e.bn_aggr(mv[:, 0:2], st6[:, :]),
                              deps=[a, b])
                    cm = s.add("vector", lambda e, i=i: e.tensor_copy(
                        stat[:, ST_MEAN1 + i:ST_MEAN1 + i + 1], mv[:, 0:1]), deps=[g])
                    cv = s.add("vector", lambda e, i=i: e.tensor_copy(
                        stat[:, ST_VAR1 + i:ST_VAR1 + i + 1], mv[:, 1:2]), deps=[g])
                    prev = cv
                    stats_done.extend([cm, cv])
                o_ln = s.add("scalar", lambda e: e.activation(
                    stat[:, ST_RSTD1:ST_RSTD1 + 16], stat[:, ST_VAR1:ST_VAR1 + 16],
                    AF.Ln, bias=EPS), deps=stats_done)
                o_rstd = s.add("scalar", lambda e: e.activation(
                    stat[:, ST_RSTD1:ST_RSTD1 + 16], stat[:, ST_RSTD1:ST_RSTD1 + 16],
                    AF.Exp, scale=-0.5), deps=[o_ln])
                o_nmr0 = s.add("vector", lambda e: e.tensor_mul(
                    stat[:, ST_NMR:ST_NMR + 16], stat[:, ST_MEAN1:ST_MEAN1 + 16],
                    stat[:, ST_RSTD1:ST_RSTD1 + 16]), deps=[o_rstd])
                o_nmr = s.add("vector", lambda e: e.tensor_scalar_mul(
                    stat[:, ST_NMR:ST_NMR + 16], stat[:, ST_NMR:ST_NMR + 16], -1.0),
                    deps=[o_nmr0])
                o_xb = s.add("sync", lambda e: e.dma_start(
                    out=xb_row[:, :], in_=x_ext[0:1, :]), dma=True, sem="ld")
                a = s.add("vector", lambda e: e.bn_stats(xb_st[:, 0:6], xb_row[:, 0:512]),
                          deps=[o_xb])
                b = s.add("vector", lambda e: e.bn_stats(xb_st[:, 6:12], xb_row[:, 512:1024]),
                          deps=[a])
                g = s.add("vector", lambda e: e.bn_aggr(xb_mv[:, 0:2], xb_st[:, :]),
                          deps=[a, b])
                l1 = s.add("scalar", lambda e: e.activation(
                    xb_mv[:, 2:3], xb_mv[:, 1:2], AF.Ln, bias=EPS), deps=[g])
                l2 = s.add("scalar", lambda e: e.activation(
                    xb_mv[:, 3:4], xb_mv[:, 2:3], AF.Exp, scale=-0.5), deps=[l1])
                l3 = s.add("vector", lambda e: e.tensor_mul(
                    xb_mv[:, 4:5], xb_mv[:, 0:1], xb_mv[:, 3:4]), deps=[l2, g])
                l4 = s.add("vector", lambda e: e.tensor_scalar_mul(
                    xb_mv[:, 4:5], xb_mv[:, 4:5], -1.0), deps=[l3])
                l5 = s.add("scalar", lambda e: e.activation(
                    xb_bf[:, :], xb_row[:, :], AF.Identity,
                    bias=xb_mv[:, 4:5], scale=xb_mv[:, 3:4]), deps=[l4])
                l6 = s.add("sync", lambda e: e.dma_start(
                    out=bnd_dram.rearrange("(o c) -> o c", o=1), in_=xb_bf[0:1, :]),
                    deps=[l5], dma=True, sem="ld")
                l7s = []
                for cc in range(NCC):
                    l7s.append(s.add("sync", lambda e, cc=cc: e.dma_start(
                        out=xnT[:, cc * TEC:cc * TEC + 1],
                        in_=bnd_dram[cc * 128:(cc + 1) * 128].rearrange("(p o) -> p o", o=1)),
                        deps=[l6], dma=True, sem="ld"))
                l7 = l7s[-1]
                ap_ops = [None] * NT
                tr_ops = [None] * NT
                cp_ops = [None] * NT
                for i in range(NT):
                    d = [o_rstd, o_nmr, xload[i]]
                    if i >= 2:
                        d.append(tr_ops[i - 2])
                    ap_ops[i] = s.add("scalar", lambda e, i=i: e.activation(
                        xn_tile[:, (i % 2) * C:(i % 2) * C + C],
                        XR[:, i * C:(i + 1) * C], AF.Identity,
                        bias=stat[:, ST_NMR + i:ST_NMR + i + 1],
                        scale=stat[:, ST_RSTD1 + i:ST_RSTD1 + i + 1]), deps=d)

                    def em_tr(e, i=i):
                        for cc in range(NCC):
                            ins = e.transpose(
                                ptr[:, (i % 2) * C + cc * 128:(i % 2) * C + cc * 128 + 128],
                                xn_tile[:, (i % 2) * C + cc * 128:(i % 2) * C + cc * 128 + 128],
                                ident_b[:, :])
                        return ins
                    d = [ap_ops[i], o_id]
                    if i >= 2:
                        d.append(cp_ops[i - 2])
                    tr_ops[i] = s.add("tensor", em_tr, deps=d)
                    cp_ops[i] = s.add("vector", lambda e, i=i: e.tensor_copy(
                        xnT.ap().rearrange("p (cc t) -> p cc t", cc=NCC)[:, :, 1 + i * 128:1 + (i + 1) * 128],
                        ptr.ap().rearrange("p (b cc t) -> p b cc t", b=2, cc=NCC)[:, i % 2]),
                        deps=[tr_ops[i]])
                mix1 = []
                for cc in range(NCC):
                    mix1.append(s.add("vector", lambda e, cc=cc: e.tensor_scalar_mul(
                        xmT[:, cc * T:(cc + 1) * T],
                        xnT[:, cc * TEC:cc * TEC + T],
                        stat[:, ST_TM1 + cc:ST_TM1 + cc + 1]),
                        deps=[o_tm1] + l7s + cp_ops))
                mix_done = []
                for cc in range(NCC):
                    mix_done.append(s.add("vector", lambda e, cc=cc: e.scalar_tensor_tensor(
                        xmT[:, cc * T:(cc + 1) * T],
                        xnT[:, cc * TEC + 1:cc * TEC + 1 + T],
                        vp[:, VP_TM + cc:VP_TM + cc + 1],
                        xmT[:, cc * T:(cc + 1) * T],
                        op0=ALU.mult, op1=ALU.add), deps=[mix1[cc]]))
                s.emit()

            if BUILD_UPTO == 12:
                _tap_out(nc, sems, XR, OUT, [])
                return nc
            with contextlib.ExitStack() as _s2x:
                ec2 = _s2x.enter_context
                wbuf = ec2(nc.sbuf_tensor("wbuf", [128, 2 * 2 * C], BF16))
                sl1 = ec2(nc.sbuf_tensor("sl1", [128, 2 * T], F32))
                sl2 = ec2(nc.sbuf_tensor("sl2", [128, 2 * T], F32))
                sl3 = ec2(nc.sbuf_tensor("sl3", [128, 2 * T], F32))
                sl6 = ec2(nc.sbuf_tensor("sl6", [128, 2 * T], F32))
                lamb = ec2(nc.sbuf_tensor("lamb", [128, T], BF16))
                psA = ec2(nc.psum_tensor("psA", [128, 2048], F32))
                psB = ec2(nc.psum_tensor("psB", [128, 1024], F32))
                psR = ec2(nc.psum_tensor("psR", [128, 1024], F32))
                _dummy = 1
                s = Section(nc, sems)
                rwkv_ops = [None] * NCC
                prev = {}
                E60 = float(np.exp(60.0))

                def emit_cc(oc):
                    if ("ccargs", oc) not in prev:
                        return
                    (snd, CM, xs) = prev.pop(("ccargs", oc))
                    cc = s.add("gpsimd", lambda e, oc=oc: e.collective_compute(
                        "AllGather", ALU.bypass, replica_groups=RG,
                        ins=[cc1_snd[oc].ap().opt()],
                        outs=[cc1_rcv[oc].ap().opt()]),
                        deps=[snd], coll=True, sem=xs)
                    last_coll[0] = cc
                    rcv = s.add("sync", lambda e, oc=oc, CM=CM: e.dma_start(
                        out=carry[:, CM:CM + 2], in_=cc1_rcv[oc].ap()[0]),
                        deps=[cc, prev.get(("cmask", oc - 2))], dma=True, sem=xs)
                    cmask = s.add("vector", lambda e, CM=CM: e.tensor_scalar_mul(
                        carry[:, CM + 2:CM + 4], carry[:, CM:CM + 2],
                        vp[:, VP_HM:VP_HM + 1]),
                        deps=[rcv, o_vp])
                    prev[("cmask", oc)] = cmask

                def emit_tail(oc):
                    if ("tailargs", oc) not in prev:
                        return
                    (s1, s2, s3, s6, lam0, CM,
                     w1b, w2e, er, ck) = prev.pop(("tailargs", oc))
                    cmask = prev[("cmask", oc)]
                    c1 = s.add("vector", lambda e, s2=s2, CM=CM:
                               e.scalar_tensor_tensor(
                                   s2, lamb[:, :], carry[:, CM + 2:CM + 3], s2,
                                   op0=ALU.mult, op1=ALU.add),
                               deps=[w1b, lam0, cmask])
                    c2 = s.add("vector", lambda e, s1=s1, CM=CM:
                               e.scalar_tensor_tensor(
                                   s1, lamb[:, :], carry[:, CM + 3:CM + 4], s1,
                                   op0=ALU.mult, op1=ALU.add),
                               deps=[w2e, lam0, cmask, c1])
                    prev[("lamused", oc)] = c2
                    dd = s.add("vector", lambda e, s1=s1, s3=s3, s6=s6:
                               e.scalar_tensor_tensor(
                                   s3, s6, 1.0, s1, op0=ALU.add, op1=ALU.mult),
                               deps=[er, c2])
                    prev[("dd", oc)] = dd
                    prev[("t2", oc)] = (s2, s3, c1, ck, dd)

                def emit_tail2a(oc):
                    if ("t2", oc) not in prev:
                        return
                    (s2, s3, c1, ck, dd) = prev[("t2", oc)]
                    lnd = s.add("scalar", lambda e, s3=s3: e.activation(s3, s3, AF.Ln),
                                deps=[dd])
                    invd = s.add("scalar", lambda e, s3=s3: e.activation(
                        s3, s3, AF.Exp, scale=-1.0), deps=[lnd])
                    prev[("invd", oc)] = invd

                def emit_tail2b(oc):
                    if ("t2", oc) not in prev:
                        return
                    (s2, s3, c1, ck, dd) = prev.pop(("t2", oc))
                    invd = prev.pop(("invd", oc))
                    rw = s.add("vector", lambda e, s2=s2, s3=s3, oc=oc: e.tensor_mul(
                        rwkvT[:, oc * T:(oc + 1) * T], s2, s3),
                        deps=[invd, c1, ck])
                    prev[("rw", oc)] = rw
                    rwkv_ops[oc] = rw

                for oc in range(NCC):
                    par = oc % 2
                    s1 = sl1[:, par * T:par * T + T]
                    s2 = sl2[:, par * T:par * T + T]
                    s3 = sl3[:, par * T:par * T + T]
                    s6 = sl6[:, par * T:par * T + T]
                    trash = rwkvT[:, oc * T:(oc + 1) * T]
                    ecol = stat[:, ST_E + oc:ST_E + oc + 1]
                    negecol = stat[:, ST_NEGE + oc:ST_NEGE + oc + 1]
                    n2047col = stat[:, ST_N2047E + oc:ST_N2047E + oc + 1]
                    fwcol = stat[:, ST_FW + oc:ST_FW + oc + 1]
                    lamcol = stat[:, ST_LAM + oc:ST_LAM + oc + 1]
                    base = (oc % 2) * 2 * C
                    CM = 16 + 4 * par

                    wdeps = [prev.get(("rmm", oc - 2))] if oc >= 2 else []
                    wk_d = s.add("gpsimd", lambda e, oc=oc, base=base: e.dma_start(
                        out=wbuf[:, base:base + C].rearrange("p (ic m) -> p ic m", ic=NCC),
                        in_=wkT[:, oc * 128:(oc + 1) * 128]
                        .rearrange("(ic p) m -> p ic m", p=128)), deps=wdeps, dma=True, sem="w")
                    wv_d = s.add("gpsimd", lambda e, oc=oc, base=base: e.dma_start(
                        out=wbuf[:, base + C:base + 2 * C].rearrange("p (ic m) -> p ic m", ic=NCC),
                        in_=wvT[:, oc * 128:(oc + 1) * 128]
                        .rearrange("(ic p) m -> p ic m", p=128)), deps=wdeps, dma=True, sem="w")
                    o_iota = s.add("gpsimd", lambda e, s3=s3: e.iota(
                        s3, [[1, T]], base=0, channel_multiplier=0,
                        allow_small_or_imprecise_dtypes=True),
                        deps=[prev.get(("rw", oc - 2))])


                    def emit_mms_h(e, ps, woff, h, base=base, po=0):
                        for qt in range(2 * h, 2 * h + 2):
                            for ic in range(NCC):
                                ins = e.matmul(
                                    ps[:, po + (qt - 2 * h) * 512:po + (qt - 2 * h + 1) * 512],
                                    wbuf[:, base + woff + ic * 128:base + woff + ic * 128 + 128],
                                    xmT[:, ic * T + qt * 512:ic * T + (qt + 1) * 512],
                                    start=(ic == 0), stop=(ic == NCC - 1))
                        return ins

                    kmm_h = [None, None]
                    vmm_h = [None, None]
                    rmm_h = [None, None]
                    ek_h = [None, None]
                    vev_h = [None, None]
                    er_h = [None, None]
                    for h in (0, 1):
                        d = [wk_d] + mix_done
                        pk = prev.get(("ek", oc - 1, h))
                        if pk is not None:
                            d.append(pk)
                        kmm_h[h] = s.add("tensor",
                                         lambda e, h=h, f=emit_mms_h:
                                         f(e, psA, 0, h, po=h * 1024),
                                         deps=d)
                        ek_h[h] = s.add("scalar", lambda e, s1=s1, h=h: e.activation(
                            s1[:, h * 1024:(h + 1) * 1024],
                            psA[:, h * 1024:(h + 1) * 1024], AF.Exp),
                            deps=[kmm_h[h], prev.get(("dd", oc - 2))])
                        prev[("ek", oc, h)] = ek_h[h]
                    wr_d = s.add("gpsimd", lambda e, oc=oc, base=base: e.dma_start(
                        out=wbuf[:, base:base + C].rearrange("p (ic m) -> p ic m", ic=NCC),
                        in_=wrT[:, oc * 128:(oc + 1) * 128]
                        .rearrange("(ic p) m -> p ic m", p=128)),
                        deps=[kmm_h[1]], dma=True, sem="w")
                    emit_cc(oc - 1)
                    emit_tail(oc - 1)
                    for h in (0, 1):
                        d = [wv_d]
                        pv = prev.get(("vev", oc - 1, 1)) if h == 0 else vev_h[0]
                        if pv is not None:
                            d.append(pv)
                        vmm_h[h] = s.add("tensor",
                                         lambda e, h=h, f=emit_mms_h: f(e, psB, C, h),
                                         deps=d)
                        vev_h[h] = s.add("scalar", lambda e, s2=s2, h=h: e.activation(
                            s2[:, h * 1024:(h + 1) * 1024], psB[:, :], AF.Identity),
                            deps=[vmm_h[h], prev.get(("rw", oc - 2))])
                        prev[("vev", oc, h)] = vev_h[h]
                    for h in (0, 1):
                        d = [wr_d]
                        pr = prev.get(("er", oc - 1, 1)) if h == 0 else er_h[0]
                        if pr is not None:
                            d.append(pr)
                        rmm_h[h] = s.add("tensor",
                                         lambda e, h=h, f=emit_mms_h: f(e, psR, 0, h),
                                         deps=d)
                        er_h[h] = s.add("scalar", lambda e, s6=s6, h=h: e.activation(
                            s6[:, h * 1024:(h + 1) * 1024], psR[:, :], AF.Exp, scale=-1.0),
                            deps=[rmm_h[h], prev.get(("dd", oc - 2))])
                        prev[("er", oc, h)] = er_h[h]
                    ek = ek_h[1]
                    vev = vev_h[1]
                    er = er_h[1]
                    prev[("er", oc)] = er
                    prev[("vev", oc)] = vev
                    prev[("rmm", oc)] = rmm_h[1]
                    lam0 = s.add("scalar", lambda e, negecol=negecol, s3=s3: e.activation(
                        lamb[:, :], s3, AF.Exp, scale=negecol),
                        deps=[o_iota, o_nege, prev.get(("lamused", oc - 1))])
                    emit_tail2a(oc - 1)
                    clp = s.add("vector", lambda e, s1=s1: e.tensor_scalar_min(s1, s1, E60),
                                deps=[ek, prev.get(("dd", oc - 2))])
                    kv = s.add("vector", lambda e, s1=s1, s2=s2: e.tensor_mul(s2, s1, s2),
                               deps=[clp, vev])
                    emit_tail2b(oc - 1)
                    lrev = s.add("scalar", lambda e, s3=s3, ecol=ecol, n2047col=n2047col:
                                 e.activation(s3, s3, AF.Exp,
                                              scale=ecol, bias=n2047col),
                                 deps=[o_iota, o_e, o_n2047, lam0])
                    ckv = s.add("vector", lambda e, s2=s2, s3=s3, trash=trash, oc=oc:
                                e.scalar_tensor_tensor(
                                    trash, s2, 1.0, s3, op0=ALU.mult, op1=ALU.mult,
                                    accum_out=carry[:, 2 * oc:2 * oc + 1]),
                                deps=[kv, lrev, o_m5])
                    ck = s.add("vector", lambda e, s1=s1, s3=s3, trash=trash, oc=oc:
                               e.scalar_tensor_tensor(
                                   trash, s1, 1.0, s3, op0=ALU.mult, op1=ALU.mult,
                                   accum_out=carry[:, 2 * oc + 1:2 * oc + 2]),
                               deps=[ckv])
                    xs = f"x{oc}"
                    snd = s.add("sync", lambda e, oc=oc: e.dma_start(
                        out=cc1_snd[oc].ap(), in_=carry[:, 2 * oc:2 * oc + 2]),
                        deps=[ck], dma=True, sem=xs)
                    prev[("ccargs", oc)] = (snd, CM, xs)
                    sckv = s.add("vector", lambda e, s2=s2, s3=s3, lamcol=lamcol:
                                 e.tensor_tensor_scan(s3, lamcol.broadcast_to([128, T]),
                                                      s2, 0.0, op0=ALU.mult, op1=ALU.add),
                                 deps=[kv, ckv, ck, o_lam, lrev])
                    w1 = s.add("vector", lambda e, s2=s2, s3=s3, fwcol=fwcol:
                               e.scalar_tensor_tensor(
                                   s2[:, 1:T], s2[:, 1:T], fwcol, s3[:, 0:T - 1],
                                   op0=ALU.mult, op1=ALU.add), deps=[sckv, ck])
                    w1b = s.add("vector", lambda e, s2=s2, fwcol=fwcol:
                                e.tensor_scalar_mul(s2[:, 0:1], s2[:, 0:1], fwcol),
                                deps=[w1])
                    sck = s.add("vector", lambda e, s1=s1, s3=s3, lamcol=lamcol:
                                e.tensor_tensor_scan(s3, lamcol.broadcast_to([128, T]),
                                                     s1, 0.0, op0=ALU.mult, op1=ALU.add),
                                deps=[w1, ck])
                    w2 = s.add("vector", lambda e, s1=s1, s3=s3, fwcol=fwcol:
                               e.scalar_tensor_tensor(
                                   s1[:, 1:T], s1[:, 1:T], fwcol, s3[:, 0:T - 1],
                                   op0=ALU.mult, op1=ALU.add), deps=[sck])
                    w2b = s.add("vector", lambda e, s1=s1, fwcol=fwcol:
                                e.tensor_scalar_mul(s1[:, 0:1], s1[:, 0:1], fwcol),
                                deps=[w2])
                    w2e = s.add("vector", lambda e, s1=s1: e.tensor_scalar_add(
                        s1, s1, K_EPS), deps=[w2b])
                    prev[("tailargs", oc)] = (s1, s2, s3, s6, lam0, CM,
                                              w1b, w2e, er, ck)
                emit_cc(NCC - 1)
                emit_tail(NCC - 1)
                emit_tail2a(NCC - 1)
                emit_tail2b(NCC - 1)
                s.emit()

            if BUILD_UPTO == 13:
                _tap_out(nc, sems, XR, OUT, rwkv_ops)
                return nc
            with contextlib.ExitStack() as _s3x:
                ec3 = _s3x.enter_context
                wo_sb = ec3(nc.sbuf_tensor("wo_sb", [128, NCC * C], BF16))
                o_ev = ec3(nc.sbuf_tensor("o_ev", [128, 2 * 512], F32))
                psO = ec3(nc.psum_tensor("psO", [128, 2 * 512], F32))
                psT = ec3(nc.psum_tensor("psT", [128, 2 * 512], F32))
                _dummy = 1
                s = Section(nc, sems)
                wo_d = s.add("gpsimd", lambda e: e.dma_start(
                    out=wo_sb.ap().rearrange("p (ic m) -> p ic m", ic=NCC),
                    in_=woT.rearrange("(ic p) m -> p ic m", p=128)), dma=True, sem="w")
                r1_ops = []
                prev_ev = {}
                prev_tr = {}
                prev_add = {}
                k = 0
                for oc in range(NCC):
                    for qt in range(4):
                        sl = k % 2

                        def em_owmm(e, oc=oc, qt=qt, sl=sl):
                            for ic in range(NCC):
                                ins = e.matmul(
                                    psO[:, sl * 512:sl * 512 + 512],
                                    wo_sb[:, ic * C + oc * 128:ic * C + oc * 128 + 128],
                                    rwkvT[:, ic * T + qt * 512:ic * T + (qt + 1) * 512],
                                    start=(ic == 0), stop=(ic == NCC - 1))
                            return ins
                        d = [wo_d] + rwkv_ops
                        if k >= 2:
                            d.append(prev_ev[k - 2])
                        mm = s.add("tensor", em_owmm, deps=d)
                        ev = s.add("vector", lambda e, sl=sl: e.tensor_copy(
                            o_ev[:, sl * 512:sl * 512 + 512], psO[:, sl * 512:sl * 512 + 512]),
                            deps=[mm] + ([prev_tr[k - 2]] if k >= 2 else []))
                        prev_ev[k] = ev

                        def em_tr(e, oc=oc, qt=qt, sl=sl):
                            for j in range(4):
                                ins = e.transpose(
                                    psT[:, sl * 512 + j * 128:sl * 512 + j * 128 + 128],
                                    o_ev[:, sl * 512 + j * 128:sl * 512 + j * 128 + 128],
                                    ident_f[:, :])
                            return ins
                        tr = s.add("tensor", em_tr,
                                   deps=[ev, o_idf] + ([prev_add[k - 2]] if k >= 2 else []))
                        prev_tr[k] = tr

                        def em_add(e, oc=oc, qt=qt, sl=sl):
                            return e.tensor_add(
                                XR.ap().rearrange("p (n c) -> p n c", n=NT)[
                                    :, 4 * qt:4 * qt + 4, oc * 128:oc * 128 + 128],
                                XR.ap().rearrange("p (n c) -> p n c", n=NT)[
                                    :, 4 * qt:4 * qt + 4, oc * 128:oc * 128 + 128],
                                psT.ap().rearrange("p (b j t) -> p (b j) t", b=2, j=4)[
                                    :, sl * 4:sl * 4 + 4, :])
                        ad = s.add("vector", em_add, deps=[tr])
                        prev_add[k] = ad
                        r1_ops.append(ad)
                        k += 1

                xn2o = ec3(nc.sbuf_tensor("xn2o", [128, NCC * T], BF16))
                x2_tile = ec3(nc.sbuf_tensor("x2_tile", [128, 2 * C], BF16))
                ptr2 = ec3(nc.psum_tensor("ptr2", [128, 2 * C], BF16))

                stats2 = []
                prevv = None
                for i in range(NT):
                    d = [r1_ops[7 * 4 + i // 4]] + ([prevv] if prevv else [])
                    a = s.add("vector", lambda e, i=i: e.bn_stats(
                        st6[:, 0:6], XR[:, i * C:i * C + 512]), deps=d)
                    b = s.add("vector", lambda e, i=i: e.bn_stats(
                        st6[:, 6:12], XR[:, i * C + 512:i * C + 1024]), deps=[a])
                    g = s.add("vector", lambda e: e.bn_aggr(mv[:, 0:2], st6[:, :]), deps=[a, b])
                    cm = s.add("vector", lambda e, i=i: e.tensor_copy(
                        stat[:, ST_MEAN1 + i:ST_MEAN1 + i + 1], mv[:, 0:1]), deps=[g])
                    cv = s.add("vector", lambda e, i=i: e.tensor_copy(
                        stat[:, ST_VAR1 + i:ST_VAR1 + i + 1], mv[:, 1:2]), deps=[g])
                    prevv = cv
                    stats2.extend([cm, cv])
                LNU, IU, V2 = ST_SY, ST_BY, ST_W1
                o1 = s.add("scalar", lambda e: e.activation(
                    stat[:, LNU:LNU + 16], stat[:, ST_VAR1:ST_VAR1 + 16], AF.Ln, bias=EPS),
                    deps=stats2)
                o2 = s.add("scalar", lambda e: e.activation(
                    stat[:, ST_RSTD1:ST_RSTD1 + 16], stat[:, LNU:LNU + 16], AF.Exp, scale=-0.5),
                    deps=[o1])
                o3 = s.add("scalar", lambda e: e.activation(
                    stat[:, IU:IU + 16], stat[:, LNU:LNU + 16], AF.Exp, scale=-1.0), deps=[o1])
                o4 = s.add("vector", lambda e: e.tensor_mul(
                    stat[:, V2:V2 + 16], stat[:, ST_VAR1:ST_VAR1 + 16], stat[:, IU:IU + 16]),
                    deps=[o3])
                o5 = s.add("scalar", lambda e: e.activation(
                    stat[:, V2:V2 + 16], stat[:, V2:V2 + 16], AF.Ln, bias=EPS), deps=[o4])
                o6 = s.add("scalar", lambda e: e.activation(
                    stat[:, V2:V2 + 16], stat[:, V2:V2 + 16], AF.Exp, scale=-0.5), deps=[o5])
                o7 = s.add("vector", lambda e: e.tensor_mul(
                    stat[:, ST_SY:ST_SY + 16], stat[:, ST_RSTD1:ST_RSTD1 + 16],
                    stat[:, V2:V2 + 16]), deps=[o6, o2, o3])
                o8 = s.add("vector", lambda e: e.tensor_mul(
                    stat[:, ST_BY:ST_BY + 16], stat[:, ST_MEAN1:ST_MEAN1 + 16],
                    stat[:, ST_SY:ST_SY + 16]), deps=[o7])
                o9 = s.add("vector", lambda e: e.tensor_scalar_mul(
                    stat[:, ST_BY:ST_BY + 16], stat[:, ST_BY:ST_BY + 16], -1.0), deps=[o8])
                oW = s.add("vector", lambda e: e.tensor_scalar(
                    stat[:, ST_W1:ST_W1 + 16], stat[:, ST_RSTD1:ST_RSTD1 + 16], 1.0, 1.0,
                    op0=ALU.mult, op1=ALU.add), deps=[o8, o6])
                oN = s.add("vector", lambda e: e.tensor_mul(
                    stat[:, ST_NMR:ST_NMR + 16], stat[:, ST_MEAN1:ST_MEAN1 + 16],
                    stat[:, ST_RSTD1:ST_RSTD1 + 16]), deps=[o7])
                oN2 = s.add("vector", lambda e: e.tensor_scalar_mul(
                    stat[:, ST_NMR:ST_NMR + 16], stat[:, ST_NMR:ST_NMR + 16], -1.0), deps=[oN])
                tr2 = [None] * NT
                cp2 = [None] * NT
                ap2 = [None] * NT
                snd2h = {}
                cc2h = {}
                for i in range(NT):
                    d = [o7, o9]
                    if i >= 2:
                        d.append(tr2[i - 2])
                    ap2[i] = s.add("scalar", lambda e, i=i: e.activation(
                        x2_tile[:, (i % 2) * C:(i % 2) * C + C],
                        XR[:, i * C:(i + 1) * C], AF.Identity,
                        bias=stat[:, ST_BY + i:ST_BY + i + 1],
                        scale=stat[:, ST_SY + i:ST_SY + i + 1]), deps=d)

                    def em_tr2(e, i=i):
                        for cc in range(NCC):
                            ins = e.transpose(
                                ptr2[:, (i % 2) * C + cc * 128:(i % 2) * C + cc * 128 + 128],
                                x2_tile[:, (i % 2) * C + cc * 128:(i % 2) * C + cc * 128 + 128],
                                ident_b[:, :])
                        return ins
                    d = [ap2[i]]
                    if i >= 2:
                        d.append(cp2[i - 2])
                    tr2[i] = s.add("tensor", em_tr2, deps=d)
                    cp2[i] = s.add("vector", lambda e, i=i: e.tensor_copy(
                        xn2o.ap().rearrange("p (cc t) -> p cc t", cc=NCC)[:, :, i * 128:(i + 1) * 128],
                        ptr2.ap().rearrange("p (b cc t) -> p b cc t", b=2, cc=NCC)[:, i % 2]),
                        deps=[tr2[i]])
                    if i in (7, 15):
                        hh = i // 8
                        affh = []
                        for cc in range(NCC):
                            affh.append(s.add("vector", lambda e, cc=cc, hh=hh: e.tensor_scalar(
                                xn2o[:, cc * T + hh * 1024:cc * T + hh * 1024 + 1024],
                                xn2o[:, cc * T + hh * 1024:cc * T + hh * 1024 + 1024],
                                vp[:, VP_NW + cc:VP_NW + cc + 1],
                                vp[:, VP_NB + cc:VP_NB + cc + 1],
                                op0=ALU.mult, op1=ALU.add),
                                deps=[o_vp] + cp2[hh * 8:hh * 8 + 8]))
                        snd2h[hh] = s.add("sync", lambda e, hh=hh: e.dma_start(
                            out=cc2_snd[hh].ap().rearrange("p (cc t) -> p cc t", cc=NCC),
                            in_=xn2o.ap().rearrange("p (cc t) -> p cc t", cc=NCC)[
                                :, :, hh * 1024:hh * 1024 + 1024]),
                            deps=affh, dma=True, sem="io")
                        cc2h[hh] = s.add("gpsimd", lambda e, hh=hh: e.collective_compute(
                            "AllGather", ALU.bypass, replica_groups=RG,
                            ins=[cc2_snd[hh].ap().opt()], outs=[cc2_rcv[hh].ap().opt()]),
                            deps=[snd2h[hh], last_coll[0] if hh == 0 else cc2h[0]],
                            coll=True, sem="cc")
                cc2 = cc2h[1]
                spill = s.add("sync", lambda e: e.dma_start(
                    out=r1_spill.rearrange("(n p) c -> p n c", p=128),
                    in_=XR.ap().rearrange("p (n c) -> p n c", c=C)),
                    deps=[ap2[NT - 1]] + stats2, dma=True, sem="spl")
                s.emit()


        if BUILD_UPTO == 1:
            _tap_out(nc, sems, XR, OUT, r1_ops)
            return nc

        qk_ops = []
        v_ops = []
        with contextlib.ExitStack() as _s5x:
            ec5 = _s5x.enter_context
            Vt = ec5(nc.sbuf_tensor("Vt", [128, 32 * 512], BF16))
            qmkm = ec5(nc.sbuf_tensor("qmkm", [128, 20], F32))
            stt_row = ec5(nc.sbuf_tensor("stt_row", [1, 4], F32))
            _dummy = 1
            with contextlib.ExitStack() as _s5y:
                ec5y = _s5y.enter_context
                xn2f = ec5y(nc.sbuf_tensor("xn2f", [128, NCC * TFULL], BF16))
                pw_sb = ec5y(nc.sbuf_tensor("pw_sb", [128, 2 * 8 * 512], BF16))
                vw_sb = ec5y(nc.sbuf_tensor("vw_sb", [128, 8 * 512], BF16))
                vb_sb = ec5y(nc.sbuf_tensor("vb_sb", [1, 512], BF16))
                _dummy = 1
                s = Section(nc, sems)
                wq_d = s.add("gpsimd", lambda e: e.dma_start(
                    out=pw_sb[:, 0:8 * 512].rearrange("p (ic m) -> p ic m", ic=NCC),
                    in_=qwT.rearrange("(ic p) m -> p ic m", p=128)), dma=True, sem="w")
                wk_d = s.add("gpsimd", lambda e: e.dma_start(
                    out=pw_sb[:, 8 * 512:16 * 512].rearrange("p (ic m) -> p ic m", ic=NCC),
                    in_=kwT.rearrange("(ic p) m -> p ic m", p=128)), dma=True, sem="w")
                wv_d = s.add("gpsimd", lambda e: e.dma_start(
                    out=vw_sb.ap().rearrange("p (ic m) -> p ic m", ic=NCC),
                    in_=vwT.rearrange("(ic p) m -> p ic m", p=128)), dma=True, sem="w")
                vb_d = s.add("gpsimd", lambda e: e.dma_start(
                    out=vb_sb[:, :], in_=vb_row), dma=True, sem="w")
                rcvh = {0: [], 1: []}
                for a in range(2):
                    for r in range(2):
                        dd = [cc2h[a], spill]
                        if a == 0:
                            dd.append(snd2h[1])
                        rcvh[a].append(s.add("sync", lambda e, a=a, r=r: e.dma_start(
                            out=xn2f.ap().rearrange("p (cc t) -> p cc t", cc=NCC)[
                                :, :, r * T + a * 1024:r * T + a * 1024 + 1024],
                            in_=cc2_rcv[a].ap()[r].rearrange("p (cc t) -> p cc t", cc=NCC)),
                            deps=dd, dma=True, sem="io" if r == 0 else "io2"))

                with contextlib.ExitStack() as _s5p:
                    psP = _s5p.enter_context(nc.psum_tensor("psP", [128, 2 * 512], F32))
                    prev_ev = {}
                    red_ops = []
                    k5 = 0
                    units = []
                    for half in range(2):
                        qts = (0, 1, 4, 5) if half == 0 else (2, 3, 6, 7)
                        tcs = (list(range(0, 8)) + list(range(16, 24)) if half == 0
                               else list(range(8, 16)) + list(range(24, 32)))
                        for which, woff, dst, bcol in (("q", 0, QT, VP_QB),
                                                       ("k", 8 * 512, KT, VP_KB)):
                            for j in range(4):
                                for qt in qts:
                                    units.append(("qk", half, which, woff, dst, bcol, j, qt))
                                if half == 1:
                                    units.append(("red", which, j))
                        for tc in tcs:
                            units.append(("v", half, tc))
                    qk_by = {}
                    for u in units:
                        if u[0] == "red":
                            _, which, j = u
                            rbase = 4 if which == "q" else 12
                            xbase = (0 if which == "q" else 16384) + j * 4096
                            red_ops.append(s.add("vector", lambda e, rbase=rbase, j=j, xbase=xbase:
                                e.tensor_reduce(
                                    qmkm[:, rbase + j:rbase + j + 1],
                                    XRb[:, xbase:xbase + 4096],
                                    axis=AX.X, op=ALU.max, apply_absolute_value=True),
                                deps=qk_by[(which, j)]))
                            continue
                        sl = k5 % 2
                        if u[0] == "qk":
                            _, half, which, woff, dst, bcol, j, qt = u
                            nic = NCC >> (qt // 2)

                            def em_p(e, j=j, qt=qt, nic=nic, sl=sl, woff=woff):
                                for ic in range(nic):
                                    ins = e.matmul(
                                        psP[:, sl * 512:sl * 512 + 512],
                                        pw_sb[:, woff + ic * 512 + j * 128:woff + ic * 512 + j * 128 + 128],
                                        xn2f[:, ic * TFULL + qt * 512:ic * TFULL + (qt + 1) * 512],
                                        start=(ic == 0), stop=(ic == nic - 1))
                                return ins
                            d = rcvh[half] + [wq_d if which == "q" else wk_d, spill]
                            if k5 >= 2:
                                d.append(prev_ev[k5 - 2])
                            mm = s.add("tensor", em_p, deps=d)
                            ev = s.add("vector", lambda e, j=j, qt=qt, sl=sl, dst=dst, bcol=bcol: e.tensor_scalar(
                                dst(j)[:, qt * 512:(qt + 1) * 512],
                                psP[:, sl * 512:sl * 512 + 512],
                                vp[:, bcol + j:bcol + j + 1], None, op0=ALU.add),
                                deps=[mm, o_vp])
                            qk_ops.append(ev)
                            qk_by.setdefault((which, j), []).append(ev)
                        else:
                            _, half, tc = u
                            nic = NCC >> (tc // 8)

                            def em_v(e, tc=tc, nic=nic, sl=sl):
                                for ic in range(nic):
                                    e.matmul(
                                        psP[:, sl * 512:sl * 512 + 512],
                                        xn2f[:, ic * TFULL + tc * 128:ic * TFULL + tc * 128 + 128],
                                        vw_sb[:, ic * 512:(ic + 1) * 512],
                                        start=(ic == 0), stop=False)
                                ins = e.matmul(psP[:, sl * 512:sl * 512 + 512],
                                               ones_r1[:, :], vb_sb[:, :],
                                               start=False, stop=True)
                                return ins
                            d = rcvh[half] + [wv_d, vb_d, o_m1]
                            if k5 >= 2:
                                d.append(prev_ev[k5 - 2])
                            mm = s.add("tensor", em_v, deps=d)
                            ev = s.add("vector", lambda e, tc=tc, sl=sl: e.tensor_copy(
                                Vt[:, tc * 512:(tc + 1) * 512], psP[:, sl * 512:sl * 512 + 512]),
                                deps=[mm])
                            v_ops.append(ev)
                        prev_ev[k5] = ev
                        k5 += 1
                    g1 = s.add("vector", lambda e: e.tensor_reduce(
                        qmkm[:, 0:1], qmkm[:, 4:8], axis=AX.X, op=ALU.max),
                        deps=qk_ops + red_ops)
                    g2 = s.add("vector", lambda e: e.tensor_reduce(
                        qmkm[:, 1:2], qmkm[:, 12:16], axis=AX.X, op=ALU.max),
                        deps=qk_ops + red_ops + [g1])
                    with nc.psum_tensor("psst", [128, 132], F32) as psst:
                        t1_ = s.add("tensor", lambda e: e.transpose(
                            psst[0:2, 0:128], qmkm[:, 0:2], ident_f[:, :]),
                            deps=[g1, g2, o_idf])
                        r1_ = s.add("vector", lambda e: e.tensor_reduce(
                            qmkm[0:2, 0:1], psst[0:2, 0:128], axis=AX.X, op=ALU.max),
                            deps=[t1_, g2])
                        t2_ = s.add("tensor", lambda e: e.transpose(
                            psst[0:1, 128:130], qmkm[0:2, 0:1], ident_f[0:2, 0:2]),
                            deps=[r1_])
                        m0_ = s.add("vector", lambda e: e.tensor_copy(
                            stt_row[0:1, 2:4], psst[0:1, 128:130]), deps=[t2_])
                        m1_ = s.add("vector", lambda e: e.tensor_mul(
                            stt_row[0:1, 0:1], stt_row[0:1, 2:3], stt_row[0:1, 3:4]),
                            deps=[m0_])
                        m2_ = s.add("vector", lambda e: e.tensor_scalar_mul(
                            stt_row[0:1, 0:1], stt_row[0:1, 0:1], -4.0), deps=[m1_])
                        b1_ = s.add("tensor", lambda e: e.matmul(
                            psst[:, 130:131], ones_r1f[:, :], stt_row[0:1, 0:1],
                            start=True, stop=True), deps=[m2_, o_m2])
                        b2_ = s.add("vector", lambda e: e.tensor_copy(
                            cbias[:, 0:1], psst[:, 130:131]), deps=[b1_])
                    s.emit()

            attn_oT = ec5(nc.sbuf_tensor("attn_oT", [128, 4 * TFULL], BF16))

            if BUILD_UPTO == 2:
                _tap_out(nc, sems, XR, OUT, qk_ops + v_ops + [b2_])
                return nc

            with contextlib.ExitStack() as _s6x:
                ec6 = _s6x.enter_context
                recipb = ec6(nc.sbuf_tensor("recipb", [128, 2 * 512], F32))
                rrow = ec6(nc.sbuf_tensor("rrow", [1, 2048], F32))
                ptb = ec6(nc.sbuf_tensor("ptb", [128, 2 * 2048], BF16))
                praw = ec6(nc.sbuf_tensor("praw", [128, 2 * 512], F32))
                pssc = ec6(nc.psum_tensor("pssc", [128, 2048], F32))
                psPV = ec6(nc.psum_tensor("psPV", [128, 2 * 512], F32))
                psDB = ec6(nc.psum_tensor("psDB", [128, 2 * 512], F32))
                _dummy = 1
                s = Section(nc, sems)
                attn_ops = []
                hist = {}
                NQ = 16

                def emit_ep_early(p, qt):
                    den_last = hist["last_den"]
                    pv_last = hist["last_pv"]
                    cr = s.add("vector", lambda e: e.tensor_copy(
                        praw[:, :], psPV[:, :]), deps=[pv_last, hist.get("raw_used")])
                    hist["raw_copy"] = cr
                    cps = []
                    for half, r0, r1_ in ((0, 0, 64), (1, 32, 96)):
                        ca = s.add("vector", lambda e, r0=r0, half=half: e.tensor_copy(
                            rrow[0:1, half * 1024:half * 1024 + 512],
                            psDB[r0:r0 + 1, 0:512]), deps=[den_last, hist.get("rr_used")])
                        cb = s.add("vector", lambda e, r1_=r1_, half=half: e.tensor_add(
                            rrow[0:1, half * 1024:half * 1024 + 512],
                            rrow[0:1, half * 1024:half * 1024 + 512],
                            psDB[r1_:r1_ + 1, 0:512]), deps=[ca])
                        cps.append(cb)
                    hist["last_rr"] = cps[-1]
                    hist["den_cp"] = cps

                def emit_ep_late_half(p, qt, half):
                    jj = 2 * p + half
                    rr1 = s.add("scalar", lambda e, half=half: e.activation(
                        rrow[0:1, half * 1024:half * 1024 + 512],
                        rrow[0:1, half * 1024:half * 1024 + 512], AF.Ln),
                        deps=[hist["den_cp"][half]])
                    rr2 = s.add("scalar", lambda e, half=half: e.activation(
                        rrow[0:1, half * 1024:half * 1024 + 512],
                        rrow[0:1, half * 1024:half * 1024 + 512],
                        AF.Exp, scale=-1.0), deps=[rr1])
                    d1 = s.add("sync", lambda e, half=half: e.dma_start(
                        out=rr_dram[half].ap(),
                        in_=rrow[0:1, half * 1024:half * 1024 + 512]),
                        deps=[rr2], dma=True, sem="rb")
                    d2 = s.add("sync", lambda e, half=half: e.dma_start(
                        out=recipb[:, half * 512:half * 512 + 512],
                        in_=rr_dram[half].ap().broadcast_to([128, 512])),
                        deps=[d1, hist.get(("rb_used", half))], dma=True, sem="rb")
                    ev = s.add("vector", lambda e, jj=jj, qt=qt, half=half: e.tensor_mul(
                        attn_oT[:, jj * TFULL + qt * 512:jj * TFULL + (qt + 1) * 512],
                        praw[:, half * 512:half * 512 + 512],
                        recipb[:, half * 512:half * 512 + 512]),
                        deps=[d2, hist["raw_copy"]])
                    attn_ops.append(ev)
                    hist[("rb_used", half)] = ev
                    if half == 1:
                        hist["raw_used"] = ev
                        hist["rr_used"] = d1

                def emit_ep_late(p, qt):
                    emit_ep_late_half(p, qt, 0)
                    emit_ep_late_half(p, qt, 1)

                pending = None
                NQ2 = 32
                units = [(p, qt) for p in range(2) for qt in range(8)]
                for p, qt in units:
                    j0, j1 = 2 * p, 2 * p + 1
                    exp_ops = [None] * NQ2
                    qk_mms = [None] * NQ2
                    pv_mms = [None] * NQ2
                    den_mms = {}

                    def em_qk(e, q, j0=j0, j1=j1, qt=qt):
                        for i, jj in enumerate((j0, j1)):
                            ins = e.matmul(
                                pssc[:, (q % 2) * 1024 + i * 512:
                                     (q % 2) * 1024 + (i + 1) * 512],
                                KT(jj)[:, q * 128:(q + 1) * 128],
                                QT(jj)[:, qt * 512:(qt + 1) * 512],
                                start=True, stop=True)
                        return ins

                    def em_pv(e, q, j0=j0, j1=j1):
                        for i, jj in enumerate((j0, j1)):
                            ins = e.matmul(
                                psPV[:, i * 512:(i + 1) * 512],
                                Vt[:, q * 512 + jj * 128:q * 512 + jj * 128 + 128],
                                ptb[:, (q % 4) * 1024 + i * 512:
                                    (q % 4) * 1024 + (i + 1) * 512],
                                start=(q == 0), stop=(q == NQ2 - 1))
                        return ins

                    def em_den(e, qa, qb):
                        first = (qa == 0)
                        last = (qb == NQ2 - 1)
                        for row, kcx, i in ((0, qa, 0), (32, qa, 1),
                                            (64, qb, 0), (96, qb, 1)):
                            ins = e.matmul(
                                psDB[row:row + 1, 0:512], ones_c1[:, :],
                                ptb[:, (kcx % 4) * 1024 + i * 512:
                                    (kcx % 4) * 1024 + (i + 1) * 512],
                                start=first, stop=last, tile_position=(0, row))
                        return ins

                    for q_ in range(NQ2):
                        if q_ >= 2:
                            d = [exp_ops[q_ - 2]]
                        elif hist.get("pexp") is not None:
                            d = [hist["pexp"][q_]]
                        else:
                            d = qk_ops + v_ops + [b2_]
                        qk_mms[q_] = s.add("tensor",
                                           lambda e, q=q_, f=em_qk: f(e, q), deps=d)

                        ex_d = [qk_mms[q_], b2_]
                        if q_ >= 4:
                            ex_d += [pv_mms[q_ - 4], den_mms[(q_ - 4) // 2]]
                        elif hist.get("ppv") is not None:
                            ex_d += [hist["ppv"][q_], hist["pdb"][q_ // 2]]
                        exp_ops[q_] = s.add("scalar", lambda e, q=q_: e.activation(
                            ptb[:, (q % 4) * 1024:(q % 4) * 1024 + 1024],
                            pssc[:, (q % 2) * 1024:(q % 2) * 1024 + 1024],
                            AF.Exp, scale=1.0 / 32.0, bias=cbias[:, 0:1]), deps=ex_d)

                        if q_ >= 1:
                            d = [exp_ops[q_ - 1]]
                            if q_ - 1 == 0 and hist.get("raw_copy") is not None:
                                d.append(hist["raw_copy"])
                            pv_mms[q_ - 1] = s.add("tensor",
                                lambda e, q=q_ - 1, f=em_pv: f(e, q), deps=d)
                            if (q_ - 1) % 2 == 1:
                                bi = (q_ - 1) // 2
                                dd_ = [exp_ops[q_ - 1], exp_ops[q_ - 2], o_m3]
                                if bi == 0 and hist.get("last_rr") is not None:
                                    dd_.append(hist["last_rr"])
                                if bi == 0 and hist.get("rr_used") is not None:
                                    dd_.append(hist.get("rr_used"))
                                den_mms[bi] = s.add("tensor",
                                    lambda e, qa=q_ - 2, qb=q_ - 1, f=em_den:
                                    f(e, qa, qb), deps=dd_)
                        if q_ == 8 and pending is not None:
                            emit_ep_late_half(pending[0], pending[1], 0)
                        if q_ == 20 and pending is not None:
                            done_u = pending
                            emit_ep_late_half(pending[0], pending[1], 1)
                            pending = None
                            if done_u == (0, 7):
                                snd3a = s.add("sync", lambda e: e.dma_start(
                                    out=cc3_snd_a.ap(),
                                    in_=attn_oT[:, 0:2 * TFULL]),
                                    deps=list(attn_ops), dma=True, sem="io")
                                cc3a = s.add("gpsimd", lambda e: e.collective_compute(
                                    "AllGather", ALU.bypass, replica_groups=RG,
                                    ins=[cc3_snd_a.ap().opt()],
                                    outs=[cc3_rcv_a.ap().opt()]),
                                    deps=[snd3a, cc2], coll=True, sem="cc")
                            elif done_u == (1, 3):
                                snd3b1 = s.add("sync", lambda e: e.dma_start(
                                    out=cc3_snd_b1.ap().rearrange(
                                        "p (j t) -> p j t", j=2),
                                    in_=attn_oT.ap().rearrange(
                                        "p (j t) -> p j t", j=4)[:, 2:4, 0:T]),
                                    deps=attn_ops[16:24], dma=True, sem="io")
                                cc3b1 = s.add("gpsimd", lambda e: e.collective_compute(
                                    "AllGather", ALU.bypass, replica_groups=RG,
                                    ins=[cc3_snd_b1.ap().opt()],
                                    outs=[cc3_rcv_b1.ap().opt()]),
                                    deps=[snd3b1, cc3a], coll=True, sem="cc")
                    pv_mms[NQ2 - 1] = s.add("tensor",
                        lambda e, q=NQ2 - 1, f=em_pv: f(e, q),
                        deps=[exp_ops[NQ2 - 1]])
                    den_mms[(NQ2 - 1) // 2] = s.add("tensor",
                        lambda e, qa=NQ2 - 2, qb=NQ2 - 1, f=em_den: f(e, qa, qb),
                        deps=[exp_ops[NQ2 - 1], exp_ops[NQ2 - 2], o_m3])
                    hist["pexp"] = (exp_ops[NQ2 - 2], exp_ops[NQ2 - 1])
                    hist["ppv"] = [pv_mms[NQ2 - 4 + i] for i in range(4)]
                    hist["pdb"] = [den_mms[(NQ2 - 4) // 2], den_mms[(NQ2 - 2) // 2]]
                    hist["last_pv"] = pv_mms[NQ2 - 1]
                    hist["last_den"] = den_mms[(NQ2 - 1) // 2]
                    emit_ep_early(p, qt)
                    pending = (p, qt)
                emit_ep_late(*pending)
                s.emit()

            if BUILD_UPTO == 3:
                _tap_out(nc, sems, attn_oT.ap().bitcast(F32), OUT, attn_ops, ntiles=8)
                return nc

            s = Section(nc, sems)
            snd3b2 = s.add("sync", lambda e: e.dma_start(
                out=cc3_snd_b2.ap().rearrange("p (j t) -> p j t", j=2),
                in_=attn_oT.ap().rearrange("p (j t) -> p j t", j=4)[:, 2:4, T:2 * T]),
                deps=attn_ops, dma=True, sem="io")
            cc3b2 = s.add("gpsimd", lambda e: e.collective_compute(
                "AllGather", ALU.bypass, replica_groups=RG,
                ins=[cc3_snd_b2.ap().opt()], outs=[cc3_rcv_b2.ap().opt()]),
                deps=[snd3b2, cc3b1], coll=True, sem="cc")
            s.emit()

        with contextlib.ExitStack() as _s7x:
                ec7 = _s7x.enter_context
                attn_all = ec7(nc.sbuf_tensor("attn_all", [128, NCC * T], BF16))
                mbufL = ec7(nc.sbuf_tensor("mbufL", [128, 2 * T], BF16))
                mbufR = ec7(nc.sbuf_tensor("mbufR", [128, 2 * T], BF16))
                ow_sb = ec7(nc.sbuf_tensor("ow_sb", [128, 2 * NCC * C], BF16))
                ob_sb = ec7(nc.sbuf_tensor("ob_sb", [128, 16], F32))
                o_ev2 = ec7(nc.sbuf_tensor("o_ev2", [128, 8 * 512], F32))
                wsb = ec7(nc.sbuf_tensor("wsb", [128, C], F32))
                fin = ec7(nc.sbuf_tensor("fin", [128, 2 * C], F32))
                psO2 = ec7(nc.psum_tensor("psO2", [128, 2 * 512], F32))
                psTk = ec7(nc.psum_tensor("psTk", [128, 2 * C], F32))
                _dummy = 1
                s = Section(nc, sems)
                ow_d0 = s.add("gpsimd", lambda e: e.dma_start(
                    out=ow_sb[:, 0:NCC * C].rearrange("p (ic m) -> p ic m", ic=NCC),
                    in_=owT0.rearrange("(ic p) m -> p ic m", p=128)),
                    deps=[snd3b2], dma=True, sem="w")
                ow_d1 = s.add("gpsimd", lambda e: e.dma_start(
                    out=ow_sb[:, NCC * C:2 * NCC * C].rearrange("p (ic m) -> p ic m", ic=NCC),
                    in_=owT1.rearrange("(ic p) m -> p ic m", p=128)),
                    deps=[snd3b2], dma=True, sem="w")
                ob_d = s.add("gpsimd", lambda e: e.dma_start(
                    out=ob_sb[:, :], in_=ob_cols), deps=[snd3b2], dma=True, sem="w")
                rel = s.add("sync", lambda e: e.dma_start(
                    out=XR.ap().rearrange("p (n c) -> p n c", c=C),
                    in_=r1_spill.rearrange("(n p) c -> p n c", p=128)),
                    deps=attn_ops, dma=True, sem="spl")
                mrg = []
                mrg_by = {}
                for idx, hh in enumerate((0, 1, 4, 5, 2, 3, 6, 7)):
                    r, j = hh // 4, hh % 4
                    pp = idx % 2
                    ms = "io" if pp == 0 else "io2"
                    if j < 2:
                        src_l = cc3_rcv_a.ap()[r][:, j * TFULL:j * TFULL + T]
                        src_r = cc3_rcv_a.ap()[r][:, j * TFULL + T:j * TFULL + 2 * T]
                        dep_l, dep_r = cc3a, cc3a
                    else:
                        src_l = cc3_rcv_b1.ap()[r][:, (j - 2) * T:(j - 1) * T]
                        src_r = cc3_rcv_b2.ap()[r][:, (j - 2) * T:(j - 1) * T]
                        dep_l, dep_r = cc3b1, cc3b2
                    prevm = mrg[idx - 2] if idx >= 2 else None
                    dl = s.add("sync", lambda e, src_l=src_l, pp=pp: e.dma_start(
                        out=mbufL[:, pp * T:(pp + 1) * T], in_=src_l),
                        deps=[dep_l, snd3b2, prevm], dma=True, sem=ms)
                    dr = s.add("sync", lambda e, src_r=src_r, pp=pp: e.dma_start(
                        out=mbufR[:, pp * T:(pp + 1) * T], in_=src_r),
                        deps=[dep_r, snd3b2, prevm], dma=True, sem=ms)
                    m1 = s.add("vector", lambda e, hh=hh, pp=pp: e.tensor_sub(
                        attn_all[:, hh * T:(hh + 1) * T],
                        mbufR[:, pp * T:(pp + 1) * T], mbufL[:, pp * T:(pp + 1) * T]),
                        deps=[dl, dr])
                    m2 = s.add("vector", lambda e, hh=hh, pp=pp: e.scalar_tensor_tensor(
                        attn_all[:, hh * T:(hh + 1) * T],
                        attn_all[:, hh * T:(hh + 1) * T],
                        vp[:, VP_HM:VP_HM + 1], mbufL[:, pp * T:(pp + 1) * T],
                        op0=ALU.mult, op1=ALU.add), deps=[m1, o_vp])
                    mrg.append(m2)
                    mrg_by[hh] = m2
                prev_ev = {}
                prev_f = {}
                last_round_trbs = []
                out_ops = []
                ko = 0
                for eo in range(2):
                    for qt2 in range(2):
                        ev_blocks = []
                        round_trbs = []
                        for oc in range(NCC):
                            sl = ko % 2

                            def em_omm(ics, first, last, e=None, eo=eo, qt2=qt2,
                                       oc=oc, sl=sl):
                                for k_, ic in enumerate(ics):
                                    ins = e.matmul(
                                        psO2[:, sl * 512:sl * 512 + 512],
                                        ow_sb[:, eo * NCC * C + ic * C + oc * 128:
                                              eo * NCC * C + ic * C + oc * 128 + 128],
                                        attn_all[:, ic * T + eo * C + qt2 * 512:
                                                 ic * T + eo * C + (qt2 + 1) * 512],
                                        start=(first and k_ == 0),
                                        stop=(last and k_ == len(ics) - 1))
                                return ins
                            ow_d = ow_d0 if eo == 0 else ow_d1
                            d = [mrg_by[0], mrg_by[1], mrg_by[4], mrg_by[5], ow_d]
                            if ko >= 2:
                                d.append(prev_ev[ko - 2])
                            mma = s.add("tensor", lambda e, f=em_omm: f(
                                (0, 1, 4, 5), True, False, e), deps=d)
                            mmb = s.add("tensor", lambda e, f=em_omm: f(
                                (2, 3, 6, 7), False, True, e),
                                deps=[mma, mrg_by[2], mrg_by[3], mrg_by[6], mrg_by[7]])
                            ev = s.add("vector", lambda e, oc=oc, eo=eo, sl=sl: e.tensor_scalar(
                                o_ev2[:, oc * 512:(oc + 1) * 512],
                                psO2[:, sl * 512:sl * 512 + 512],
                                ob_sb[:, eo * 8 + oc:eo * 8 + oc + 1], None, op0=ALU.add),
                                deps=[mmb, ob_d] + last_round_trbs)
                            prev_ev[ko] = ev
                            ev_blocks.append(ev)
                            ko += 1
                        for blk_ in range(4):
                            ti = eo * 8 + qt2 * 4 + blk_
                            slt = ti % 2

                            def em_trb(e, blk_=blk_, slt=slt):
                                for oc in range(NCC):
                                    ins = e.transpose(
                                        psTk[:, slt * C + oc * 128:slt * C + oc * 128 + 128],
                                        o_ev2[:, oc * 512 + blk_ * 128:oc * 512 + blk_ * 128 + 128],
                                        ident_f[:, :])
                                return ins
                            d = ev_blocks + [o_idf]
                            if ti >= 2:
                                d.append(prev_f[ti - 2])
                            trb = s.add("tensor", em_trb, deps=d)
                            round_trbs.append(trb)
                            wop = s.add("scalar", lambda e, ti=ti: e.activation(
                                wsb[:, :], XR[:, ti * C:(ti + 1) * C], AF.Identity,
                                bias=stat[:, ST_NMR + ti:ST_NMR + ti + 1],
                                scale=stat[:, ST_W1 + ti:ST_W1 + ti + 1]),
                                deps=[rel, oW, oN2, prev_f.get(ti - 1)])
                            fo = s.add("vector", lambda e, ti=ti, slt=slt: e.tensor_add(
                                fin[:, (ti % 2) * C:(ti % 2) * C + C],
                                wsb[:, :], psTk[:, slt * C:slt * C + C]),
                                deps=[wop, trb] +
                                ([out_ops[ti - 2]] if ti >= 2 else []))
                            prev_f[ti] = fo
                            oo = s.add("sync", lambda e, ti=ti: e.dma_start(
                                out=OUT[ti * 128:(ti + 1) * 128, :],
                                in_=fin[:, (ti % 2) * C:(ti % 2) * C + C]),
                                deps=[fo], dma=True, sem="io")
                            out_ops.append(oo)
                        last_round_trbs = round_trbs
                s.add("sync", lambda e: e.nop(), deps=out_ops)
                s.emit()
    return nc


def _tap_out(nc, sems, XR, OUT, deps, ntiles=NT):
    s = Section(nc, sems)
    outs = []
    for i in range(ntiles):
        outs.append(s.add("sync", lambda e, i=i: e.dma_start(
            out=OUT[i * 128:(i + 1) * 128, :], in_=XR[:, i * C:(i + 1) * C]),
            deps=deps, dma=True, sem="io"))
    s.add("sync", lambda e: e.nop(), deps=outs)
    s.emit()


_NC_CACHE = {}


def kernel(L0_attn_x, L0_attn_time_decay, L0_attn_time_first, L0_attn_time_mix,
           L0_attn_Wk, L0_attn_Wv, L0_attn_Wr, L0_attn_Wo,
           L0_ffn_router_prob, L0_ffn_norm_w, L0_ffn_norm_b,
           L0_ffn_q_w, L0_ffn_q_b, L0_ffn_k_w, L0_ffn_k_b,
           L0_ffn_v_w, L0_ffn_v_b, L0_ffn_o_w, L0_ffn_o_b):
    x = np.ascontiguousarray(np.asarray(L0_attn_x, dtype=np.float32))
    B, Tf, D = x.shape
    f32 = np.float32

    def packcols(v):
        return np.ascontiguousarray(np.asarray(v, f32).reshape(NCC, 128).T)

    wkT = np.ascontiguousarray(np.asarray(L0_attn_Wk, f32).T)
    wvT = np.ascontiguousarray(np.asarray(L0_attn_Wv, f32).T)
    wrT = np.ascontiguousarray(np.asarray(L0_attn_Wr, f32).T)
    woT = np.ascontiguousarray(np.asarray(L0_attn_Wo, f32).T)
    qwTf = np.asarray(L0_ffn_q_w, f32).T
    kwTf = np.asarray(L0_ffn_k_w, f32).T
    vwTf = np.asarray(L0_ffn_v_w, f32).T
    owTf = np.asarray(L0_ffn_o_w, f32).T

    key = (BUILD_UPTO, len(RG_GROUPS))
    if key not in _NC_CACHE:
        _NC_CACHE[key] = build_nc()
    nc = _NC_CACHE[key]

    in_maps = []
    for c in range(8):
        b, h = c // 2, c % 2
        xe = np.zeros((T + 1, C), f32)
        if h == 1:
            xe[0] = x[b, T - 1]
        xe[1:] = x[b, h * T:(h + 1) * T]
        ow0 = np.zeros((C, C), f32)
        ow1 = np.zeros((C, C), f32)
        m0 = C >> (2 * h)
        m1 = C >> (2 * h + 1)
        ow0[:m0, :m0] = owTf[:m0, :m0]
        ow1[:m1, :m1] = owTf[:m1, :m1]
        obc = np.zeros((2, C), f32)
        obc[0, :m0] = np.asarray(L0_ffn_o_b, f32)[:m0]
        obc[1, :m1] = np.asarray(L0_ffn_o_b, f32)[:m1]
        ob_cols = np.zeros((128, 16), f32)
        ob_cols[:, 0:8] = obc[0].reshape(NCC, 128).T
        ob_cols[:, 8:16] = obc[1].reshape(NCC, 128).T
        veca = np.zeros((128, 49), f32)
        veca[:, VP_TD:VP_TD + 8] = packcols(L0_attn_time_decay)
        veca[:, VP_TF:VP_TF + 8] = packcols(L0_attn_time_first)
        veca[:, VP_TM:VP_TM + 8] = packcols(L0_attn_time_mix)
        veca[:, VP_NW:VP_NW + 8] = packcols(L0_ffn_norm_w)
        veca[:, VP_NB:VP_NB + 8] = packcols(L0_ffn_norm_b)
        qb = np.asarray(L0_ffn_q_b, f32)[h * 512:(h + 1) * 512]
        kb = np.asarray(L0_ffn_k_b, f32)[h * 512:(h + 1) * 512]
        veca[:, VP_QB:VP_QB + 4] = qb.reshape(4, 128).T
        veca[:, VP_KB:VP_KB + 4] = kb.reshape(4, 128).T
        veca[:, VP_HM] = float(h)
        in_maps.append(dict(
            x_ext=xe,
            wkT=wkT, wvT=wvT, wrT=wrT, woT=woT,
            qwT=np.ascontiguousarray(qwTf[:, h * 512:(h + 1) * 512]),
            kwT=np.ascontiguousarray(kwTf[:, h * 512:(h + 1) * 512]),
            vwT=np.ascontiguousarray(vwTf[:, h * 512:(h + 1) * 512]),
            owT0=ow0, owT1=ow1,
            veca=veca,
            vb_row=np.ascontiguousarray(
                np.asarray(L0_ffn_v_b, f32)[h * 512:(h + 1) * 512].reshape(1, 512)),
            ob_cols=ob_cols,
        ))

    res = run_bass_kernel_spmd(nc, in_maps, core_ids=list(range(8)))
    out = np.empty((B, Tf, D), np.float32)
    for c in range(8):
        b, h = c // 2, c % 2
        out[b, h * T:(h + 1) * T] = res.results[c]["OUT"]
    return out


# version 45
# speedup vs baseline: 1.0386x; 1.0076x over previous
import contextlib
import numpy as np
import concourse.bass as bass
import concourse.mybir as mybir
from concourse.bass_utils import run_bass_kernel_spmd

F32 = mybir.dt.float32
BF16 = mybir.dt.bfloat16
F16 = mybir.dt.float16
AF = mybir.ActivationFunctionType
ALU = mybir.AluOpType
AX = mybir.AxisListType

T = 2048
TFULL = 4096
NT = 16
C = 1024
NCC = 8
TEC = 2052
EPS = 1e-5
K_CLAMP = 60.0
K_EPS = 1e-16

BUILD_UPTO = 99
S2_LEVEL = 4
RG_GROUPS = [[0, 1], [2, 3], [4, 5], [6, 7]]

ST_TM1, ST_E, ST_LAM, ST_FW, ST_NEGE, ST_N2047E = 0, 8, 16, 24, 32, 40
ST_MEAN1, ST_VAR1, ST_RSTD1, ST_NMR, ST_SY, ST_BY, ST_W1 = 48, 64, 80, 96, 112, 128, 144
ST_COLS = 160
VP_TD, VP_TF, VP_TM, VP_NW, VP_NB, VP_QB, VP_KB, VP_HM = 0, 8, 16, 24, 32, 40, 44, 48


class Op:
    __slots__ = ("eng", "sem", "emit", "deps", "inc_k", "val", "needed", "sec",
                 "async_cls", "dwaits")

    def __init__(self, eng, sem, emit, deps, inc_k, sec, async_cls):
        self.eng = eng
        self.sem = sem
        self.emit = emit
        self.deps = deps
        self.dwaits = {}
        self.inc_k = inc_k
        self.val = None
        self.needed = False
        self.sec = sec
        self.async_cls = async_cls


class Section:

    ENGS = ("tensor", "vector", "scalar", "gpsimd", "sync")
    COUNTER = [0]

    def __init__(self, nc, sems):
        self.nc = nc
        self.sems = sems
        self.ops = {e: [] for e in self.ENGS}
        Section.COUNTER[0] += 1
        self.sec_id = Section.COUNTER[0]

    def add(self, eng, emit, deps=(), dma=False, coll=False, sem=None):
        async_cls = dma or coll
        if async_cls:
            skey = sem or "io"
            inc_k = 16 if dma else 1
        else:
            skey = eng
            inc_k = 1
        op = Op(eng, skey, emit, [], inc_k, self.sec_id, async_cls)
        for d in deps:
            if d is None:
                continue
            if d.async_cls:
                op.dwaits[d.sem] = self.sems[d.sem].n
            elif d.sec == self.sec_id:
                d.needed = True
                op.deps.append(d)
        if async_cls:
            if self.sems[skey].n > 0:
                op.dwaits[skey] = max(op.dwaits.get(skey, 0), self.sems[skey].n)
            self.sems[skey].n += inc_k
            op.val = self.sems[skey].n
        self.ops[eng].append(op)
        return op

    def emit(self):
        for e in self.ENGS:
            for op in self.ops[e]:
                if not op.async_cls and op.needed:
                    self.sems[e].n += 1
                    op.val = self.sems[e].n
        with self.nc.Block() as blk:
            for ename in self.ENGS:
                if not self.ops[ename]:
                    continue

                def make_body(ename):
                    def body(eng):
                        for op in self.ops[ename]:
                            waits = dict(op.dwaits)
                            for d in op.deps:
                                if d.val is None:
                                    raise RuntimeError("dep without inc")
                                waits[d.sem] = max(waits.get(d.sem, 0), d.val)
                            for de, dv in waits.items():
                                if dv > 0:
                                    eng.wait_ge(self.sems[de].h, dv)
                            ins = op.emit(eng)
                            if op.val is not None:
                                ins.then_inc(self.sems[op.sem].h, op.inc_k)
                    return body

                getattr(blk, ename)(make_body(ename))


class Sem:
    __slots__ = ("h", "n")

    def __init__(self, h):
        self.h = h
        self.n = 0


def build_nc():
    nc = bass.Bass("TRN2", target_bir_lowering=False)

    x_ext = nc.dram_tensor("x_ext", [T + 1, C], F32, kind="ExternalInput").ap()
    wkT = nc.dram_tensor("wkT", [C, C], F32, kind="ExternalInput").ap()
    wvT = nc.dram_tensor("wvT", [C, C], F32, kind="ExternalInput").ap()
    wrT = nc.dram_tensor("wrT", [C, C], F32, kind="ExternalInput").ap()
    woT = nc.dram_tensor("woT", [C, C], F32, kind="ExternalInput").ap()
    qwT = nc.dram_tensor("qwT", [C, 512], F32, kind="ExternalInput").ap()
    kwT = nc.dram_tensor("kwT", [C, 512], F32, kind="ExternalInput").ap()
    vwT = nc.dram_tensor("vwT", [C, 512], F32, kind="ExternalInput").ap()
    owT0 = nc.dram_tensor("owT0", [C, C], F32, kind="ExternalInput").ap()
    owT1 = nc.dram_tensor("owT1", [C, C], F32, kind="ExternalInput").ap()
    veca = nc.dram_tensor("veca", [128, 49], F32, kind="ExternalInput").ap()
    vb_row = nc.dram_tensor("vb_row", [1, 512], F32, kind="ExternalInput").ap()
    ob_cols = nc.dram_tensor("ob_cols", [128, 16], F32, kind="ExternalInput").ap()
    OUT = nc.dram_tensor("OUT", [T, C], F32, kind="ExternalOutput").ap()

    r1_spill = nc.dram_tensor("r1_spill", [T, C], F32).ap()
    bnd_dram = nc.dram_tensor("bnd_dram", [C], BF16).ap()
    cc1_snd = [nc.dram_tensor(f"cc1s{i}", [128, 2], F32) for i in range(NCC)]
    cc1_rcv = [nc.dram_tensor(f"cc1r{i}", [2, 128, 2], F32) for i in range(NCC)]
    cc2_snd = [nc.dram_tensor(f"cc2s{a}", [128, NCC * (T // 2)], BF16)
               for a in range(2)]
    cc2_rcv = [nc.dram_tensor(f"cc2r{a}", [2, 128, NCC * (T // 2)], BF16)
               for a in range(2)]
    cc3_snd_a = nc.dram_tensor("cc3sa", [128, 2 * TFULL], BF16)
    cc3_rcv_a = nc.dram_tensor("cc3ra", [2, 128, 2 * TFULL], BF16)
    cc3_snd_b1 = nc.dram_tensor("cc3sb1", [128, 2 * T], BF16)
    cc3_rcv_b1 = nc.dram_tensor("cc3rb1", [2, 128, 2 * T], BF16)
    cc3_snd_b2 = nc.dram_tensor("cc3sb2", [128, 2 * T], BF16)
    cc3_rcv_b2 = nc.dram_tensor("cc3rb2", [2, 128, 2 * T], BF16)
    rr_dram = [nc.dram_tensor(f"rrd{h}", [1, 512], F32) for h in range(2)]
    RG = RG_GROUPS

    with contextlib.ExitStack() as _top:
        ec = _top.enter_context
        XR = ec(nc.sbuf_tensor("XR", [128, NT * C], F32))
        vp = ec(nc.sbuf_tensor("vp", [128, 52], F32))
        stat = ec(nc.sbuf_tensor("stat", [128, ST_COLS], F32))
        ident_b = ec(nc.sbuf_tensor("ident_b", [128, 128], BF16))
        ident_f = ec(nc.sbuf_tensor("ident_f", [128, 128], F32))
        ones_r1 = ec(nc.sbuf_tensor("ones_r1", [1, 128], BF16))
        ones_r1f = ec(nc.sbuf_tensor("ones_r1f", [1, 128], F32))
        ones_c1 = ec(nc.sbuf_tensor("ones_c1", [128, 1], BF16))
        ones128 = ec(nc.sbuf_tensor("ones128", [128, 128], BF16))
        carry = ec(nc.sbuf_tensor("carry", [128, 24], F32))
        st6 = ec(nc.sbuf_tensor("st6", [128, 12], F32))
        mv = ec(nc.sbuf_tensor("mv", [128, 4], F32))
        cbias = ec(nc.sbuf_tensor("cbias", [128, 2], F32))
        cst = ec(nc.sbuf_tensor("cst", [128, 1], F32))
        nc.const_aps.register(F32, EPS, cst[:, 0:1])
        sems = {n: Sem(ec(nc.semaphore("q_" + n)))
                for n in ("tensor", "vector", "scalar", "gpsimd", "sync",
                          "ld", "ld2", "w", "io", "io2", "cc", "cc1d", "spl", "rb",
                          "x0", "x1", "x2", "x3", "x4", "x5", "x6", "x7")}

        XRb = XR.ap().bitcast(BF16)

        def QT(j):
            return XRb[:, j * TFULL:(j + 1) * TFULL]

        def KT(j):
            return XRb[:, 16384 + j * TFULL:16384 + (j + 1) * TFULL]

        s = Section(nc, sems)
        o_m1 = s.add("gpsimd", lambda e: e.memset(ones_r1[:, :], 1.0))
        o_m2 = s.add("gpsimd", lambda e: e.memset(ones_r1f[:, :], 1.0))
        o_m3 = s.add("gpsimd", lambda e: e.memset(ones_c1[:, :], 1.0))
        o_m4 = s.add("gpsimd", lambda e: e.memset(ones128[:, :], 1.0))
        o_m5 = s.add("gpsimd", lambda e: e.memset(carry[:, :], 0.0))
        o_m6 = s.add("gpsimd", lambda e: e.memset(cst[:, :], EPS))
        o_vp = s.add("sync", lambda e: e.dma_start(out=vp[:, 0:49], in_=veca), dma=True, sem="ld")
        xload = []
        for i in range(NT):
            xload.append(s.add("sync", lambda e, i=i: e.dma_start(
                out=XR[:, i * C:(i + 1) * C],
                in_=x_ext[1 + i * 128:1 + (i + 1) * 128, :]), dma=True,
                sem="ld" if i % 2 == 0 else "ld2"))
        o_id = s.add("gpsimd", lambda e: e.affine_select(
            ident_b[:, :], ones128[:, :], [[1, 128]],
            compare_op=ALU.is_equal, fill=0.0, base=0, channel_multiplier=-1),
            deps=[o_m4])
        o_idf = s.add("vector", lambda e: e.tensor_copy(ident_f[:, :], ident_b[:, :]),
                      deps=[o_id])
        o_tm1 = s.add("vector", lambda e: e.tensor_scalar(
            stat[:, ST_TM1:ST_TM1 + 8], vp[:, VP_TM:VP_TM + 8], -1.0, 1.0,
            op0=ALU.mult, op1=ALU.add), deps=[o_vp])
        o_e = s.add("scalar", lambda e: e.activation(
            stat[:, ST_E:ST_E + 8], vp[:, VP_TD:VP_TD + 8], AF.Exp), deps=[o_vp])
        o_lam = s.add("scalar", lambda e: e.activation(
            stat[:, ST_LAM:ST_LAM + 8], stat[:, ST_E:ST_E + 8], AF.Exp, scale=-1.0),
            deps=[o_e])
        o_fw = s.add("scalar", lambda e: e.activation(
            stat[:, ST_FW:ST_FW + 8], vp[:, VP_TF:VP_TF + 8], AF.Exp), deps=[o_vp])
        o_nege = s.add("vector", lambda e: e.tensor_scalar_mul(
            stat[:, ST_NEGE:ST_NEGE + 8], stat[:, ST_E:ST_E + 8], -1.0), deps=[o_e])
        o_n2047 = s.add("vector", lambda e: e.tensor_scalar_mul(
            stat[:, ST_N2047E:ST_N2047E + 8], stat[:, ST_E:ST_E + 8], -2047.0),
            deps=[o_e])

        last_coll = [None]
        with contextlib.ExitStack() as _sx0:
            xmT = _sx0.enter_context(nc.sbuf_tensor("xmT", [128, NCC * T], BF16))
            rwkvT = _sx0.enter_context(nc.sbuf_tensor("rwkvT", [128, NCC * T], BF16))
            with contextlib.ExitStack() as _s1x:
                ec1 = _s1x.enter_context
                xnT = ec1(nc.sbuf_tensor("xnT", [128, NCC * TEC], BF16))
                xn_tile = ec1(nc.sbuf_tensor("xn_tile", [128, 2 * C], BF16))
                xb_row = ec1(nc.sbuf_tensor("xb_row", [1, C], F32))
                xb_st = ec1(nc.sbuf_tensor("xb_st", [1, 12], F32))
                xb_mv = ec1(nc.sbuf_tensor("xb_mv", [1, 6], F32))
                xb_bf = ec1(nc.sbuf_tensor("xb_bf", [1, C], BF16))
                ptr = ec1(nc.psum_tensor("ptr", [128, 2 * C], BF16))
                _dummy = 1
                stats_done = []
                prev = None
                for i in range(NT):
                    d = [xload[i]] + ([prev] if prev else [])
                    a = s.add("vector", lambda e, i=i: e.bn_stats(
                        st6[:, 0:6], XR[:, i * C:i * C + 512]), deps=d)
                    b = s.add("vector", lambda e, i=i: e.bn_stats(
                        st6[:, 6:12], XR[:, i * C + 512:i * C + 1024]), deps=[a])
                    g = s.add("vector", lambda e: e.bn_aggr(mv[:, 0:2], st6[:, :]),
                              deps=[a, b])
                    cm = s.add("vector", lambda e, i=i: e.tensor_copy(
                        stat[:, ST_MEAN1 + i:ST_MEAN1 + i + 1], mv[:, 0:1]), deps=[g])
                    cv = s.add("vector", lambda e, i=i: e.tensor_copy(
                        stat[:, ST_VAR1 + i:ST_VAR1 + i + 1], mv[:, 1:2]), deps=[g])
                    prev = cv
                    stats_done.extend([cm, cv])
                o_ln = s.add("scalar", lambda e: e.activation(
                    stat[:, ST_RSTD1:ST_RSTD1 + 16], stat[:, ST_VAR1:ST_VAR1 + 16],
                    AF.Ln, bias=EPS), deps=stats_done)
                o_rstd = s.add("scalar", lambda e: e.activation(
                    stat[:, ST_RSTD1:ST_RSTD1 + 16], stat[:, ST_RSTD1:ST_RSTD1 + 16],
                    AF.Exp, scale=-0.5), deps=[o_ln])
                o_nmr0 = s.add("vector", lambda e: e.tensor_mul(
                    stat[:, ST_NMR:ST_NMR + 16], stat[:, ST_MEAN1:ST_MEAN1 + 16],
                    stat[:, ST_RSTD1:ST_RSTD1 + 16]), deps=[o_rstd])
                o_nmr = s.add("vector", lambda e: e.tensor_scalar_mul(
                    stat[:, ST_NMR:ST_NMR + 16], stat[:, ST_NMR:ST_NMR + 16], -1.0),
                    deps=[o_nmr0])
                o_xb = s.add("sync", lambda e: e.dma_start(
                    out=xb_row[:, :], in_=x_ext[0:1, :]), dma=True, sem="ld")
                a = s.add("vector", lambda e: e.bn_stats(xb_st[:, 0:6], xb_row[:, 0:512]),
                          deps=[o_xb])
                b = s.add("vector", lambda e: e.bn_stats(xb_st[:, 6:12], xb_row[:, 512:1024]),
                          deps=[a])
                g = s.add("vector", lambda e: e.bn_aggr(xb_mv[:, 0:2], xb_st[:, :]),
                          deps=[a, b])
                l1 = s.add("scalar", lambda e: e.activation(
                    xb_mv[:, 2:3], xb_mv[:, 1:2], AF.Ln, bias=EPS), deps=[g])
                l2 = s.add("scalar", lambda e: e.activation(
                    xb_mv[:, 3:4], xb_mv[:, 2:3], AF.Exp, scale=-0.5), deps=[l1])
                l3 = s.add("vector", lambda e: e.tensor_mul(
                    xb_mv[:, 4:5], xb_mv[:, 0:1], xb_mv[:, 3:4]), deps=[l2, g])
                l4 = s.add("vector", lambda e: e.tensor_scalar_mul(
                    xb_mv[:, 4:5], xb_mv[:, 4:5], -1.0), deps=[l3])
                l5 = s.add("scalar", lambda e: e.activation(
                    xb_bf[:, :], xb_row[:, :], AF.Identity,
                    bias=xb_mv[:, 4:5], scale=xb_mv[:, 3:4]), deps=[l4])
                l6 = s.add("sync", lambda e: e.dma_start(
                    out=bnd_dram.rearrange("(o c) -> o c", o=1), in_=xb_bf[0:1, :]),
                    deps=[l5], dma=True, sem="ld")
                l7s = []
                for cc in range(NCC):
                    l7s.append(s.add("sync", lambda e, cc=cc: e.dma_start(
                        out=xnT[:, cc * TEC:cc * TEC + 1],
                        in_=bnd_dram[cc * 128:(cc + 1) * 128].rearrange("(p o) -> p o", o=1)),
                        deps=[l6], dma=True, sem="ld"))
                l7 = l7s[-1]
                ap_ops = [None] * NT
                tr_ops = [None] * NT
                cp_ops = [None] * NT
                for i in range(NT):
                    d = [o_rstd, o_nmr, xload[i]]
                    if i >= 2:
                        d.append(tr_ops[i - 2])
                    ap_ops[i] = s.add("scalar", lambda e, i=i: e.activation(
                        xn_tile[:, (i % 2) * C:(i % 2) * C + C],
                        XR[:, i * C:(i + 1) * C], AF.Identity,
                        bias=stat[:, ST_NMR + i:ST_NMR + i + 1],
                        scale=stat[:, ST_RSTD1 + i:ST_RSTD1 + i + 1]), deps=d)

                    def em_tr(e, i=i):
                        for cc in range(NCC):
                            ins = e.transpose(
                                ptr[:, (i % 2) * C + cc * 128:(i % 2) * C + cc * 128 + 128],
                                xn_tile[:, (i % 2) * C + cc * 128:(i % 2) * C + cc * 128 + 128],
                                ident_b[:, :])
                        return ins
                    d = [ap_ops[i], o_id]
                    if i >= 2:
                        d.append(cp_ops[i - 2])
                    tr_ops[i] = s.add("tensor", em_tr, deps=d)
                    cp_ops[i] = s.add("vector", lambda e, i=i: e.tensor_copy(
                        xnT.ap().rearrange("p (cc t) -> p cc t", cc=NCC)[:, :, 1 + i * 128:1 + (i + 1) * 128],
                        ptr.ap().rearrange("p (b cc t) -> p b cc t", b=2, cc=NCC)[:, i % 2]),
                        deps=[tr_ops[i]])
                mix1 = []
                for cc in range(NCC):
                    mix1.append(s.add("vector", lambda e, cc=cc: e.tensor_scalar_mul(
                        xmT[:, cc * T:(cc + 1) * T],
                        xnT[:, cc * TEC:cc * TEC + T],
                        stat[:, ST_TM1 + cc:ST_TM1 + cc + 1]),
                        deps=[o_tm1] + l7s + cp_ops))
                mix_done = []
                for cc in range(NCC):
                    mix_done.append(s.add("vector", lambda e, cc=cc: e.scalar_tensor_tensor(
                        xmT[:, cc * T:(cc + 1) * T],
                        xnT[:, cc * TEC + 1:cc * TEC + 1 + T],
                        vp[:, VP_TM + cc:VP_TM + cc + 1],
                        xmT[:, cc * T:(cc + 1) * T],
                        op0=ALU.mult, op1=ALU.add), deps=[mix1[cc]]))
                s.emit()

            if BUILD_UPTO == 12:
                _tap_out(nc, sems, XR, OUT, [])
                return nc
            with contextlib.ExitStack() as _s2x:
                ec2 = _s2x.enter_context
                wbuf = ec2(nc.sbuf_tensor("wbuf", [128, 2 * 2 * C], BF16))
                sl1 = ec2(nc.sbuf_tensor("sl1", [128, 2 * T], F32))
                sl2 = ec2(nc.sbuf_tensor("sl2", [128, 2 * T], F32))
                sl3 = ec2(nc.sbuf_tensor("sl3", [128, 2 * T], F32))
                sl6 = ec2(nc.sbuf_tensor("sl6", [128, 2 * T], F32))
                lamb = ec2(nc.sbuf_tensor("lamb", [128, T], BF16))
                psA = ec2(nc.psum_tensor("psA", [128, 2048], F32))
                psB = ec2(nc.psum_tensor("psB", [128, 2048], F32))
                _dummy = 1
                s = Section(nc, sems)
                rwkv_ops = [None] * NCC
                prev = {}
                E60 = float(np.exp(60.0))

                def emit_cc(oc):
                    if ("ccargs", oc) not in prev:
                        return
                    (snd, CM, xs) = prev.pop(("ccargs", oc))
                    cc = s.add("gpsimd", lambda e, oc=oc: e.collective_compute(
                        "AllGather", ALU.bypass, replica_groups=RG,
                        ins=[cc1_snd[oc].ap().opt()],
                        outs=[cc1_rcv[oc].ap().opt()]),
                        deps=[snd], coll=True, sem=xs)
                    last_coll[0] = cc
                    rcv = s.add("sync", lambda e, oc=oc, CM=CM: e.dma_start(
                        out=carry[:, CM:CM + 2], in_=cc1_rcv[oc].ap()[0]),
                        deps=[cc, prev.get(("cmask", oc - 2))], dma=True, sem=xs)
                    cmask = s.add("vector", lambda e, CM=CM: e.tensor_scalar_mul(
                        carry[:, CM + 2:CM + 4], carry[:, CM:CM + 2],
                        vp[:, VP_HM:VP_HM + 1]),
                        deps=[rcv, o_vp])
                    prev[("cmask", oc)] = cmask

                def emit_tail(oc):
                    if ("tailargs", oc) not in prev:
                        return
                    (s1, s2, s3, s6, lam0, CM,
                     w1b, w2e, er, ck) = prev.pop(("tailargs", oc))
                    cmask = prev[("cmask", oc)]
                    c1 = s.add("vector", lambda e, s2=s2, CM=CM:
                               e.scalar_tensor_tensor(
                                   s2, lamb[:, :], carry[:, CM + 2:CM + 3], s2,
                                   op0=ALU.mult, op1=ALU.add),
                               deps=[w1b, lam0, cmask])
                    c2 = s.add("vector", lambda e, s1=s1, CM=CM:
                               e.scalar_tensor_tensor(
                                   s1, lamb[:, :], carry[:, CM + 3:CM + 4], s1,
                                   op0=ALU.mult, op1=ALU.add),
                               deps=[w2e, lam0, cmask, c1])
                    prev[("lamused", oc)] = c2
                    dd = s.add("vector", lambda e, s1=s1, s3=s3, s6=s6:
                               e.scalar_tensor_tensor(
                                   s3, s6, 1.0, s1, op0=ALU.add, op1=ALU.mult),
                               deps=[er, c2])
                    prev[("dd", oc)] = dd
                    prev[("t2", oc)] = (s2, s3, c1, ck, dd)

                def emit_tail2a(oc):
                    if ("t2", oc) not in prev:
                        return
                    (s2, s3, c1, ck, dd) = prev[("t2", oc)]
                    lnd = s.add("scalar", lambda e, s3=s3: e.activation(s3, s3, AF.Ln),
                                deps=[dd])
                    invd = s.add("scalar", lambda e, s3=s3: e.activation(
                        s3, s3, AF.Exp, scale=-1.0), deps=[lnd])
                    prev[("invd", oc)] = invd

                def emit_tail2b(oc):
                    if ("t2", oc) not in prev:
                        return
                    (s2, s3, c1, ck, dd) = prev.pop(("t2", oc))
                    invd = prev.pop(("invd", oc))
                    rw = s.add("vector", lambda e, s2=s2, s3=s3, oc=oc: e.tensor_mul(
                        rwkvT[:, oc * T:(oc + 1) * T], s2, s3),
                        deps=[invd, c1, ck])
                    prev[("rw", oc)] = rw
                    rwkv_ops[oc] = rw

                for oc in range(NCC):
                    par = oc % 2
                    s1 = sl1[:, par * T:par * T + T]
                    s2 = sl2[:, par * T:par * T + T]
                    s3 = sl3[:, par * T:par * T + T]
                    s6 = sl6[:, par * T:par * T + T]
                    trash = rwkvT[:, oc * T:(oc + 1) * T]
                    ecol = stat[:, ST_E + oc:ST_E + oc + 1]
                    negecol = stat[:, ST_NEGE + oc:ST_NEGE + oc + 1]
                    n2047col = stat[:, ST_N2047E + oc:ST_N2047E + oc + 1]
                    fwcol = stat[:, ST_FW + oc:ST_FW + oc + 1]
                    lamcol = stat[:, ST_LAM + oc:ST_LAM + oc + 1]
                    base = (oc % 2) * 2 * C
                    CM = 16 + 4 * par

                    wdeps = [prev.get(("rmm", oc - 2))] if oc >= 2 else []
                    wk_d = s.add("gpsimd", lambda e, oc=oc, base=base: e.dma_start(
                        out=wbuf[:, base:base + C].rearrange("p (ic m) -> p ic m", ic=NCC),
                        in_=wkT[:, oc * 128:(oc + 1) * 128]
                        .rearrange("(ic p) m -> p ic m", p=128)), deps=wdeps, dma=True, sem="w")
                    wv_d = s.add("gpsimd", lambda e, oc=oc, base=base: e.dma_start(
                        out=wbuf[:, base + C:base + 2 * C].rearrange("p (ic m) -> p ic m", ic=NCC),
                        in_=wvT[:, oc * 128:(oc + 1) * 128]
                        .rearrange("(ic p) m -> p ic m", p=128)), deps=wdeps, dma=True, sem="w")
                    o_iota = s.add("gpsimd", lambda e, s3=s3: e.iota(
                        s3, [[1, T]], base=0, channel_multiplier=0,
                        allow_small_or_imprecise_dtypes=True),
                        deps=[prev.get(("rw", oc - 2))])


                    def emit_mms_h(e, ps, woff, h, base=base, po=0):
                        for qt in range(2 * h, 2 * h + 2):
                            for ic in range(NCC):
                                ins = e.matmul(
                                    ps[:, po + (qt - 2 * h) * 512:po + (qt - 2 * h + 1) * 512],
                                    wbuf[:, base + woff + ic * 128:base + woff + ic * 128 + 128],
                                    xmT[:, ic * T + qt * 512:ic * T + (qt + 1) * 512],
                                    start=(ic == 0), stop=(ic == NCC - 1))
                        return ins

                    kmm_h = [None, None]
                    vmm_h = [None, None]
                    rmm_h = [None, None]
                    ek_h = [None, None]
                    vev_h = [None, None]
                    er_h = [None, None]
                    for h in (0, 1):
                        d = [wk_d] + mix_done
                        pk = prev.get(("ek", oc - 1, h))
                        if pk is not None:
                            d.append(pk)
                        kmm_h[h] = s.add("tensor",
                                         lambda e, h=h, f=emit_mms_h:
                                         f(e, psA, 0, h, po=h * 1024),
                                         deps=d)
                        ek_h[h] = s.add("scalar", lambda e, s1=s1, h=h: e.activation(
                            s1[:, h * 1024:(h + 1) * 1024],
                            psA[:, h * 1024:(h + 1) * 1024], AF.Exp),
                            deps=[kmm_h[h], prev.get(("dd", oc - 2))])
                        prev[("ek", oc, h)] = ek_h[h]
                    wr_d = s.add("gpsimd", lambda e, oc=oc, base=base: e.dma_start(
                        out=wbuf[:, base:base + C].rearrange("p (ic m) -> p ic m", ic=NCC),
                        in_=wrT[:, oc * 128:(oc + 1) * 128]
                        .rearrange("(ic p) m -> p ic m", p=128)),
                        deps=[kmm_h[1]], dma=True, sem="w")
                    emit_cc(oc - 1)
                    emit_tail(oc - 1)
                    for h in (0, 1):
                        d = [wv_d]
                        pv = prev.get(("er", oc - 1, h))
                        if pv is not None:
                            d.append(pv)
                        vmm_h[h] = s.add("tensor",
                                         lambda e, h=h, f=emit_mms_h:
                                         f(e, psB, C, h, po=h * 1024),
                                         deps=d)
                        vev_h[h] = s.add("scalar", lambda e, s2=s2, h=h: e.activation(
                            s2[:, h * 1024:(h + 1) * 1024],
                            psB[:, h * 1024:(h + 1) * 1024], AF.Identity),
                            deps=[vmm_h[h], prev.get(("rw", oc - 2))])
                        prev[("vev", oc, h)] = vev_h[h]
                    for h in (0, 1):
                        d = [wr_d, vev_h[h]]
                        rmm_h[h] = s.add("tensor",
                                         lambda e, h=h, f=emit_mms_h:
                                         f(e, psB, 0, h, po=h * 1024),
                                         deps=d)
                        er_h[h] = s.add("scalar", lambda e, s6=s6, h=h: e.activation(
                            s6[:, h * 1024:(h + 1) * 1024],
                            psB[:, h * 1024:(h + 1) * 1024], AF.Exp, scale=-1.0),
                            deps=[rmm_h[h], prev.get(("dd", oc - 2))])
                        prev[("er", oc, h)] = er_h[h]
                    ek = ek_h[1]
                    vev = vev_h[1]
                    er = er_h[1]
                    prev[("er", oc)] = er
                    prev[("vev", oc)] = vev
                    prev[("rmm", oc)] = rmm_h[1]
                    lam0 = s.add("scalar", lambda e, negecol=negecol, s3=s3: e.activation(
                        lamb[:, :], s3, AF.Exp, scale=negecol),
                        deps=[o_iota, o_nege, prev.get(("lamused", oc - 1))])
                    emit_tail2a(oc - 1)
                    clp = s.add("vector", lambda e, s1=s1: e.tensor_scalar_min(s1, s1, E60),
                                deps=[ek, prev.get(("dd", oc - 2))])
                    kv = s.add("vector", lambda e, s1=s1, s2=s2: e.tensor_mul(s2, s1, s2),
                               deps=[clp, vev])
                    emit_tail2b(oc - 1)
                    lrev = s.add("scalar", lambda e, s3=s3, ecol=ecol, n2047col=n2047col:
                                 e.activation(s3, s3, AF.Exp,
                                              scale=ecol, bias=n2047col),
                                 deps=[o_iota, o_e, o_n2047, lam0])
                    ckv = s.add("vector", lambda e, s2=s2, s3=s3, trash=trash, oc=oc:
                                e.scalar_tensor_tensor(
                                    trash, s2, 1.0, s3, op0=ALU.mult, op1=ALU.mult,
                                    accum_out=carry[:, 2 * oc:2 * oc + 1]),
                                deps=[kv, lrev, o_m5])
                    ck = s.add("vector", lambda e, s1=s1, s3=s3, trash=trash, oc=oc:
                               e.scalar_tensor_tensor(
                                   trash, s1, 1.0, s3, op0=ALU.mult, op1=ALU.mult,
                                   accum_out=carry[:, 2 * oc + 1:2 * oc + 2]),
                               deps=[ckv])
                    xs = f"x{oc}"
                    snd = s.add("sync", lambda e, oc=oc: e.dma_start(
                        out=cc1_snd[oc].ap(), in_=carry[:, 2 * oc:2 * oc + 2]),
                        deps=[ck], dma=True, sem=xs)
                    prev[("ccargs", oc)] = (snd, CM, xs)
                    sckv = s.add("vector", lambda e, s2=s2, s3=s3, lamcol=lamcol:
                                 e.tensor_tensor_scan(s3, lamcol.broadcast_to([128, T]),
                                                      s2, 0.0, op0=ALU.mult, op1=ALU.add),
                                 deps=[kv, ckv, ck, o_lam, lrev])
                    w1 = s.add("vector", lambda e, s2=s2, s3=s3, fwcol=fwcol:
                               e.scalar_tensor_tensor(
                                   s2[:, 1:T], s2[:, 1:T], fwcol, s3[:, 0:T - 1],
                                   op0=ALU.mult, op1=ALU.add), deps=[sckv, ck])
                    w1b = s.add("vector", lambda e, s2=s2, fwcol=fwcol:
                                e.tensor_scalar_mul(s2[:, 0:1], s2[:, 0:1], fwcol),
                                deps=[w1])
                    sck = s.add("vector", lambda e, s1=s1, s3=s3, lamcol=lamcol:
                                e.tensor_tensor_scan(s3, lamcol.broadcast_to([128, T]),
                                                     s1, 0.0, op0=ALU.mult, op1=ALU.add),
                                deps=[w1, ck])
                    w2 = s.add("vector", lambda e, s1=s1, s3=s3, fwcol=fwcol:
                               e.scalar_tensor_tensor(
                                   s1[:, 1:T], s1[:, 1:T], fwcol, s3[:, 0:T - 1],
                                   op0=ALU.mult, op1=ALU.add), deps=[sck])
                    w2b = s.add("vector", lambda e, s1=s1, fwcol=fwcol:
                                e.tensor_scalar_mul(s1[:, 0:1], s1[:, 0:1], fwcol),
                                deps=[w2])
                    w2e = s.add("vector", lambda e, s1=s1: e.tensor_scalar_add(
                        s1, s1, K_EPS), deps=[w2b])
                    prev[("tailargs", oc)] = (s1, s2, s3, s6, lam0, CM,
                                              w1b, w2e, er, ck)
                emit_cc(NCC - 1)
                emit_tail(NCC - 1)
                emit_tail2a(NCC - 1)
                emit_tail2b(NCC - 1)
                s.emit()

            if BUILD_UPTO == 13:
                _tap_out(nc, sems, XR, OUT, rwkv_ops)
                return nc
            with contextlib.ExitStack() as _s3x:
                ec3 = _s3x.enter_context
                wo_sb = ec3(nc.sbuf_tensor("wo_sb", [128, NCC * C], BF16))
                o_ev = ec3(nc.sbuf_tensor("o_ev", [128, 2 * 512], F32))
                psO = ec3(nc.psum_tensor("psO", [128, 2 * 512], F32))
                psT = ec3(nc.psum_tensor("psT", [128, 2 * 512], F32))
                _dummy = 1
                s = Section(nc, sems)
                wo_d = s.add("gpsimd", lambda e: e.dma_start(
                    out=wo_sb.ap().rearrange("p (ic m) -> p ic m", ic=NCC),
                    in_=woT.rearrange("(ic p) m -> p ic m", p=128)), dma=True, sem="w")
                r1_ops = []
                prev_ev = {}
                prev_tr = {}
                prev_add = {}
                k = 0
                for oc in range(NCC):
                    for qt in range(4):
                        sl = k % 2

                        def em_owmm(e, oc=oc, qt=qt, sl=sl):
                            for ic in range(NCC):
                                ins = e.matmul(
                                    psO[:, sl * 512:sl * 512 + 512],
                                    wo_sb[:, ic * C + oc * 128:ic * C + oc * 128 + 128],
                                    rwkvT[:, ic * T + qt * 512:ic * T + (qt + 1) * 512],
                                    start=(ic == 0), stop=(ic == NCC - 1))
                            return ins
                        d = [wo_d] + rwkv_ops
                        if k >= 2:
                            d.append(prev_ev[k - 2])
                        mm = s.add("tensor", em_owmm, deps=d)
                        ev = s.add("vector", lambda e, sl=sl: e.tensor_copy(
                            o_ev[:, sl * 512:sl * 512 + 512], psO[:, sl * 512:sl * 512 + 512]),
                            deps=[mm] + ([prev_tr[k - 2]] if k >= 2 else []))
                        prev_ev[k] = ev

                        def em_tr(e, oc=oc, qt=qt, sl=sl):
                            for j in range(4):
                                ins = e.transpose(
                                    psT[:, sl * 512 + j * 128:sl * 512 + j * 128 + 128],
                                    o_ev[:, sl * 512 + j * 128:sl * 512 + j * 128 + 128],
                                    ident_f[:, :])
                            return ins
                        tr = s.add("tensor", em_tr,
                                   deps=[ev, o_idf] + ([prev_add[k - 2]] if k >= 2 else []))
                        prev_tr[k] = tr

                        def em_add(e, oc=oc, qt=qt, sl=sl):
                            return e.tensor_add(
                                XR.ap().rearrange("p (n c) -> p n c", n=NT)[
                                    :, 4 * qt:4 * qt + 4, oc * 128:oc * 128 + 128],
                                XR.ap().rearrange("p (n c) -> p n c", n=NT)[
                                    :, 4 * qt:4 * qt + 4, oc * 128:oc * 128 + 128],
                                psT.ap().rearrange("p (b j t) -> p (b j) t", b=2, j=4)[
                                    :, sl * 4:sl * 4 + 4, :])
                        ad = s.add("vector", em_add, deps=[tr])
                        prev_add[k] = ad
                        r1_ops.append(ad)
                        k += 1

                xn2o = ec3(nc.sbuf_tensor("xn2o", [128, NCC * T], BF16))
                x2_tile = ec3(nc.sbuf_tensor("x2_tile", [128, 2 * C], BF16))
                ptr2 = ec3(nc.psum_tensor("ptr2", [128, 2 * C], BF16))

                stats2 = []
                prevv = None
                for i in range(NT):
                    d = [r1_ops[7 * 4 + i // 4]] + ([prevv] if prevv else [])
                    a = s.add("vector", lambda e, i=i: e.bn_stats(
                        st6[:, 0:6], XR[:, i * C:i * C + 512]), deps=d)
                    b = s.add("vector", lambda e, i=i: e.bn_stats(
                        st6[:, 6:12], XR[:, i * C + 512:i * C + 1024]), deps=[a])
                    g = s.add("vector", lambda e: e.bn_aggr(mv[:, 0:2], st6[:, :]), deps=[a, b])
                    cm = s.add("vector", lambda e, i=i: e.tensor_copy(
                        stat[:, ST_MEAN1 + i:ST_MEAN1 + i + 1], mv[:, 0:1]), deps=[g])
                    cv = s.add("vector", lambda e, i=i: e.tensor_copy(
                        stat[:, ST_VAR1 + i:ST_VAR1 + i + 1], mv[:, 1:2]), deps=[g])
                    prevv = cv
                    stats2.extend([cm, cv])
                LNU, IU, V2 = ST_SY, ST_BY, ST_W1
                o1 = s.add("scalar", lambda e: e.activation(
                    stat[:, LNU:LNU + 16], stat[:, ST_VAR1:ST_VAR1 + 16], AF.Ln, bias=EPS),
                    deps=stats2)
                o2 = s.add("scalar", lambda e: e.activation(
                    stat[:, ST_RSTD1:ST_RSTD1 + 16], stat[:, LNU:LNU + 16], AF.Exp, scale=-0.5),
                    deps=[o1])
                o3 = s.add("scalar", lambda e: e.activation(
                    stat[:, IU:IU + 16], stat[:, LNU:LNU + 16], AF.Exp, scale=-1.0), deps=[o1])
                o4 = s.add("vector", lambda e: e.tensor_mul(
                    stat[:, V2:V2 + 16], stat[:, ST_VAR1:ST_VAR1 + 16], stat[:, IU:IU + 16]),
                    deps=[o3])
                o5 = s.add("scalar", lambda e: e.activation(
                    stat[:, V2:V2 + 16], stat[:, V2:V2 + 16], AF.Ln, bias=EPS), deps=[o4])
                o6 = s.add("scalar", lambda e: e.activation(
                    stat[:, V2:V2 + 16], stat[:, V2:V2 + 16], AF.Exp, scale=-0.5), deps=[o5])
                o7 = s.add("vector", lambda e: e.tensor_mul(
                    stat[:, ST_SY:ST_SY + 16], stat[:, ST_RSTD1:ST_RSTD1 + 16],
                    stat[:, V2:V2 + 16]), deps=[o6, o2, o3])
                o8 = s.add("vector", lambda e: e.tensor_mul(
                    stat[:, ST_BY:ST_BY + 16], stat[:, ST_MEAN1:ST_MEAN1 + 16],
                    stat[:, ST_SY:ST_SY + 16]), deps=[o7])
                o9 = s.add("vector", lambda e: e.tensor_scalar_mul(
                    stat[:, ST_BY:ST_BY + 16], stat[:, ST_BY:ST_BY + 16], -1.0), deps=[o8])
                oW = s.add("vector", lambda e: e.tensor_scalar(
                    stat[:, ST_W1:ST_W1 + 16], stat[:, ST_RSTD1:ST_RSTD1 + 16], 1.0, 1.0,
                    op0=ALU.mult, op1=ALU.add), deps=[o8, o6])
                oN = s.add("vector", lambda e: e.tensor_mul(
                    stat[:, ST_NMR:ST_NMR + 16], stat[:, ST_MEAN1:ST_MEAN1 + 16],
                    stat[:, ST_RSTD1:ST_RSTD1 + 16]), deps=[o7])
                oN2 = s.add("vector", lambda e: e.tensor_scalar_mul(
                    stat[:, ST_NMR:ST_NMR + 16], stat[:, ST_NMR:ST_NMR + 16], -1.0), deps=[oN])
                tr2 = [None] * NT
                cp2 = [None] * NT
                ap2 = [None] * NT
                snd2h = {}
                cc2h = {}
                for i in range(NT):
                    d = [o7, o9]
                    if i >= 2:
                        d.append(tr2[i - 2])
                    ap2[i] = s.add("scalar", lambda e, i=i: e.activation(
                        x2_tile[:, (i % 2) * C:(i % 2) * C + C],
                        XR[:, i * C:(i + 1) * C], AF.Identity,
                        bias=stat[:, ST_BY + i:ST_BY + i + 1],
                        scale=stat[:, ST_SY + i:ST_SY + i + 1]), deps=d)

                    def em_tr2(e, i=i):
                        for cc in range(NCC):
                            ins = e.transpose(
                                ptr2[:, (i % 2) * C + cc * 128:(i % 2) * C + cc * 128 + 128],
                                x2_tile[:, (i % 2) * C + cc * 128:(i % 2) * C + cc * 128 + 128],
                                ident_b[:, :])
                        return ins
                    d = [ap2[i]]
                    if i >= 2:
                        d.append(cp2[i - 2])
                    tr2[i] = s.add("tensor", em_tr2, deps=d)
                    cp2[i] = s.add("vector", lambda e, i=i: e.tensor_copy(
                        xn2o.ap().rearrange("p (cc t) -> p cc t", cc=NCC)[:, :, i * 128:(i + 1) * 128],
                        ptr2.ap().rearrange("p (b cc t) -> p b cc t", b=2, cc=NCC)[:, i % 2]),
                        deps=[tr2[i]])
                    if i in (7, 15):
                        hh = i // 8
                        affh = []
                        for cc in range(NCC):
                            affh.append(s.add("vector", lambda e, cc=cc, hh=hh: e.tensor_scalar(
                                xn2o[:, cc * T + hh * 1024:cc * T + hh * 1024 + 1024],
                                xn2o[:, cc * T + hh * 1024:cc * T + hh * 1024 + 1024],
                                vp[:, VP_NW + cc:VP_NW + cc + 1],
                                vp[:, VP_NB + cc:VP_NB + cc + 1],
                                op0=ALU.mult, op1=ALU.add),
                                deps=[o_vp] + cp2[hh * 8:hh * 8 + 8]))
                        snd2h[hh] = s.add("sync", lambda e, hh=hh: e.dma_start(
                            out=cc2_snd[hh].ap().rearrange("p (cc t) -> p cc t", cc=NCC),
                            in_=xn2o.ap().rearrange("p (cc t) -> p cc t", cc=NCC)[
                                :, :, hh * 1024:hh * 1024 + 1024]),
                            deps=affh, dma=True, sem="io")
                        cc2h[hh] = s.add("gpsimd", lambda e, hh=hh: e.collective_compute(
                            "AllGather", ALU.bypass, replica_groups=RG,
                            ins=[cc2_snd[hh].ap().opt()], outs=[cc2_rcv[hh].ap().opt()]),
                            deps=[snd2h[hh], last_coll[0] if hh == 0 else cc2h[0]],
                            coll=True, sem="cc")
                cc2 = cc2h[1]
                spill = s.add("sync", lambda e: e.dma_start(
                    out=r1_spill.rearrange("(n p) c -> p n c", p=128),
                    in_=XR.ap().rearrange("p (n c) -> p n c", c=C)),
                    deps=[ap2[NT - 1]] + stats2, dma=True, sem="spl")
                s.emit()


        if BUILD_UPTO == 1:
            _tap_out(nc, sems, XR, OUT, r1_ops)
            return nc

        qk_ops = []
        v_ops = []
        with contextlib.ExitStack() as _s5x:
            ec5 = _s5x.enter_context
            Vt = ec5(nc.sbuf_tensor("Vt", [128, 32 * 512], BF16))
            qmkm = ec5(nc.sbuf_tensor("qmkm", [128, 20], F32))
            stt_row = ec5(nc.sbuf_tensor("stt_row", [1, 4], F32))
            _dummy = 1
            with contextlib.ExitStack() as _s5y:
                ec5y = _s5y.enter_context
                xn2f = ec5y(nc.sbuf_tensor("xn2f", [128, NCC * TFULL], BF16))
                pw_sb = ec5y(nc.sbuf_tensor("pw_sb", [128, 2 * 8 * 512], BF16))
                vw_sb = ec5y(nc.sbuf_tensor("vw_sb", [128, 8 * 512], BF16))
                vb_sb = ec5y(nc.sbuf_tensor("vb_sb", [1, 512], BF16))
                _dummy = 1
                s = Section(nc, sems)
                wq_d = s.add("gpsimd", lambda e: e.dma_start(
                    out=pw_sb[:, 0:8 * 512].rearrange("p (ic m) -> p ic m", ic=NCC),
                    in_=qwT.rearrange("(ic p) m -> p ic m", p=128)), dma=True, sem="w")
                wk_d = s.add("gpsimd", lambda e: e.dma_start(
                    out=pw_sb[:, 8 * 512:16 * 512].rearrange("p (ic m) -> p ic m", ic=NCC),
                    in_=kwT.rearrange("(ic p) m -> p ic m", p=128)), dma=True, sem="w")
                wv_d = s.add("gpsimd", lambda e: e.dma_start(
                    out=vw_sb.ap().rearrange("p (ic m) -> p ic m", ic=NCC),
                    in_=vwT.rearrange("(ic p) m -> p ic m", p=128)), dma=True, sem="w")
                vb_d = s.add("gpsimd", lambda e: e.dma_start(
                    out=vb_sb[:, :], in_=vb_row), dma=True, sem="w")
                rcvh = {0: [], 1: []}
                for a in range(2):
                    for r in range(2):
                        dd = [cc2h[a], spill]
                        if a == 0:
                            dd.append(snd2h[1])
                        rcvh[a].append(s.add("sync", lambda e, a=a, r=r: e.dma_start(
                            out=xn2f.ap().rearrange("p (cc t) -> p cc t", cc=NCC)[
                                :, :, r * T + a * 1024:r * T + a * 1024 + 1024],
                            in_=cc2_rcv[a].ap()[r].rearrange("p (cc t) -> p cc t", cc=NCC)),
                            deps=dd, dma=True, sem="io" if r == 0 else "io2"))

                with contextlib.ExitStack() as _s5p:
                    psP = _s5p.enter_context(nc.psum_tensor("psP", [128, 2 * 512], F32))
                    prev_ev = {}
                    red_ops = []
                    k5 = 0
                    units = []
                    for half in range(2):
                        qts = (0, 1, 4, 5) if half == 0 else (2, 3, 6, 7)
                        tcs = (list(range(0, 8)) + list(range(16, 24)) if half == 0
                               else list(range(8, 16)) + list(range(24, 32)))
                        for which, woff, dst, bcol in (("q", 0, QT, VP_QB),
                                                       ("k", 8 * 512, KT, VP_KB)):
                            for j in range(4):
                                for qt in qts:
                                    units.append(("qk", half, which, woff, dst, bcol, j, qt))
                                if half == 1:
                                    units.append(("red", which, j))
                        for tc in tcs:
                            units.append(("v", half, tc))
                    qk_by = {}
                    for u in units:
                        if u[0] == "red":
                            _, which, j = u
                            rbase = 4 if which == "q" else 12
                            xbase = (0 if which == "q" else 16384) + j * 4096
                            red_ops.append(s.add("vector", lambda e, rbase=rbase, j=j, xbase=xbase:
                                e.tensor_reduce(
                                    qmkm[:, rbase + j:rbase + j + 1],
                                    XRb[:, xbase:xbase + 4096],
                                    axis=AX.X, op=ALU.max, apply_absolute_value=True),
                                deps=qk_by[(which, j)]))
                            continue
                        sl = k5 % 2
                        if u[0] == "qk":
                            _, half, which, woff, dst, bcol, j, qt = u
                            nic = NCC >> (qt // 2)

                            def em_p(e, j=j, qt=qt, nic=nic, sl=sl, woff=woff):
                                for ic in range(nic):
                                    ins = e.matmul(
                                        psP[:, sl * 512:sl * 512 + 512],
                                        pw_sb[:, woff + ic * 512 + j * 128:woff + ic * 512 + j * 128 + 128],
                                        xn2f[:, ic * TFULL + qt * 512:ic * TFULL + (qt + 1) * 512],
                                        start=(ic == 0), stop=(ic == nic - 1))
                                return ins
                            d = rcvh[half] + [wq_d if which == "q" else wk_d, spill]
                            if k5 >= 2:
                                d.append(prev_ev[k5 - 2])
                            mm = s.add("tensor", em_p, deps=d)
                            ev = s.add("vector", lambda e, j=j, qt=qt, sl=sl, dst=dst, bcol=bcol: e.tensor_scalar(
                                dst(j)[:, qt * 512:(qt + 1) * 512],
                                psP[:, sl * 512:sl * 512 + 512],
                                vp[:, bcol + j:bcol + j + 1], None, op0=ALU.add),
                                deps=[mm, o_vp])
                            qk_ops.append(ev)
                            qk_by.setdefault((which, j), []).append(ev)
                        else:
                            _, half, tc = u
                            nic = NCC >> (tc // 8)

                            def em_v(e, tc=tc, nic=nic, sl=sl):
                                for ic in range(nic):
                                    e.matmul(
                                        psP[:, sl * 512:sl * 512 + 512],
                                        xn2f[:, ic * TFULL + tc * 128:ic * TFULL + tc * 128 + 128],
                                        vw_sb[:, ic * 512:(ic + 1) * 512],
                                        start=(ic == 0), stop=False)
                                ins = e.matmul(psP[:, sl * 512:sl * 512 + 512],
                                               ones_r1[:, :], vb_sb[:, :],
                                               start=False, stop=True)
                                return ins
                            d = rcvh[half] + [wv_d, vb_d, o_m1]
                            if k5 >= 2:
                                d.append(prev_ev[k5 - 2])
                            mm = s.add("tensor", em_v, deps=d)
                            ev = s.add("vector", lambda e, tc=tc, sl=sl: e.tensor_copy(
                                Vt[:, tc * 512:(tc + 1) * 512], psP[:, sl * 512:sl * 512 + 512]),
                                deps=[mm])
                            v_ops.append(ev)
                        prev_ev[k5] = ev
                        k5 += 1
                    g1 = s.add("vector", lambda e: e.tensor_reduce(
                        qmkm[:, 0:1], qmkm[:, 4:8], axis=AX.X, op=ALU.max),
                        deps=qk_ops + red_ops)
                    g2 = s.add("vector", lambda e: e.tensor_reduce(
                        qmkm[:, 1:2], qmkm[:, 12:16], axis=AX.X, op=ALU.max),
                        deps=qk_ops + red_ops + [g1])
                    with nc.psum_tensor("psst", [128, 132], F32) as psst:
                        t1_ = s.add("tensor", lambda e: e.transpose(
                            psst[0:2, 0:128], qmkm[:, 0:2], ident_f[:, :]),
                            deps=[g1, g2, o_idf])
                        r1_ = s.add("vector", lambda e: e.tensor_reduce(
                            qmkm[0:2, 0:1], psst[0:2, 0:128], axis=AX.X, op=ALU.max),
                            deps=[t1_, g2])
                        t2_ = s.add("tensor", lambda e: e.transpose(
                            psst[0:1, 128:130], qmkm[0:2, 0:1], ident_f[0:2, 0:2]),
                            deps=[r1_])
                        m0_ = s.add("vector", lambda e: e.tensor_copy(
                            stt_row[0:1, 2:4], psst[0:1, 128:130]), deps=[t2_])
                        m1_ = s.add("vector", lambda e: e.tensor_mul(
                            stt_row[0:1, 0:1], stt_row[0:1, 2:3], stt_row[0:1, 3:4]),
                            deps=[m0_])
                        m2_ = s.add("vector", lambda e: e.tensor_scalar_mul(
                            stt_row[0:1, 0:1], stt_row[0:1, 0:1], -4.0), deps=[m1_])
                        b1_ = s.add("tensor", lambda e: e.matmul(
                            psst[:, 130:131], ones_r1f[:, :], stt_row[0:1, 0:1],
                            start=True, stop=True), deps=[m2_, o_m2])
                        b2_ = s.add("vector", lambda e: e.tensor_copy(
                            cbias[:, 0:1], psst[:, 130:131]), deps=[b1_])
                    s.emit()

            attn_oT = ec5(nc.sbuf_tensor("attn_oT", [128, 4 * TFULL], BF16))

            if BUILD_UPTO == 2:
                _tap_out(nc, sems, XR, OUT, qk_ops + v_ops + [b2_])
                return nc

            with contextlib.ExitStack() as _s6x:
                ec6 = _s6x.enter_context
                recipb = ec6(nc.sbuf_tensor("recipb", [128, 2 * 512], F32))
                rrow = ec6(nc.sbuf_tensor("rrow", [1, 2048], F32))
                ptb = ec6(nc.sbuf_tensor("ptb", [128, 2 * 2048], BF16))
                praw = ec6(nc.sbuf_tensor("praw", [128, 2 * 512], F32))
                pssc = ec6(nc.psum_tensor("pssc", [128, 2048], F32))
                psPV = ec6(nc.psum_tensor("psPV", [128, 2 * 512], F32))
                psDB = ec6(nc.psum_tensor("psDB", [128, 2 * 512], F32))
                _dummy = 1
                s = Section(nc, sems)
                attn_ops = []
                hist = {}
                NQ = 16

                def emit_ep_early(p, qt):
                    den_last = hist["last_den"]
                    pv_last = hist["last_pv"]
                    cr = s.add("vector", lambda e: e.tensor_copy(
                        praw[:, :], psPV[:, :]), deps=[pv_last, hist.get("raw_used")])
                    hist["raw_copy"] = cr
                    cps = []
                    for half, r0, r1_ in ((0, 0, 64), (1, 32, 96)):
                        ca = s.add("vector", lambda e, r0=r0, half=half: e.tensor_copy(
                            rrow[0:1, half * 1024:half * 1024 + 512],
                            psDB[r0:r0 + 1, 0:512]), deps=[den_last, hist.get("rr_used")])
                        cb = s.add("vector", lambda e, r1_=r1_, half=half: e.tensor_add(
                            rrow[0:1, half * 1024:half * 1024 + 512],
                            rrow[0:1, half * 1024:half * 1024 + 512],
                            psDB[r1_:r1_ + 1, 0:512]), deps=[ca])
                        cps.append(cb)
                    hist["last_rr"] = cps[-1]
                    hist["den_cp"] = cps

                def emit_ep_late_half(p, qt, half):
                    jj = 2 * p + half
                    rr1 = s.add("scalar", lambda e, half=half: e.activation(
                        rrow[0:1, half * 1024:half * 1024 + 512],
                        rrow[0:1, half * 1024:half * 1024 + 512], AF.Ln),
                        deps=[hist["den_cp"][half]])
                    rr2 = s.add("scalar", lambda e, half=half: e.activation(
                        rrow[0:1, half * 1024:half * 1024 + 512],
                        rrow[0:1, half * 1024:half * 1024 + 512],
                        AF.Exp, scale=-1.0), deps=[rr1])
                    d1 = s.add("sync", lambda e, half=half: e.dma_start(
                        out=rr_dram[half].ap(),
                        in_=rrow[0:1, half * 1024:half * 1024 + 512]),
                        deps=[rr2], dma=True, sem="rb")
                    d2 = s.add("sync", lambda e, half=half: e.dma_start(
                        out=recipb[:, half * 512:half * 512 + 512],
                        in_=rr_dram[half].ap().broadcast_to([128, 512])),
                        deps=[d1, hist.get(("rb_used", half))], dma=True, sem="rb")
                    ev = s.add("vector", lambda e, jj=jj, qt=qt, half=half: e.tensor_mul(
                        attn_oT[:, jj * TFULL + qt * 512:jj * TFULL + (qt + 1) * 512],
                        praw[:, half * 512:half * 512 + 512],
                        recipb[:, half * 512:half * 512 + 512]),
                        deps=[d2, hist["raw_copy"]])
                    attn_ops.append(ev)
                    hist[("rb_used", half)] = ev
                    if half == 1:
                        hist["raw_used"] = ev
                        hist["rr_used"] = d1

                def emit_ep_late(p, qt):
                    emit_ep_late_half(p, qt, 0)
                    emit_ep_late_half(p, qt, 1)

                pending = None
                NQ2 = 32
                units = [(p, qt) for p in range(2) for qt in range(8)]
                for p, qt in units:
                    j0, j1 = 2 * p, 2 * p + 1
                    exp_ops = [None] * NQ2
                    qk_mms = [None] * NQ2
                    pv_mms = [None] * NQ2
                    den_mms = {}

                    def em_qk(e, q, j0=j0, j1=j1, qt=qt):
                        for i, jj in enumerate((j0, j1)):
                            ins = e.matmul(
                                pssc[:, (q % 2) * 1024 + i * 512:
                                     (q % 2) * 1024 + (i + 1) * 512],
                                KT(jj)[:, q * 128:(q + 1) * 128],
                                QT(jj)[:, qt * 512:(qt + 1) * 512],
                                start=True, stop=True)
                        return ins

                    def em_pv(e, q, j0=j0, j1=j1):
                        for i, jj in enumerate((j0, j1)):
                            ins = e.matmul(
                                psPV[:, i * 512:(i + 1) * 512],
                                Vt[:, q * 512 + jj * 128:q * 512 + jj * 128 + 128],
                                ptb[:, (q % 4) * 1024 + i * 512:
                                    (q % 4) * 1024 + (i + 1) * 512],
                                start=(q == 0), stop=(q == NQ2 - 1))
                        return ins

                    def em_den(e, qa, qb):
                        first = (qa == 0)
                        last = (qb == NQ2 - 1)
                        for row, kcx, i in ((0, qa, 0), (32, qa, 1),
                                            (64, qb, 0), (96, qb, 1)):
                            ins = e.matmul(
                                psDB[row:row + 1, 0:512], ones_c1[:, :],
                                ptb[:, (kcx % 4) * 1024 + i * 512:
                                    (kcx % 4) * 1024 + (i + 1) * 512],
                                start=first, stop=last, tile_position=(0, row))
                        return ins

                    for q_ in range(NQ2):
                        if q_ >= 2:
                            d = [exp_ops[q_ - 2]]
                        elif hist.get("pexp") is not None:
                            d = [hist["pexp"][q_]]
                        else:
                            d = qk_ops + v_ops + [b2_]
                        qk_mms[q_] = s.add("tensor",
                                           lambda e, q=q_, f=em_qk: f(e, q), deps=d)

                        ex_d = [qk_mms[q_], b2_]
                        if q_ >= 4:
                            ex_d += [pv_mms[q_ - 4], den_mms[(q_ - 4) // 2]]
                        elif hist.get("ppv") is not None:
                            ex_d += [hist["ppv"][q_], hist["pdb"][q_ // 2]]
                        exp_ops[q_] = s.add("scalar", lambda e, q=q_: e.activation(
                            ptb[:, (q % 4) * 1024:(q % 4) * 1024 + 1024],
                            pssc[:, (q % 2) * 1024:(q % 2) * 1024 + 1024],
                            AF.Exp, scale=1.0 / 32.0, bias=cbias[:, 0:1]), deps=ex_d)

                        if q_ >= 1:
                            d = [exp_ops[q_ - 1]]
                            if q_ - 1 == 0 and hist.get("raw_copy") is not None:
                                d.append(hist["raw_copy"])
                            pv_mms[q_ - 1] = s.add("tensor",
                                lambda e, q=q_ - 1, f=em_pv: f(e, q), deps=d)
                            if (q_ - 1) % 2 == 1:
                                bi = (q_ - 1) // 2
                                dd_ = [exp_ops[q_ - 1], exp_ops[q_ - 2], o_m3]
                                if bi == 0 and hist.get("last_rr") is not None:
                                    dd_.append(hist["last_rr"])
                                if bi == 0 and hist.get("rr_used") is not None:
                                    dd_.append(hist.get("rr_used"))
                                den_mms[bi] = s.add("tensor",
                                    lambda e, qa=q_ - 2, qb=q_ - 1, f=em_den:
                                    f(e, qa, qb), deps=dd_)
                        if q_ == 8 and pending is not None:
                            emit_ep_late_half(pending[0], pending[1], 0)
                        if q_ == 20 and pending is not None:
                            done_u = pending
                            emit_ep_late_half(pending[0], pending[1], 1)
                            pending = None
                            if done_u == (0, 7):
                                snd3a = s.add("sync", lambda e: e.dma_start(
                                    out=cc3_snd_a.ap(),
                                    in_=attn_oT[:, 0:2 * TFULL]),
                                    deps=list(attn_ops), dma=True, sem="io")
                                cc3a = s.add("gpsimd", lambda e: e.collective_compute(
                                    "AllGather", ALU.bypass, replica_groups=RG,
                                    ins=[cc3_snd_a.ap().opt()],
                                    outs=[cc3_rcv_a.ap().opt()]),
                                    deps=[snd3a, cc2], coll=True, sem="cc")
                            elif done_u == (1, 3):
                                snd3b1 = s.add("sync", lambda e: e.dma_start(
                                    out=cc3_snd_b1.ap().rearrange(
                                        "p (j t) -> p j t", j=2),
                                    in_=attn_oT.ap().rearrange(
                                        "p (j t) -> p j t", j=4)[:, 2:4, 0:T]),
                                    deps=attn_ops[16:24], dma=True, sem="io")
                                cc3b1 = s.add("gpsimd", lambda e: e.collective_compute(
                                    "AllGather", ALU.bypass, replica_groups=RG,
                                    ins=[cc3_snd_b1.ap().opt()],
                                    outs=[cc3_rcv_b1.ap().opt()]),
                                    deps=[snd3b1, cc3a], coll=True, sem="cc")
                    pv_mms[NQ2 - 1] = s.add("tensor",
                        lambda e, q=NQ2 - 1, f=em_pv: f(e, q),
                        deps=[exp_ops[NQ2 - 1]])
                    den_mms[(NQ2 - 1) // 2] = s.add("tensor",
                        lambda e, qa=NQ2 - 2, qb=NQ2 - 1, f=em_den: f(e, qa, qb),
                        deps=[exp_ops[NQ2 - 1], exp_ops[NQ2 - 2], o_m3])
                    hist["pexp"] = (exp_ops[NQ2 - 2], exp_ops[NQ2 - 1])
                    hist["ppv"] = [pv_mms[NQ2 - 4 + i] for i in range(4)]
                    hist["pdb"] = [den_mms[(NQ2 - 4) // 2], den_mms[(NQ2 - 2) // 2]]
                    hist["last_pv"] = pv_mms[NQ2 - 1]
                    hist["last_den"] = den_mms[(NQ2 - 1) // 2]
                    emit_ep_early(p, qt)
                    pending = (p, qt)
                emit_ep_late(*pending)
                s.emit()

            if BUILD_UPTO == 3:
                _tap_out(nc, sems, attn_oT.ap().bitcast(F32), OUT, attn_ops, ntiles=8)
                return nc

            s = Section(nc, sems)
            snd3b2 = s.add("sync", lambda e: e.dma_start(
                out=cc3_snd_b2.ap().rearrange("p (j t) -> p j t", j=2),
                in_=attn_oT.ap().rearrange("p (j t) -> p j t", j=4)[:, 2:4, T:2 * T]),
                deps=attn_ops, dma=True, sem="io")
            cc3b2 = s.add("gpsimd", lambda e: e.collective_compute(
                "AllGather", ALU.bypass, replica_groups=RG,
                ins=[cc3_snd_b2.ap().opt()], outs=[cc3_rcv_b2.ap().opt()]),
                deps=[snd3b2, cc3b1], coll=True, sem="cc")
            s.emit()

        with contextlib.ExitStack() as _s7x:
                ec7 = _s7x.enter_context
                attn_all = ec7(nc.sbuf_tensor("attn_all", [128, NCC * T], BF16))
                mbufL = ec7(nc.sbuf_tensor("mbufL", [128, 2 * T], BF16))
                mbufR = ec7(nc.sbuf_tensor("mbufR", [128, 2 * T], BF16))
                ow_sb = ec7(nc.sbuf_tensor("ow_sb", [128, 2 * NCC * C], BF16))
                ob_sb = ec7(nc.sbuf_tensor("ob_sb", [128, 16], F32))
                o_ev2 = ec7(nc.sbuf_tensor("o_ev2", [128, 8 * 512], F32))
                wsb = ec7(nc.sbuf_tensor("wsb", [128, C], F32))
                fin = ec7(nc.sbuf_tensor("fin", [128, 2 * C], F32))
                psO2 = ec7(nc.psum_tensor("psO2", [128, 2 * 512], F32))
                psTk = ec7(nc.psum_tensor("psTk", [128, 2 * C], F32))
                _dummy = 1
                s = Section(nc, sems)
                ow_d0 = s.add("gpsimd", lambda e: e.dma_start(
                    out=ow_sb[:, 0:NCC * C].rearrange("p (ic m) -> p ic m", ic=NCC),
                    in_=owT0.rearrange("(ic p) m -> p ic m", p=128)),
                    deps=[snd3b2], dma=True, sem="w")
                ow_d1 = s.add("gpsimd", lambda e: e.dma_start(
                    out=ow_sb[:, NCC * C:2 * NCC * C].rearrange("p (ic m) -> p ic m", ic=NCC),
                    in_=owT1.rearrange("(ic p) m -> p ic m", p=128)),
                    deps=[snd3b2], dma=True, sem="w")
                ob_d = s.add("gpsimd", lambda e: e.dma_start(
                    out=ob_sb[:, :], in_=ob_cols), deps=[snd3b2], dma=True, sem="w")
                rel = s.add("sync", lambda e: e.dma_start(
                    out=XR.ap().rearrange("p (n c) -> p n c", c=C),
                    in_=r1_spill.rearrange("(n p) c -> p n c", p=128)),
                    deps=attn_ops, dma=True, sem="spl")
                mrg = []
                mrg_by = {}
                for idx, hh in enumerate((0, 1, 4, 5, 2, 3, 6, 7)):
                    r, j = hh // 4, hh % 4
                    pp = idx % 2
                    ms = "io" if pp == 0 else "io2"
                    if j < 2:
                        src_l = cc3_rcv_a.ap()[r][:, j * TFULL:j * TFULL + T]
                        src_r = cc3_rcv_a.ap()[r][:, j * TFULL + T:j * TFULL + 2 * T]
                        dep_l, dep_r = cc3a, cc3a
                    else:
                        src_l = cc3_rcv_b1.ap()[r][:, (j - 2) * T:(j - 1) * T]
                        src_r = cc3_rcv_b2.ap()[r][:, (j - 2) * T:(j - 1) * T]
                        dep_l, dep_r = cc3b1, cc3b2
                    prevm = mrg[idx - 2] if idx >= 2 else None
                    dl = s.add("sync", lambda e, src_l=src_l, pp=pp: e.dma_start(
                        out=mbufL[:, pp * T:(pp + 1) * T], in_=src_l),
                        deps=[dep_l, snd3b2, prevm], dma=True, sem=ms)
                    dr = s.add("sync", lambda e, src_r=src_r, pp=pp: e.dma_start(
                        out=mbufR[:, pp * T:(pp + 1) * T], in_=src_r),
                        deps=[dep_r, snd3b2, prevm], dma=True, sem=ms)
                    m1 = s.add("vector", lambda e, hh=hh, pp=pp: e.tensor_sub(
                        attn_all[:, hh * T:(hh + 1) * T],
                        mbufR[:, pp * T:(pp + 1) * T], mbufL[:, pp * T:(pp + 1) * T]),
                        deps=[dl, dr])
                    m2 = s.add("vector", lambda e, hh=hh, pp=pp: e.scalar_tensor_tensor(
                        attn_all[:, hh * T:(hh + 1) * T],
                        attn_all[:, hh * T:(hh + 1) * T],
                        vp[:, VP_HM:VP_HM + 1], mbufL[:, pp * T:(pp + 1) * T],
                        op0=ALU.mult, op1=ALU.add), deps=[m1, o_vp])
                    mrg.append(m2)
                    mrg_by[hh] = m2
                prev_ev = {}
                prev_f = {}
                last_round_trbs = []
                out_ops = []
                ko = 0
                for eo in range(2):
                    for qt2 in range(2):
                        ev_blocks = []
                        round_trbs = []
                        for oc in range(NCC):
                            sl = ko % 2

                            def em_omm(ics, first, last, e=None, eo=eo, qt2=qt2,
                                       oc=oc, sl=sl):
                                for k_, ic in enumerate(ics):
                                    ins = e.matmul(
                                        psO2[:, sl * 512:sl * 512 + 512],
                                        ow_sb[:, eo * NCC * C + ic * C + oc * 128:
                                              eo * NCC * C + ic * C + oc * 128 + 128],
                                        attn_all[:, ic * T + eo * C + qt2 * 512:
                                                 ic * T + eo * C + (qt2 + 1) * 512],
                                        start=(first and k_ == 0),
                                        stop=(last and k_ == len(ics) - 1))
                                return ins
                            ow_d = ow_d0 if eo == 0 else ow_d1
                            d = [mrg_by[0], mrg_by[1], mrg_by[4], mrg_by[5], ow_d]
                            if ko >= 2:
                                d.append(prev_ev[ko - 2])
                            mma = s.add("tensor", lambda e, f=em_omm: f(
                                (0, 1, 4, 5), True, False, e), deps=d)
                            mmb = s.add("tensor", lambda e, f=em_omm: f(
                                (2, 3, 6, 7), False, True, e),
                                deps=[mma, mrg_by[2], mrg_by[3], mrg_by[6], mrg_by[7]])
                            ev = s.add("vector", lambda e, oc=oc, eo=eo, sl=sl: e.tensor_scalar(
                                o_ev2[:, oc * 512:(oc + 1) * 512],
                                psO2[:, sl * 512:sl * 512 + 512],
                                ob_sb[:, eo * 8 + oc:eo * 8 + oc + 1], None, op0=ALU.add),
                                deps=[mmb, ob_d] + last_round_trbs)
                            prev_ev[ko] = ev
                            ev_blocks.append(ev)
                            ko += 1
                        for blk_ in range(4):
                            ti = eo * 8 + qt2 * 4 + blk_
                            slt = ti % 2

                            def em_trb(e, blk_=blk_, slt=slt):
                                for oc in range(NCC):
                                    ins = e.transpose(
                                        psTk[:, slt * C + oc * 128:slt * C + oc * 128 + 128],
                                        o_ev2[:, oc * 512 + blk_ * 128:oc * 512 + blk_ * 128 + 128],
                                        ident_f[:, :])
                                return ins
                            d = ev_blocks + [o_idf]
                            if ti >= 2:
                                d.append(prev_f[ti - 2])
                            trb = s.add("tensor", em_trb, deps=d)
                            round_trbs.append(trb)
                            wop = s.add("scalar", lambda e, ti=ti: e.activation(
                                wsb[:, :], XR[:, ti * C:(ti + 1) * C], AF.Identity,
                                bias=stat[:, ST_NMR + ti:ST_NMR + ti + 1],
                                scale=stat[:, ST_W1 + ti:ST_W1 + ti + 1]),
                                deps=[rel, oW, oN2, prev_f.get(ti - 1)])
                            fo = s.add("vector", lambda e, ti=ti, slt=slt: e.tensor_add(
                                fin[:, (ti % 2) * C:(ti % 2) * C + C],
                                wsb[:, :], psTk[:, slt * C:slt * C + C]),
                                deps=[wop, trb] +
                                ([out_ops[ti - 2]] if ti >= 2 else []))
                            prev_f[ti] = fo
                            oo = s.add("sync", lambda e, ti=ti: e.dma_start(
                                out=OUT[ti * 128:(ti + 1) * 128, :],
                                in_=fin[:, (ti % 2) * C:(ti % 2) * C + C]),
                                deps=[fo], dma=True, sem="io")
                            out_ops.append(oo)
                        last_round_trbs = round_trbs
                s.add("sync", lambda e: e.nop(), deps=out_ops)
                s.emit()
    return nc


def _tap_out(nc, sems, XR, OUT, deps, ntiles=NT):
    s = Section(nc, sems)
    outs = []
    for i in range(ntiles):
        outs.append(s.add("sync", lambda e, i=i: e.dma_start(
            out=OUT[i * 128:(i + 1) * 128, :], in_=XR[:, i * C:(i + 1) * C]),
            deps=deps, dma=True, sem="io"))
    s.add("sync", lambda e: e.nop(), deps=outs)
    s.emit()


_NC_CACHE = {}


def kernel(L0_attn_x, L0_attn_time_decay, L0_attn_time_first, L0_attn_time_mix,
           L0_attn_Wk, L0_attn_Wv, L0_attn_Wr, L0_attn_Wo,
           L0_ffn_router_prob, L0_ffn_norm_w, L0_ffn_norm_b,
           L0_ffn_q_w, L0_ffn_q_b, L0_ffn_k_w, L0_ffn_k_b,
           L0_ffn_v_w, L0_ffn_v_b, L0_ffn_o_w, L0_ffn_o_b):
    x = np.ascontiguousarray(np.asarray(L0_attn_x, dtype=np.float32))
    B, Tf, D = x.shape
    f32 = np.float32

    def packcols(v):
        return np.ascontiguousarray(np.asarray(v, f32).reshape(NCC, 128).T)

    wkT = np.ascontiguousarray(np.asarray(L0_attn_Wk, f32).T)
    wvT = np.ascontiguousarray(np.asarray(L0_attn_Wv, f32).T)
    wrT = np.ascontiguousarray(np.asarray(L0_attn_Wr, f32).T)
    woT = np.ascontiguousarray(np.asarray(L0_attn_Wo, f32).T)
    qwTf = np.asarray(L0_ffn_q_w, f32).T
    kwTf = np.asarray(L0_ffn_k_w, f32).T
    vwTf = np.asarray(L0_ffn_v_w, f32).T
    owTf = np.asarray(L0_ffn_o_w, f32).T

    key = (BUILD_UPTO, len(RG_GROUPS))
    if key not in _NC_CACHE:
        _NC_CACHE[key] = build_nc()
    nc = _NC_CACHE[key]

    in_maps = []
    for c in range(8):
        b, h = c // 2, c % 2
        xe = np.zeros((T + 1, C), f32)
        if h == 1:
            xe[0] = x[b, T - 1]
        xe[1:] = x[b, h * T:(h + 1) * T]
        ow0 = np.zeros((C, C), f32)
        ow1 = np.zeros((C, C), f32)
        m0 = C >> (2 * h)
        m1 = C >> (2 * h + 1)
        ow0[:m0, :m0] = owTf[:m0, :m0]
        ow1[:m1, :m1] = owTf[:m1, :m1]
        obc = np.zeros((2, C), f32)
        obc[0, :m0] = np.asarray(L0_ffn_o_b, f32)[:m0]
        obc[1, :m1] = np.asarray(L0_ffn_o_b, f32)[:m1]
        ob_cols = np.zeros((128, 16), f32)
        ob_cols[:, 0:8] = obc[0].reshape(NCC, 128).T
        ob_cols[:, 8:16] = obc[1].reshape(NCC, 128).T
        veca = np.zeros((128, 49), f32)
        veca[:, VP_TD:VP_TD + 8] = packcols(L0_attn_time_decay)
        veca[:, VP_TF:VP_TF + 8] = packcols(L0_attn_time_first)
        veca[:, VP_TM:VP_TM + 8] = packcols(L0_attn_time_mix)
        veca[:, VP_NW:VP_NW + 8] = packcols(L0_ffn_norm_w)
        veca[:, VP_NB:VP_NB + 8] = packcols(L0_ffn_norm_b)
        qb = np.asarray(L0_ffn_q_b, f32)[h * 512:(h + 1) * 512]
        kb = np.asarray(L0_ffn_k_b, f32)[h * 512:(h + 1) * 512]
        veca[:, VP_QB:VP_QB + 4] = qb.reshape(4, 128).T
        veca[:, VP_KB:VP_KB + 4] = kb.reshape(4, 128).T
        veca[:, VP_HM] = float(h)
        in_maps.append(dict(
            x_ext=xe,
            wkT=wkT, wvT=wvT, wrT=wrT, woT=woT,
            qwT=np.ascontiguousarray(qwTf[:, h * 512:(h + 1) * 512]),
            kwT=np.ascontiguousarray(kwTf[:, h * 512:(h + 1) * 512]),
            vwT=np.ascontiguousarray(vwTf[:, h * 512:(h + 1) * 512]),
            owT0=ow0, owT1=ow1,
            veca=veca,
            vb_row=np.ascontiguousarray(
                np.asarray(L0_ffn_v_b, f32)[h * 512:(h + 1) * 512].reshape(1, 512)),
            ob_cols=ob_cols,
        ))

    res = run_bass_kernel_spmd(nc, in_maps, core_ids=list(range(8)))
    out = np.empty((B, Tf, D), np.float32)
    for c in range(8):
        b, h = c // 2, c % 2
        out[b, h * T:(h + 1) * T] = res.results[c]["OUT"]
    return out


# version 47 (final)
# speedup vs baseline: 1.0474x; 1.0085x over previous
import contextlib
import numpy as np
import concourse.bass as bass
import concourse.mybir as mybir
from concourse.bass_utils import run_bass_kernel_spmd

F32 = mybir.dt.float32
BF16 = mybir.dt.bfloat16
F16 = mybir.dt.float16
AF = mybir.ActivationFunctionType
ALU = mybir.AluOpType
AX = mybir.AxisListType

T = 2048
TFULL = 4096
NT = 16
C = 1024
NCC = 8
TEC = 2052
EPS = 1e-5
K_CLAMP = 60.0
K_EPS = 1e-16

BUILD_UPTO = 99
S2_LEVEL = 4
RG_GROUPS = [[0, 1], [2, 3], [4, 5], [6, 7]]

ST_TM1, ST_E, ST_LAM, ST_FW, ST_NEGE, ST_N2047E = 0, 8, 16, 24, 32, 40
ST_MEAN1, ST_VAR1, ST_RSTD1, ST_NMR, ST_SY, ST_BY, ST_W1 = 48, 64, 80, 96, 112, 128, 144
ST_COLS = 160
VP_TD, VP_TF, VP_TM, VP_NW, VP_NB, VP_QB, VP_KB, VP_HM = 0, 8, 16, 24, 32, 40, 44, 48


class Op:
    __slots__ = ("eng", "sem", "emit", "deps", "inc_k", "val", "needed", "sec",
                 "async_cls", "dwaits")

    def __init__(self, eng, sem, emit, deps, inc_k, sec, async_cls):
        self.eng = eng
        self.sem = sem
        self.emit = emit
        self.deps = deps
        self.dwaits = {}
        self.inc_k = inc_k
        self.val = None
        self.needed = False
        self.sec = sec
        self.async_cls = async_cls


class Section:

    ENGS = ("tensor", "vector", "scalar", "gpsimd", "sync")
    COUNTER = [0]

    def __init__(self, nc, sems):
        self.nc = nc
        self.sems = sems
        self.ops = {e: [] for e in self.ENGS}
        Section.COUNTER[0] += 1
        self.sec_id = Section.COUNTER[0]

    def add(self, eng, emit, deps=(), dma=False, coll=False, sem=None):
        async_cls = dma or coll
        if async_cls:
            skey = sem or "io"
            inc_k = 16 if dma else 1
        else:
            skey = eng
            inc_k = 1
        op = Op(eng, skey, emit, [], inc_k, self.sec_id, async_cls)
        for d in deps:
            if d is None:
                continue
            if d.async_cls:
                op.dwaits[d.sem] = self.sems[d.sem].n
            elif d.sec == self.sec_id:
                d.needed = True
                op.deps.append(d)
        if async_cls:
            if self.sems[skey].n > 0:
                op.dwaits[skey] = max(op.dwaits.get(skey, 0), self.sems[skey].n)
            self.sems[skey].n += inc_k
            op.val = self.sems[skey].n
        self.ops[eng].append(op)
        return op

    def emit(self):
        for e in self.ENGS:
            for op in self.ops[e]:
                if not op.async_cls and op.needed:
                    self.sems[e].n += 1
                    op.val = self.sems[e].n
        with self.nc.Block() as blk:
            for ename in self.ENGS:
                if not self.ops[ename]:
                    continue

                def make_body(ename):
                    def body(eng):
                        for op in self.ops[ename]:
                            waits = dict(op.dwaits)
                            for d in op.deps:
                                if d.val is None:
                                    raise RuntimeError("dep without inc")
                                waits[d.sem] = max(waits.get(d.sem, 0), d.val)
                            for de, dv in waits.items():
                                if dv > 0:
                                    eng.wait_ge(self.sems[de].h, dv)
                            ins = op.emit(eng)
                            if op.val is not None:
                                ins.then_inc(self.sems[op.sem].h, op.inc_k)
                    return body

                getattr(blk, ename)(make_body(ename))


class Sem:
    __slots__ = ("h", "n")

    def __init__(self, h):
        self.h = h
        self.n = 0


def build_nc():
    nc = bass.Bass("TRN2", target_bir_lowering=False)

    x_ext = nc.dram_tensor("x_ext", [T + 1, C], F32, kind="ExternalInput").ap()
    wkT = nc.dram_tensor("wkT", [C, C], F32, kind="ExternalInput").ap()
    wvT = nc.dram_tensor("wvT", [C, C], F32, kind="ExternalInput").ap()
    wrT = nc.dram_tensor("wrT", [C, C], F32, kind="ExternalInput").ap()
    woT = nc.dram_tensor("woT", [C, C], F32, kind="ExternalInput").ap()
    qwT = nc.dram_tensor("qwT", [C, 512], F32, kind="ExternalInput").ap()
    kwT = nc.dram_tensor("kwT", [C, 512], F32, kind="ExternalInput").ap()
    vwT = nc.dram_tensor("vwT", [C, 512], F32, kind="ExternalInput").ap()
    owT0 = nc.dram_tensor("owT0", [C, C], F32, kind="ExternalInput").ap()
    owT1 = nc.dram_tensor("owT1", [C, C], F32, kind="ExternalInput").ap()
    veca = nc.dram_tensor("veca", [128, 49], F32, kind="ExternalInput").ap()
    vb_row = nc.dram_tensor("vb_row", [1, 512], F32, kind="ExternalInput").ap()
    ob_cols = nc.dram_tensor("ob_cols", [128, 16], F32, kind="ExternalInput").ap()
    OUT = nc.dram_tensor("OUT", [T, C], F32, kind="ExternalOutput").ap()

    r1_spill = nc.dram_tensor("r1_spill", [T, C], F32).ap()
    bnd_dram = nc.dram_tensor("bnd_dram", [C], BF16).ap()
    cc1_snd = [nc.dram_tensor(f"cc1s{i}", [128, 2], F32) for i in range(NCC)]
    cc1_rcv = [nc.dram_tensor(f"cc1r{i}", [2, 128, 2], F32) for i in range(NCC)]
    cc2_snd = [nc.dram_tensor(f"cc2s{a}", [128, NCC * (T // 2)], BF16)
               for a in range(2)]
    cc2_rcv = [nc.dram_tensor(f"cc2r{a}", [2, 128, NCC * (T // 2)], BF16)
               for a in range(2)]
    cc3_snd_a = nc.dram_tensor("cc3sa", [128, 2 * TFULL], BF16)
    cc3_rcv_a = nc.dram_tensor("cc3ra", [2, 128, 2 * TFULL], BF16)
    cc3_snd_b1 = nc.dram_tensor("cc3sb1", [128, 2 * T], BF16)
    cc3_rcv_b1 = nc.dram_tensor("cc3rb1", [2, 128, 2 * T], BF16)
    cc3_snd_b2 = nc.dram_tensor("cc3sb2", [128, 2 * T], BF16)
    cc3_rcv_b2 = nc.dram_tensor("cc3rb2", [2, 128, 2 * T], BF16)
    rr_dram = [nc.dram_tensor(f"rrd{h}", [1, 512], F32) for h in range(2)]
    RG = RG_GROUPS

    with contextlib.ExitStack() as _top:
        ec = _top.enter_context
        XR = ec(nc.sbuf_tensor("XR", [128, NT * C], F32))
        vp = ec(nc.sbuf_tensor("vp", [128, 52], F32))
        stat = ec(nc.sbuf_tensor("stat", [128, ST_COLS], F32))
        ident_b = ec(nc.sbuf_tensor("ident_b", [128, 128], BF16))
        ident_f = ec(nc.sbuf_tensor("ident_f", [128, 128], F32))
        ones_r1 = ec(nc.sbuf_tensor("ones_r1", [1, 128], BF16))
        ones_r1f = ec(nc.sbuf_tensor("ones_r1f", [1, 128], F32))
        ones_c1 = ec(nc.sbuf_tensor("ones_c1", [128, 1], BF16))
        ones128 = ec(nc.sbuf_tensor("ones128", [128, 128], BF16))
        carry = ec(nc.sbuf_tensor("carry", [128, 24], F32))
        st6 = ec(nc.sbuf_tensor("st6", [128, 12], F32))
        mv = ec(nc.sbuf_tensor("mv", [128, 4], F32))
        cbias = ec(nc.sbuf_tensor("cbias", [128, 2], F32))
        cst = ec(nc.sbuf_tensor("cst", [128, 1], F32))
        nc.const_aps.register(F32, EPS, cst[:, 0:1])
        sems = {n: Sem(ec(nc.semaphore("q_" + n)))
                for n in ("tensor", "vector", "scalar", "gpsimd", "sync",
                          "ld", "ld2", "w", "io", "io2", "cc", "cc1d", "spl", "rb",
                          "x0", "x1", "x2", "x3", "x4", "x5", "x6", "x7")}

        XRb = XR.ap().bitcast(BF16)

        def QT(j):
            return XRb[:, j * TFULL:(j + 1) * TFULL]

        def KT(j):
            return XRb[:, 16384 + j * TFULL:16384 + (j + 1) * TFULL]

        s = Section(nc, sems)
        o_m1 = s.add("gpsimd", lambda e: e.memset(ones_r1[:, :], 1.0))
        o_m2 = s.add("gpsimd", lambda e: e.memset(ones_r1f[:, :], 1.0))
        o_m3 = s.add("gpsimd", lambda e: e.memset(ones_c1[:, :], 1.0))
        o_m4 = s.add("gpsimd", lambda e: e.memset(ones128[:, :], 1.0))
        o_m5 = s.add("gpsimd", lambda e: e.memset(carry[:, :], 0.0))
        o_m6 = s.add("gpsimd", lambda e: e.memset(cst[:, :], EPS))
        o_vp = s.add("sync", lambda e: e.dma_start(out=vp[:, 0:49], in_=veca), dma=True, sem="ld")
        xload = []
        for i in range(NT):
            xload.append(s.add("sync", lambda e, i=i: e.dma_start(
                out=XR[:, i * C:(i + 1) * C],
                in_=x_ext[1 + i * 128:1 + (i + 1) * 128, :]), dma=True,
                sem="ld" if i % 2 == 0 else "ld2"))
        o_id = s.add("gpsimd", lambda e: e.affine_select(
            ident_b[:, :], ones128[:, :], [[1, 128]],
            compare_op=ALU.is_equal, fill=0.0, base=0, channel_multiplier=-1),
            deps=[o_m4])
        o_idf = s.add("vector", lambda e: e.tensor_copy(ident_f[:, :], ident_b[:, :]),
                      deps=[o_id])
        o_tm1 = s.add("vector", lambda e: e.tensor_scalar(
            stat[:, ST_TM1:ST_TM1 + 8], vp[:, VP_TM:VP_TM + 8], -1.0, 1.0,
            op0=ALU.mult, op1=ALU.add), deps=[o_vp])
        o_e = s.add("scalar", lambda e: e.activation(
            stat[:, ST_E:ST_E + 8], vp[:, VP_TD:VP_TD + 8], AF.Exp), deps=[o_vp])
        o_lam = s.add("scalar", lambda e: e.activation(
            stat[:, ST_LAM:ST_LAM + 8], stat[:, ST_E:ST_E + 8], AF.Exp, scale=-1.0),
            deps=[o_e])
        o_fw = s.add("scalar", lambda e: e.activation(
            stat[:, ST_FW:ST_FW + 8], vp[:, VP_TF:VP_TF + 8], AF.Exp), deps=[o_vp])
        o_nege = s.add("vector", lambda e: e.tensor_scalar_mul(
            stat[:, ST_NEGE:ST_NEGE + 8], stat[:, ST_E:ST_E + 8], -1.0), deps=[o_e])
        o_n2047 = s.add("vector", lambda e: e.tensor_scalar_mul(
            stat[:, ST_N2047E:ST_N2047E + 8], stat[:, ST_E:ST_E + 8], -2047.0),
            deps=[o_e])

        last_coll = [None]
        with contextlib.ExitStack() as _sx0:
            xmT = _sx0.enter_context(nc.sbuf_tensor("xmT", [128, NCC * T], BF16))
            rwkvT = _sx0.enter_context(nc.sbuf_tensor("rwkvT", [128, NCC * T], BF16))
            with contextlib.ExitStack() as _s1x:
                ec1 = _s1x.enter_context
                xnT = ec1(nc.sbuf_tensor("xnT", [128, NCC * TEC], BF16))
                xn_tile = ec1(nc.sbuf_tensor("xn_tile", [128, 2 * C], BF16))
                xb_row = ec1(nc.sbuf_tensor("xb_row", [1, C], F32))
                xb_st = ec1(nc.sbuf_tensor("xb_st", [1, 12], F32))
                xb_mv = ec1(nc.sbuf_tensor("xb_mv", [1, 6], F32))
                xb_bf = ec1(nc.sbuf_tensor("xb_bf", [1, C], BF16))
                ptr = ec1(nc.psum_tensor("ptr", [128, 2 * C], BF16))
                _dummy = 1
                stats_done = []
                prev = None
                for i in range(NT):
                    d = [xload[i]] + ([prev] if prev else [])
                    a = s.add("vector", lambda e, i=i: e.bn_stats(
                        st6[:, 0:6], XR[:, i * C:i * C + 512]), deps=d)
                    b = s.add("vector", lambda e, i=i: e.bn_stats(
                        st6[:, 6:12], XR[:, i * C + 512:i * C + 1024]), deps=[a])
                    g = s.add("vector", lambda e: e.bn_aggr(mv[:, 0:2], st6[:, :]),
                              deps=[a, b])
                    cm = s.add("vector", lambda e, i=i: e.tensor_copy(
                        stat[:, ST_MEAN1 + i:ST_MEAN1 + i + 1], mv[:, 0:1]), deps=[g])
                    cv = s.add("vector", lambda e, i=i: e.tensor_copy(
                        stat[:, ST_VAR1 + i:ST_VAR1 + i + 1], mv[:, 1:2]), deps=[g])
                    prev = cv
                    stats_done.extend([cm, cv])
                o_ln = s.add("scalar", lambda e: e.activation(
                    stat[:, ST_RSTD1:ST_RSTD1 + 16], stat[:, ST_VAR1:ST_VAR1 + 16],
                    AF.Ln, bias=EPS), deps=stats_done)
                o_rstd = s.add("scalar", lambda e: e.activation(
                    stat[:, ST_RSTD1:ST_RSTD1 + 16], stat[:, ST_RSTD1:ST_RSTD1 + 16],
                    AF.Exp, scale=-0.5), deps=[o_ln])
                o_nmr0 = s.add("vector", lambda e: e.tensor_mul(
                    stat[:, ST_NMR:ST_NMR + 16], stat[:, ST_MEAN1:ST_MEAN1 + 16],
                    stat[:, ST_RSTD1:ST_RSTD1 + 16]), deps=[o_rstd])
                o_nmr = s.add("vector", lambda e: e.tensor_scalar_mul(
                    stat[:, ST_NMR:ST_NMR + 16], stat[:, ST_NMR:ST_NMR + 16], -1.0),
                    deps=[o_nmr0])
                o_xb = s.add("sync", lambda e: e.dma_start(
                    out=xb_row[:, :], in_=x_ext[0:1, :]), dma=True, sem="ld")
                a = s.add("vector", lambda e: e.bn_stats(xb_st[:, 0:6], xb_row[:, 0:512]),
                          deps=[o_xb])
                b = s.add("vector", lambda e: e.bn_stats(xb_st[:, 6:12], xb_row[:, 512:1024]),
                          deps=[a])
                g = s.add("vector", lambda e: e.bn_aggr(xb_mv[:, 0:2], xb_st[:, :]),
                          deps=[a, b])
                l1 = s.add("scalar", lambda e: e.activation(
                    xb_mv[:, 2:3], xb_mv[:, 1:2], AF.Ln, bias=EPS), deps=[g])
                l2 = s.add("scalar", lambda e: e.activation(
                    xb_mv[:, 3:4], xb_mv[:, 2:3], AF.Exp, scale=-0.5), deps=[l1])
                l3 = s.add("vector", lambda e: e.tensor_mul(
                    xb_mv[:, 4:5], xb_mv[:, 0:1], xb_mv[:, 3:4]), deps=[l2, g])
                l4 = s.add("vector", lambda e: e.tensor_scalar_mul(
                    xb_mv[:, 4:5], xb_mv[:, 4:5], -1.0), deps=[l3])
                l5 = s.add("scalar", lambda e: e.activation(
                    xb_bf[:, :], xb_row[:, :], AF.Identity,
                    bias=xb_mv[:, 4:5], scale=xb_mv[:, 3:4]), deps=[l4])
                l6 = s.add("sync", lambda e: e.dma_start(
                    out=bnd_dram.rearrange("(o c) -> o c", o=1), in_=xb_bf[0:1, :]),
                    deps=[l5], dma=True, sem="ld")
                l7s = []
                for cc in range(NCC):
                    l7s.append(s.add("sync", lambda e, cc=cc: e.dma_start(
                        out=xnT[:, cc * TEC:cc * TEC + 1],
                        in_=bnd_dram[cc * 128:(cc + 1) * 128].rearrange("(p o) -> p o", o=1)),
                        deps=[l6], dma=True, sem="ld"))
                l7 = l7s[-1]
                ap_ops = [None] * NT
                tr_ops = [None] * NT
                cp_ops = [None] * NT
                for i in range(NT):
                    d = [o_rstd, o_nmr, xload[i]]
                    if i >= 2:
                        d.append(tr_ops[i - 2])
                    ap_ops[i] = s.add("scalar", lambda e, i=i: e.activation(
                        xn_tile[:, (i % 2) * C:(i % 2) * C + C],
                        XR[:, i * C:(i + 1) * C], AF.Identity,
                        bias=stat[:, ST_NMR + i:ST_NMR + i + 1],
                        scale=stat[:, ST_RSTD1 + i:ST_RSTD1 + i + 1]), deps=d)

                    def em_tr(e, i=i):
                        for cc in range(NCC):
                            ins = e.transpose(
                                ptr[:, (i % 2) * C + cc * 128:(i % 2) * C + cc * 128 + 128],
                                xn_tile[:, (i % 2) * C + cc * 128:(i % 2) * C + cc * 128 + 128],
                                ident_b[:, :])
                        return ins
                    d = [ap_ops[i], o_id]
                    if i >= 2:
                        d.append(cp_ops[i - 2])
                    tr_ops[i] = s.add("tensor", em_tr, deps=d)
                    cp_ops[i] = s.add("vector", lambda e, i=i: e.tensor_copy(
                        xnT.ap().rearrange("p (cc t) -> p cc t", cc=NCC)[:, :, 1 + i * 128:1 + (i + 1) * 128],
                        ptr.ap().rearrange("p (b cc t) -> p b cc t", b=2, cc=NCC)[:, i % 2]),
                        deps=[tr_ops[i]])
                mix1 = []
                for cc in range(NCC):
                    mix1.append(s.add("vector", lambda e, cc=cc: e.tensor_scalar_mul(
                        xmT[:, cc * T:(cc + 1) * T],
                        xnT[:, cc * TEC:cc * TEC + T],
                        stat[:, ST_TM1 + cc:ST_TM1 + cc + 1]),
                        deps=[o_tm1] + l7s + cp_ops))
                mix_done = []
                for cc in range(NCC):
                    mix_done.append(s.add("vector", lambda e, cc=cc: e.scalar_tensor_tensor(
                        xmT[:, cc * T:(cc + 1) * T],
                        xnT[:, cc * TEC + 1:cc * TEC + 1 + T],
                        vp[:, VP_TM + cc:VP_TM + cc + 1],
                        xmT[:, cc * T:(cc + 1) * T],
                        op0=ALU.mult, op1=ALU.add), deps=[mix1[cc]]))
                s.emit()

            if BUILD_UPTO == 12:
                _tap_out(nc, sems, XR, OUT, [])
                return nc
            with contextlib.ExitStack() as _s2x:
                ec2 = _s2x.enter_context
                wbuf = ec2(nc.sbuf_tensor("wbuf", [128, 2 * 2 * C], BF16))
                sl1 = ec2(nc.sbuf_tensor("sl1", [128, 2 * T], F32))
                sl2 = ec2(nc.sbuf_tensor("sl2", [128, 2 * T], F32))
                sl3 = ec2(nc.sbuf_tensor("sl3", [128, 2 * T], F32))
                sl6 = ec2(nc.sbuf_tensor("sl6", [128, 2 * T], F32))
                lamb = ec2(nc.sbuf_tensor("lamb", [128, T], BF16))
                psA = ec2(nc.psum_tensor("psA", [128, 2048], F32))
                psB = ec2(nc.psum_tensor("psB", [128, 2048], F32))
                _dummy = 1
                s = Section(nc, sems)
                rwkv_ops = [None] * NCC
                prev = {}
                E60 = float(np.exp(60.0))

                def emit_cc(oc):
                    if ("ccargs", oc) not in prev:
                        return
                    (snd, CM, xs) = prev.pop(("ccargs", oc))
                    cc = s.add("gpsimd", lambda e, oc=oc: e.collective_compute(
                        "AllGather", ALU.bypass, replica_groups=RG,
                        ins=[cc1_snd[oc].ap().opt()],
                        outs=[cc1_rcv[oc].ap().opt()]),
                        deps=[snd], coll=True, sem=xs)
                    last_coll[0] = cc
                    rcv = s.add("sync", lambda e, oc=oc, CM=CM: e.dma_start(
                        out=carry[:, CM:CM + 2], in_=cc1_rcv[oc].ap()[0]),
                        deps=[cc, prev.get(("cmask", oc - 2))], dma=True, sem=xs)
                    cmask = s.add("vector", lambda e, CM=CM: e.tensor_scalar_mul(
                        carry[:, CM + 2:CM + 4], carry[:, CM:CM + 2],
                        vp[:, VP_HM:VP_HM + 1]),
                        deps=[rcv, o_vp])
                    prev[("cmask", oc)] = cmask

                def emit_tail(oc):
                    if ("tailargs", oc) not in prev:
                        return
                    (s1, s2, s3, s6, lam0, CM,
                     w1b, w2e, er, ck) = prev.pop(("tailargs", oc))
                    cmask = prev[("cmask", oc)]
                    c1 = s.add("vector", lambda e, s2=s2, CM=CM:
                               e.scalar_tensor_tensor(
                                   s2, lamb[:, :], carry[:, CM + 2:CM + 3], s2,
                                   op0=ALU.mult, op1=ALU.add),
                               deps=[w1b, lam0, cmask])
                    c2 = s.add("vector", lambda e, s1=s1, CM=CM:
                               e.scalar_tensor_tensor(
                                   s1, lamb[:, :], carry[:, CM + 3:CM + 4], s1,
                                   op0=ALU.mult, op1=ALU.add),
                               deps=[w2e, lam0, cmask, c1])
                    prev[("lamused", oc)] = c2
                    dd = s.add("vector", lambda e, s1=s1, s3=s3, s6=s6:
                               e.scalar_tensor_tensor(
                                   s3, s6, 1.0, s1, op0=ALU.add, op1=ALU.mult),
                               deps=[er, c2])
                    prev[("dd", oc)] = dd
                    prev[("t2", oc)] = (s2, s3, c1, ck, dd)

                def emit_tail2a(oc):
                    if ("t2", oc) not in prev:
                        return
                    (s2, s3, c1, ck, dd) = prev[("t2", oc)]
                    lnd = s.add("scalar", lambda e, s3=s3: e.activation(s3, s3, AF.Ln),
                                deps=[dd])
                    invd = s.add("scalar", lambda e, s3=s3: e.activation(
                        s3, s3, AF.Exp, scale=-1.0), deps=[lnd])
                    prev[("invd", oc)] = invd

                def emit_tail2b(oc):
                    if ("t2", oc) not in prev:
                        return
                    (s2, s3, c1, ck, dd) = prev.pop(("t2", oc))
                    invd = prev.pop(("invd", oc))
                    rw = s.add("vector", lambda e, s2=s2, s3=s3, oc=oc: e.tensor_mul(
                        rwkvT[:, oc * T:(oc + 1) * T], s2, s3),
                        deps=[invd, c1, ck])
                    prev[("rw", oc)] = rw
                    rwkv_ops[oc] = rw

                for oc in range(NCC):
                    par = oc % 2
                    s1 = sl1[:, par * T:par * T + T]
                    s2 = sl2[:, par * T:par * T + T]
                    s3 = sl3[:, par * T:par * T + T]
                    s6 = sl6[:, par * T:par * T + T]
                    trash = rwkvT[:, oc * T:(oc + 1) * T]
                    ecol = stat[:, ST_E + oc:ST_E + oc + 1]
                    negecol = stat[:, ST_NEGE + oc:ST_NEGE + oc + 1]
                    n2047col = stat[:, ST_N2047E + oc:ST_N2047E + oc + 1]
                    fwcol = stat[:, ST_FW + oc:ST_FW + oc + 1]
                    lamcol = stat[:, ST_LAM + oc:ST_LAM + oc + 1]
                    base = (oc % 2) * 2 * C
                    CM = 16 + 4 * par

                    wdeps = [prev.get(("rmm", oc - 2))] if oc >= 2 else []
                    wk_d = s.add("gpsimd", lambda e, oc=oc, base=base: e.dma_start(
                        out=wbuf[:, base:base + C].rearrange("p (ic m) -> p ic m", ic=NCC),
                        in_=wkT[:, oc * 128:(oc + 1) * 128]
                        .rearrange("(ic p) m -> p ic m", p=128)), deps=wdeps, dma=True, sem="w")
                    wv_d = s.add("gpsimd", lambda e, oc=oc, base=base: e.dma_start(
                        out=wbuf[:, base + C:base + 2 * C].rearrange("p (ic m) -> p ic m", ic=NCC),
                        in_=wvT[:, oc * 128:(oc + 1) * 128]
                        .rearrange("(ic p) m -> p ic m", p=128)), deps=wdeps, dma=True, sem="w")
                    o_iota = s.add("gpsimd", lambda e, s3=s3: e.iota(
                        s3, [[1, T]], base=0, channel_multiplier=0,
                        allow_small_or_imprecise_dtypes=True),
                        deps=[prev.get(("rw", oc - 2))])


                    def emit_mms_h(e, ps, woff, h, base=base, po=0):
                        for qt in range(2 * h, 2 * h + 2):
                            for ic in range(NCC):
                                ins = e.matmul(
                                    ps[:, po + (qt - 2 * h) * 512:po + (qt - 2 * h + 1) * 512],
                                    wbuf[:, base + woff + ic * 128:base + woff + ic * 128 + 128],
                                    xmT[:, ic * T + qt * 512:ic * T + (qt + 1) * 512],
                                    start=(ic == 0), stop=(ic == NCC - 1))
                        return ins

                    kmm_h = [None, None]
                    vmm_h = [None, None]
                    rmm_h = [None, None]
                    ek_h = [None, None]
                    vev_h = [None, None]
                    er_h = [None, None]
                    for h in (0, 1):
                        d = [wk_d] + mix_done
                        pk = prev.get(("ek", oc - 1, h))
                        if pk is not None:
                            d.append(pk)
                        kmm_h[h] = s.add("tensor",
                                         lambda e, h=h, f=emit_mms_h:
                                         f(e, psA, 0, h, po=h * 1024),
                                         deps=d)
                        ek_h[h] = s.add("scalar", lambda e, s1=s1, h=h: e.activation(
                            s1[:, h * 1024:(h + 1) * 1024],
                            psA[:, h * 1024:(h + 1) * 1024], AF.Exp),
                            deps=[kmm_h[h], prev.get(("dd", oc - 2))])
                        prev[("ek", oc, h)] = ek_h[h]
                    wr_d = s.add("gpsimd", lambda e, oc=oc, base=base: e.dma_start(
                        out=wbuf[:, base:base + C].rearrange("p (ic m) -> p ic m", ic=NCC),
                        in_=wrT[:, oc * 128:(oc + 1) * 128]
                        .rearrange("(ic p) m -> p ic m", p=128)),
                        deps=[kmm_h[1]], dma=True, sem="w")
                    emit_cc(oc - 1)
                    emit_tail(oc - 1)
                    for h in (0, 1):
                        d = [wv_d]
                        pv = prev.get(("er", oc - 1, h))
                        if pv is not None:
                            d.append(pv)
                        vmm_h[h] = s.add("tensor",
                                         lambda e, h=h, f=emit_mms_h:
                                         f(e, psB, C, h, po=h * 1024),
                                         deps=d)
                        vev_h[h] = s.add("scalar", lambda e, s2=s2, h=h: e.activation(
                            s2[:, h * 1024:(h + 1) * 1024],
                            psB[:, h * 1024:(h + 1) * 1024], AF.Identity),
                            deps=[vmm_h[h], prev.get(("rw", oc - 2))])
                        prev[("vev", oc, h)] = vev_h[h]
                    for h in (0, 1):
                        d = [wr_d, vev_h[h]]
                        rmm_h[h] = s.add("tensor",
                                         lambda e, h=h, f=emit_mms_h:
                                         f(e, psB, 0, h, po=h * 1024),
                                         deps=d)
                        er_h[h] = s.add("scalar", lambda e, s6=s6, h=h: e.activation(
                            s6[:, h * 1024:(h + 1) * 1024],
                            psB[:, h * 1024:(h + 1) * 1024], AF.Exp, scale=-1.0),
                            deps=[rmm_h[h], prev.get(("dd", oc - 2))])
                        prev[("er", oc, h)] = er_h[h]
                    ek = ek_h[1]
                    vev = vev_h[1]
                    er = er_h[1]
                    prev[("er", oc)] = er
                    prev[("vev", oc)] = vev
                    prev[("rmm", oc)] = rmm_h[1]
                    lam0 = s.add("scalar", lambda e, negecol=negecol, s3=s3: e.activation(
                        lamb[:, :], s3, AF.Exp, scale=negecol),
                        deps=[o_iota, o_nege, prev.get(("lamused", oc - 1))])
                    emit_tail2a(oc - 1)
                    clp = s.add("vector", lambda e, s1=s1: e.tensor_scalar_min(s1, s1, E60),
                                deps=[ek, prev.get(("dd", oc - 2))])
                    kv = s.add("vector", lambda e, s1=s1, s2=s2: e.tensor_mul(s2, s1, s2),
                               deps=[clp, vev])
                    emit_tail2b(oc - 1)
                    lrev = s.add("scalar", lambda e, s3=s3, ecol=ecol, n2047col=n2047col:
                                 e.activation(s3, s3, AF.Exp,
                                              scale=ecol, bias=n2047col),
                                 deps=[o_iota, o_e, o_n2047, lam0])
                    ckv = s.add("vector", lambda e, s2=s2, s3=s3, trash=trash, oc=oc:
                                e.scalar_tensor_tensor(
                                    trash, s2, 1.0, s3, op0=ALU.mult, op1=ALU.mult,
                                    accum_out=carry[:, 2 * oc:2 * oc + 1]),
                                deps=[kv, lrev, o_m5])
                    ck = s.add("vector", lambda e, s1=s1, s3=s3, trash=trash, oc=oc:
                               e.scalar_tensor_tensor(
                                   trash, s1, 1.0, s3, op0=ALU.mult, op1=ALU.mult,
                                   accum_out=carry[:, 2 * oc + 1:2 * oc + 2]),
                               deps=[ckv])
                    xs = f"x{oc}"
                    snd = s.add("sync", lambda e, oc=oc: e.dma_start(
                        out=cc1_snd[oc].ap(), in_=carry[:, 2 * oc:2 * oc + 2]),
                        deps=[ck], dma=True, sem=xs)
                    prev[("ccargs", oc)] = (snd, CM, xs)
                    sckv = s.add("vector", lambda e, s2=s2, s3=s3, lamcol=lamcol:
                                 e.tensor_tensor_scan(s3, lamcol.broadcast_to([128, T]),
                                                      s2, 0.0, op0=ALU.mult, op1=ALU.add),
                                 deps=[kv, ckv, ck, o_lam, lrev])
                    w1 = s.add("vector", lambda e, s2=s2, s3=s3, fwcol=fwcol:
                               e.scalar_tensor_tensor(
                                   s2[:, 1:T], s2[:, 1:T], fwcol, s3[:, 0:T - 1],
                                   op0=ALU.mult, op1=ALU.add), deps=[sckv, ck])
                    w1b = s.add("vector", lambda e, s2=s2, fwcol=fwcol:
                                e.tensor_scalar_mul(s2[:, 0:1], s2[:, 0:1], fwcol),
                                deps=[w1])
                    sck = s.add("vector", lambda e, s1=s1, s3=s3, lamcol=lamcol:
                                e.tensor_tensor_scan(s3, lamcol.broadcast_to([128, T]),
                                                     s1, 0.0, op0=ALU.mult, op1=ALU.add),
                                deps=[w1, ck])
                    w2 = s.add("vector", lambda e, s1=s1, s3=s3, fwcol=fwcol:
                               e.scalar_tensor_tensor(
                                   s1[:, 1:T], s1[:, 1:T], fwcol, s3[:, 0:T - 1],
                                   op0=ALU.mult, op1=ALU.add), deps=[sck])
                    w2b = s.add("vector", lambda e, s1=s1, fwcol=fwcol:
                                e.tensor_scalar_mul(s1[:, 0:1], s1[:, 0:1], fwcol),
                                deps=[w2])
                    w2e = s.add("vector", lambda e, s1=s1: e.tensor_scalar_add(
                        s1, s1, K_EPS), deps=[w2b])
                    prev[("tailargs", oc)] = (s1, s2, s3, s6, lam0, CM,
                                              w1b, w2e, er, ck)
                emit_cc(NCC - 1)
                emit_tail(NCC - 1)
                emit_tail2a(NCC - 1)
                emit_tail2b(NCC - 1)
                s.emit()

            if BUILD_UPTO == 13:
                _tap_out(nc, sems, XR, OUT, rwkv_ops)
                return nc
            with contextlib.ExitStack() as _s3x:
                ec3 = _s3x.enter_context
                wo_sb = ec3(nc.sbuf_tensor("wo_sb", [128, NCC * C], BF16))
                o_ev = ec3(nc.sbuf_tensor("o_ev", [128, 2 * 512], F32))
                psO = ec3(nc.psum_tensor("psO", [128, 2 * 512], F32))
                psT = ec3(nc.psum_tensor("psT", [128, 2 * 512], F32))
                _dummy = 1
                s = Section(nc, sems)
                wo_d = s.add("gpsimd", lambda e: e.dma_start(
                    out=wo_sb.ap().rearrange("p (ic m) -> p ic m", ic=NCC),
                    in_=woT.rearrange("(ic p) m -> p ic m", p=128)), dma=True, sem="w")
                r1_ops = []
                prev_ev = {}
                prev_tr = {}
                prev_add = {}
                k = 0
                for oc in range(NCC):
                    for qt in range(4):
                        sl = k % 2

                        def em_owmm(e, oc=oc, qt=qt, sl=sl):
                            for ic in range(NCC):
                                ins = e.matmul(
                                    psO[:, sl * 512:sl * 512 + 512],
                                    wo_sb[:, ic * C + oc * 128:ic * C + oc * 128 + 128],
                                    rwkvT[:, ic * T + qt * 512:ic * T + (qt + 1) * 512],
                                    start=(ic == 0), stop=(ic == NCC - 1))
                            return ins
                        d = [wo_d] + rwkv_ops
                        if k >= 2:
                            d.append(prev_ev[k - 2])
                        mm = s.add("tensor", em_owmm, deps=d)
                        ev = s.add("vector", lambda e, sl=sl: e.tensor_copy(
                            o_ev[:, sl * 512:sl * 512 + 512], psO[:, sl * 512:sl * 512 + 512]),
                            deps=[mm] + ([prev_tr[k - 2]] if k >= 2 else []))
                        prev_ev[k] = ev

                        def em_tr(e, oc=oc, qt=qt, sl=sl):
                            for j in range(4):
                                ins = e.transpose(
                                    psT[:, sl * 512 + j * 128:sl * 512 + j * 128 + 128],
                                    o_ev[:, sl * 512 + j * 128:sl * 512 + j * 128 + 128],
                                    ident_f[:, :])
                            return ins
                        tr = s.add("tensor", em_tr,
                                   deps=[ev, o_idf] + ([prev_add[k - 2]] if k >= 2 else []))
                        prev_tr[k] = tr

                        def em_add(e, oc=oc, qt=qt, sl=sl):
                            return e.tensor_add(
                                XR.ap().rearrange("p (n c) -> p n c", n=NT)[
                                    :, 4 * qt:4 * qt + 4, oc * 128:oc * 128 + 128],
                                XR.ap().rearrange("p (n c) -> p n c", n=NT)[
                                    :, 4 * qt:4 * qt + 4, oc * 128:oc * 128 + 128],
                                psT.ap().rearrange("p (b j t) -> p (b j) t", b=2, j=4)[
                                    :, sl * 4:sl * 4 + 4, :])
                        ad = s.add("vector", em_add, deps=[tr])
                        prev_add[k] = ad
                        r1_ops.append(ad)
                        k += 1

                xn2o = ec3(nc.sbuf_tensor("xn2o", [128, NCC * T], BF16))
                x2_tile = ec3(nc.sbuf_tensor("x2_tile", [128, 2 * C], BF16))
                ptr2 = ec3(nc.psum_tensor("ptr2", [128, 2 * C], BF16))

                stats2 = []
                prevv = None
                for i in range(NT):
                    d = [r1_ops[7 * 4 + i // 4]] + ([prevv] if prevv else [])
                    a = s.add("vector", lambda e, i=i: e.bn_stats(
                        st6[:, 0:6], XR[:, i * C:i * C + 512]), deps=d)
                    b = s.add("vector", lambda e, i=i: e.bn_stats(
                        st6[:, 6:12], XR[:, i * C + 512:i * C + 1024]), deps=[a])
                    g = s.add("vector", lambda e: e.bn_aggr(mv[:, 0:2], st6[:, :]), deps=[a, b])
                    cm = s.add("vector", lambda e, i=i: e.tensor_copy(
                        stat[:, ST_MEAN1 + i:ST_MEAN1 + i + 1], mv[:, 0:1]), deps=[g])
                    cv = s.add("vector", lambda e, i=i: e.tensor_copy(
                        stat[:, ST_VAR1 + i:ST_VAR1 + i + 1], mv[:, 1:2]), deps=[g])
                    prevv = cv
                    stats2.extend([cm, cv])
                LNU, IU, V2 = ST_SY, ST_BY, ST_W1
                o1 = s.add("scalar", lambda e: e.activation(
                    stat[:, LNU:LNU + 16], stat[:, ST_VAR1:ST_VAR1 + 16], AF.Ln, bias=EPS),
                    deps=stats2)
                o2 = s.add("scalar", lambda e: e.activation(
                    stat[:, ST_RSTD1:ST_RSTD1 + 16], stat[:, LNU:LNU + 16], AF.Exp, scale=-0.5),
                    deps=[o1])
                o3 = s.add("scalar", lambda e: e.activation(
                    stat[:, IU:IU + 16], stat[:, LNU:LNU + 16], AF.Exp, scale=-1.0), deps=[o1])
                o4 = s.add("vector", lambda e: e.tensor_mul(
                    stat[:, V2:V2 + 16], stat[:, ST_VAR1:ST_VAR1 + 16], stat[:, IU:IU + 16]),
                    deps=[o3])
                o5 = s.add("scalar", lambda e: e.activation(
                    stat[:, V2:V2 + 16], stat[:, V2:V2 + 16], AF.Ln, bias=EPS), deps=[o4])
                o6 = s.add("scalar", lambda e: e.activation(
                    stat[:, V2:V2 + 16], stat[:, V2:V2 + 16], AF.Exp, scale=-0.5), deps=[o5])
                o7 = s.add("vector", lambda e: e.tensor_mul(
                    stat[:, ST_SY:ST_SY + 16], stat[:, ST_RSTD1:ST_RSTD1 + 16],
                    stat[:, V2:V2 + 16]), deps=[o6, o2, o3])
                o8 = s.add("vector", lambda e: e.tensor_mul(
                    stat[:, ST_BY:ST_BY + 16], stat[:, ST_MEAN1:ST_MEAN1 + 16],
                    stat[:, ST_SY:ST_SY + 16]), deps=[o7])
                o9 = s.add("vector", lambda e: e.tensor_scalar_mul(
                    stat[:, ST_BY:ST_BY + 16], stat[:, ST_BY:ST_BY + 16], -1.0), deps=[o8])
                oW = s.add("vector", lambda e: e.tensor_scalar(
                    stat[:, ST_W1:ST_W1 + 16], stat[:, ST_RSTD1:ST_RSTD1 + 16], 1.0, 1.0,
                    op0=ALU.mult, op1=ALU.add), deps=[o8, o6])
                oN = s.add("vector", lambda e: e.tensor_mul(
                    stat[:, ST_NMR:ST_NMR + 16], stat[:, ST_MEAN1:ST_MEAN1 + 16],
                    stat[:, ST_RSTD1:ST_RSTD1 + 16]), deps=[o7])
                oN2 = s.add("vector", lambda e: e.tensor_scalar_mul(
                    stat[:, ST_NMR:ST_NMR + 16], stat[:, ST_NMR:ST_NMR + 16], -1.0), deps=[oN])
                tr2 = [None] * NT
                cp2 = [None] * NT
                ap2 = [None] * NT
                snd2h = {}
                cc2h = {}
                for i in range(NT):
                    d = [o7, o9]
                    if i >= 2:
                        d.append(tr2[i - 2])
                    ap2[i] = s.add("scalar", lambda e, i=i: e.activation(
                        x2_tile[:, (i % 2) * C:(i % 2) * C + C],
                        XR[:, i * C:(i + 1) * C], AF.Identity,
                        bias=stat[:, ST_BY + i:ST_BY + i + 1],
                        scale=stat[:, ST_SY + i:ST_SY + i + 1]), deps=d)

                    def em_tr2(e, i=i):
                        for cc in range(NCC):
                            ins = e.transpose(
                                ptr2[:, (i % 2) * C + cc * 128:(i % 2) * C + cc * 128 + 128],
                                x2_tile[:, (i % 2) * C + cc * 128:(i % 2) * C + cc * 128 + 128],
                                ident_b[:, :])
                        return ins
                    d = [ap2[i]]
                    if i >= 2:
                        d.append(cp2[i - 2])
                    tr2[i] = s.add("tensor", em_tr2, deps=d)
                    cp2[i] = s.add("vector", lambda e, i=i: e.tensor_copy(
                        xn2o.ap().rearrange("p (cc t) -> p cc t", cc=NCC)[:, :, i * 128:(i + 1) * 128],
                        ptr2.ap().rearrange("p (b cc t) -> p b cc t", b=2, cc=NCC)[:, i % 2]),
                        deps=[tr2[i]])
                    if i in (7, 15):
                        hh = i // 8
                        affh = []
                        for cc in range(NCC):
                            affh.append(s.add("vector", lambda e, cc=cc, hh=hh: e.tensor_scalar(
                                xn2o[:, cc * T + hh * 1024:cc * T + hh * 1024 + 1024],
                                xn2o[:, cc * T + hh * 1024:cc * T + hh * 1024 + 1024],
                                vp[:, VP_NW + cc:VP_NW + cc + 1],
                                vp[:, VP_NB + cc:VP_NB + cc + 1],
                                op0=ALU.mult, op1=ALU.add),
                                deps=[o_vp] + cp2[hh * 8:hh * 8 + 8]))
                        snd2h[hh] = s.add("sync", lambda e, hh=hh: e.dma_start(
                            out=cc2_snd[hh].ap().rearrange("p (cc t) -> p cc t", cc=NCC),
                            in_=xn2o.ap().rearrange("p (cc t) -> p cc t", cc=NCC)[
                                :, :, hh * 1024:hh * 1024 + 1024]),
                            deps=affh, dma=True, sem="io")
                        cc2h[hh] = s.add("gpsimd", lambda e, hh=hh: e.collective_compute(
                            "AllGather", ALU.bypass, replica_groups=RG,
                            ins=[cc2_snd[hh].ap().opt()], outs=[cc2_rcv[hh].ap().opt()]),
                            deps=[snd2h[hh], last_coll[0] if hh == 0 else cc2h[0]],
                            coll=True, sem="cc")
                cc2 = cc2h[1]
                spill = s.add("sync", lambda e: e.dma_start(
                    out=r1_spill.rearrange("(n p) c -> p n c", p=128),
                    in_=XR.ap().rearrange("p (n c) -> p n c", c=C)),
                    deps=[ap2[NT - 1]] + stats2, dma=True, sem="spl")
                s.emit()


        if BUILD_UPTO == 1:
            _tap_out(nc, sems, XR, OUT, r1_ops)
            return nc

        qk_ops = []
        v_ops = []
        with contextlib.ExitStack() as _s5x:
            ec5 = _s5x.enter_context
            Vt = ec5(nc.sbuf_tensor("Vt", [128, 32 * 512], BF16))
            qmkm = ec5(nc.sbuf_tensor("qmkm", [128, 20], F32))
            stt_row = ec5(nc.sbuf_tensor("stt_row", [1, 4], F32))
            _dummy = 1
            with contextlib.ExitStack() as _s5y:
                ec5y = _s5y.enter_context
                xn2f = ec5y(nc.sbuf_tensor("xn2f", [128, NCC * TFULL], BF16))
                pw_sb = ec5y(nc.sbuf_tensor("pw_sb", [128, 2 * 8 * 512], BF16))
                vw_sb = ec5y(nc.sbuf_tensor("vw_sb", [128, 8 * 512], BF16))
                vb_sb = ec5y(nc.sbuf_tensor("vb_sb", [1, 512], BF16))
                _dummy = 1
                s = Section(nc, sems)
                wq_d = s.add("gpsimd", lambda e: e.dma_start(
                    out=pw_sb[:, 0:8 * 512].rearrange("p (ic m) -> p ic m", ic=NCC),
                    in_=qwT.rearrange("(ic p) m -> p ic m", p=128)), dma=True, sem="w")
                wk_d = s.add("gpsimd", lambda e: e.dma_start(
                    out=pw_sb[:, 8 * 512:16 * 512].rearrange("p (ic m) -> p ic m", ic=NCC),
                    in_=kwT.rearrange("(ic p) m -> p ic m", p=128)), dma=True, sem="w")
                wv_d = s.add("gpsimd", lambda e: e.dma_start(
                    out=vw_sb.ap().rearrange("p (ic m) -> p ic m", ic=NCC),
                    in_=vwT.rearrange("(ic p) m -> p ic m", p=128)), dma=True, sem="w")
                vb_d = s.add("gpsimd", lambda e: e.dma_start(
                    out=vb_sb[:, :], in_=vb_row), dma=True, sem="w")
                rcvh = {0: [], 1: []}
                for a in range(2):
                    for r in range(2):
                        dd = [cc2h[a], spill]
                        if a == 0:
                            dd.append(snd2h[1])
                        rcvh[a].append(s.add("sync", lambda e, a=a, r=r: e.dma_start(
                            out=xn2f.ap().rearrange("p (cc t) -> p cc t", cc=NCC)[
                                :, :, r * T + a * 1024:r * T + a * 1024 + 1024],
                            in_=cc2_rcv[a].ap()[r].rearrange("p (cc t) -> p cc t", cc=NCC)),
                            deps=dd, dma=True, sem="io" if r == 0 else "io2"))

                with contextlib.ExitStack() as _s5p:
                    psP = _s5p.enter_context(nc.psum_tensor("psP", [128, 2 * 512], F32))
                    prev_ev = {}
                    red_ops = []
                    k5 = 0
                    units = []
                    for half in range(2):
                        qts = (0, 1, 4, 5) if half == 0 else (2, 3, 6, 7)
                        tcs = (list(range(0, 8)) + list(range(16, 24)) if half == 0
                               else list(range(8, 16)) + list(range(24, 32)))
                        for which, woff, dst, bcol in (("q", 0, QT, VP_QB),
                                                       ("k", 8 * 512, KT, VP_KB)):
                            for j in range(4):
                                for qt in qts:
                                    units.append(("qk", half, which, woff, dst, bcol, j, qt))
                                if half == 1:
                                    units.append(("red", which, j))
                        for tc in tcs:
                            units.append(("v", half, tc))
                    qk_by = {}
                    for u in units:
                        if u[0] == "red":
                            _, which, j = u
                            rbase = 4 if which == "q" else 12
                            xbase = (0 if which == "q" else 16384) + j * 4096
                            red_ops.append(s.add("vector", lambda e, rbase=rbase, j=j, xbase=xbase:
                                e.tensor_reduce(
                                    qmkm[:, rbase + j:rbase + j + 1],
                                    XRb[:, xbase:xbase + 4096],
                                    axis=AX.X, op=ALU.max, apply_absolute_value=True),
                                deps=qk_by[(which, j)]))
                            continue
                        sl = k5 % 2
                        if u[0] == "qk":
                            _, half, which, woff, dst, bcol, j, qt = u
                            nic = NCC >> (qt // 2)

                            def em_p(e, j=j, qt=qt, nic=nic, sl=sl, woff=woff):
                                for ic in range(nic):
                                    ins = e.matmul(
                                        psP[:, sl * 512:sl * 512 + 512],
                                        pw_sb[:, woff + ic * 512 + j * 128:woff + ic * 512 + j * 128 + 128],
                                        xn2f[:, ic * TFULL + qt * 512:ic * TFULL + (qt + 1) * 512],
                                        start=(ic == 0), stop=(ic == nic - 1))
                                return ins
                            d = rcvh[half] + [wq_d if which == "q" else wk_d, spill]
                            if k5 >= 2:
                                d.append(prev_ev[k5 - 2])
                            mm = s.add("tensor", em_p, deps=d)
                            ev = s.add("vector", lambda e, j=j, qt=qt, sl=sl, dst=dst, bcol=bcol: e.tensor_scalar(
                                dst(j)[:, qt * 512:(qt + 1) * 512],
                                psP[:, sl * 512:sl * 512 + 512],
                                vp[:, bcol + j:bcol + j + 1], None, op0=ALU.add),
                                deps=[mm, o_vp])
                            qk_ops.append(ev)
                            qk_by.setdefault((which, j), []).append(ev)
                        else:
                            _, half, tc = u
                            nic = NCC >> (tc // 8)

                            def em_v(e, tc=tc, nic=nic, sl=sl):
                                for ic in range(nic):
                                    e.matmul(
                                        psP[:, sl * 512:sl * 512 + 512],
                                        xn2f[:, ic * TFULL + tc * 128:ic * TFULL + tc * 128 + 128],
                                        vw_sb[:, ic * 512:(ic + 1) * 512],
                                        start=(ic == 0), stop=False)
                                ins = e.matmul(psP[:, sl * 512:sl * 512 + 512],
                                               ones_r1[:, :], vb_sb[:, :],
                                               start=False, stop=True)
                                return ins
                            d = rcvh[half] + [wv_d, vb_d, o_m1]
                            if k5 >= 2:
                                d.append(prev_ev[k5 - 2])
                            mm = s.add("tensor", em_v, deps=d)
                            ev = s.add("vector", lambda e, tc=tc, sl=sl: e.tensor_copy(
                                Vt[:, tc * 512:(tc + 1) * 512], psP[:, sl * 512:sl * 512 + 512]),
                                deps=[mm])
                            v_ops.append(ev)
                        prev_ev[k5] = ev
                        k5 += 1
                    g1 = s.add("vector", lambda e: e.tensor_reduce(
                        qmkm[:, 0:1], qmkm[:, 4:8], axis=AX.X, op=ALU.max),
                        deps=qk_ops + red_ops)
                    g2 = s.add("vector", lambda e: e.tensor_reduce(
                        qmkm[:, 1:2], qmkm[:, 12:16], axis=AX.X, op=ALU.max),
                        deps=qk_ops + red_ops + [g1])
                    with nc.psum_tensor("psst", [128, 132], F32) as psst:
                        t1_ = s.add("tensor", lambda e: e.transpose(
                            psst[0:2, 0:128], qmkm[:, 0:2], ident_f[:, :]),
                            deps=[g1, g2, o_idf])
                        r1_ = s.add("vector", lambda e: e.tensor_reduce(
                            qmkm[0:2, 0:1], psst[0:2, 0:128], axis=AX.X, op=ALU.max),
                            deps=[t1_, g2])
                        t2_ = s.add("tensor", lambda e: e.transpose(
                            psst[0:1, 128:130], qmkm[0:2, 0:1], ident_f[0:2, 0:2]),
                            deps=[r1_])
                        m0_ = s.add("vector", lambda e: e.tensor_copy(
                            stt_row[0:1, 2:4], psst[0:1, 128:130]), deps=[t2_])
                        m1_ = s.add("vector", lambda e: e.tensor_mul(
                            stt_row[0:1, 0:1], stt_row[0:1, 2:3], stt_row[0:1, 3:4]),
                            deps=[m0_])
                        m2_ = s.add("vector", lambda e: e.tensor_scalar_mul(
                            stt_row[0:1, 0:1], stt_row[0:1, 0:1], -4.0), deps=[m1_])
                        b1_ = s.add("tensor", lambda e: e.matmul(
                            psst[:, 130:131], ones_r1f[:, :], stt_row[0:1, 0:1],
                            start=True, stop=True), deps=[m2_, o_m2])
                        b2_ = s.add("vector", lambda e: e.tensor_copy(
                            cbias[:, 0:1], psst[:, 130:131]), deps=[b1_])
                    s.emit()

            attn_oT = ec5(nc.sbuf_tensor("attn_oT", [128, 4 * TFULL], BF16))

            if BUILD_UPTO == 2:
                _tap_out(nc, sems, XR, OUT, qk_ops + v_ops + [b2_])
                return nc

            with contextlib.ExitStack() as _s6x:
                ec6 = _s6x.enter_context
                recipb = ec6(nc.sbuf_tensor("recipb", [128, 2 * 512], F32))
                rrow = ec6(nc.sbuf_tensor("rrow", [1, 2048], F32))
                ptb = ec6(nc.sbuf_tensor("ptb", [128, 2 * 2048], BF16))
                praw = ec6(nc.sbuf_tensor("praw", [128, 2 * 512], F32))
                pssc = ec6(nc.psum_tensor("pssc", [128, 2048], F32))
                psPV = ec6(nc.psum_tensor("psPV", [128, 2 * 512], F32))
                psDB = ec6(nc.psum_tensor("psDB", [128, 2 * 512], F32))
                _dummy = 1
                s = Section(nc, sems)
                attn_ops = []
                hist = {}
                NQ = 16

                def emit_ep_early(p, qt):
                    den_last = hist["last_den"]
                    pv_last = hist["last_pv"]
                    cr = s.add("vector", lambda e: e.tensor_copy(
                        praw[:, :], psPV[:, :]), deps=[pv_last, hist.get("raw_used")])
                    hist["raw_copy"] = cr
                    cps = []
                    for half, r0, r1_ in ((0, 0, 64), (1, 32, 96)):
                        ca = s.add("vector", lambda e, r0=r0, half=half: e.tensor_copy(
                            rrow[0:1, half * 1024:half * 1024 + 512],
                            psDB[r0:r0 + 1, 0:512]), deps=[den_last, hist.get("rr_used")])
                        cb = s.add("vector", lambda e, r1_=r1_, half=half: e.tensor_add(
                            rrow[0:1, half * 1024:half * 1024 + 512],
                            rrow[0:1, half * 1024:half * 1024 + 512],
                            psDB[r1_:r1_ + 1, 0:512]), deps=[ca])
                        cps.append(cb)
                    hist["last_rr"] = cps[-1]
                    hist["den_cp"] = cps

                def emit_ep_late_half(p, qt, half):
                    jj = 2 * p + half
                    rr1 = s.add("scalar", lambda e, half=half: e.activation(
                        rrow[0:1, half * 1024:half * 1024 + 512],
                        rrow[0:1, half * 1024:half * 1024 + 512], AF.Ln),
                        deps=[hist["den_cp"][half]])
                    rr2 = s.add("scalar", lambda e, half=half: e.activation(
                        rrow[0:1, half * 1024:half * 1024 + 512],
                        rrow[0:1, half * 1024:half * 1024 + 512],
                        AF.Exp, scale=-1.0), deps=[rr1])
                    d1 = s.add("sync", lambda e, half=half: e.dma_start(
                        out=rr_dram[half].ap(),
                        in_=rrow[0:1, half * 1024:half * 1024 + 512]),
                        deps=[rr2], dma=True, sem="rb")
                    d2 = s.add("sync", lambda e, half=half: e.dma_start(
                        out=recipb[:, half * 512:half * 512 + 512],
                        in_=rr_dram[half].ap().broadcast_to([128, 512])),
                        deps=[d1, hist.get(("rb_used", half))], dma=True, sem="rb")
                    ev = s.add("vector", lambda e, jj=jj, qt=qt, half=half: e.tensor_mul(
                        attn_oT[:, jj * TFULL + qt * 512:jj * TFULL + (qt + 1) * 512],
                        praw[:, half * 512:half * 512 + 512],
                        recipb[:, half * 512:half * 512 + 512]),
                        deps=[d2, hist["raw_copy"]])
                    attn_ops.append(ev)
                    hist[("rb_used", half)] = ev
                    if half == 1:
                        hist["raw_used"] = ev
                        hist["rr_used"] = d1

                def emit_ep_late(p, qt):
                    emit_ep_late_half(p, qt, 0)
                    emit_ep_late_half(p, qt, 1)

                pending = None
                NQ2 = 32
                units = [(p, qt) for p in range(2) for qt in range(8)]
                for p, qt in units:
                    j0, j1 = 2 * p, 2 * p + 1
                    exp_ops = [None] * NQ2
                    qk_mms = [None] * NQ2
                    pv_mms = [None] * NQ2
                    den_mms = {}

                    def em_qk(e, q, j0=j0, j1=j1, qt=qt):
                        for i, jj in enumerate((j0, j1)):
                            ins = e.matmul(
                                pssc[:, (q % 2) * 1024 + i * 512:
                                     (q % 2) * 1024 + (i + 1) * 512],
                                KT(jj)[:, q * 128:(q + 1) * 128],
                                QT(jj)[:, qt * 512:(qt + 1) * 512],
                                start=True, stop=True)
                        return ins

                    def em_pv(e, q, j0=j0, j1=j1):
                        for i, jj in enumerate((j0, j1)):
                            ins = e.matmul(
                                psPV[:, i * 512:(i + 1) * 512],
                                Vt[:, q * 512 + jj * 128:q * 512 + jj * 128 + 128],
                                ptb[:, (q % 4) * 1024 + i * 512:
                                    (q % 4) * 1024 + (i + 1) * 512],
                                start=(q == 0), stop=(q == NQ2 - 1))
                        return ins

                    def em_den(e, qa, qb):
                        first = (qa == 0)
                        last = (qb == NQ2 - 1)
                        for row, kcx, i in ((0, qa, 0), (32, qa, 1),
                                            (64, qb, 0), (96, qb, 1)):
                            ins = e.matmul(
                                psDB[row:row + 1, 0:512], ones_c1[:, :],
                                ptb[:, (kcx % 4) * 1024 + i * 512:
                                    (kcx % 4) * 1024 + (i + 1) * 512],
                                start=first, stop=last, tile_position=(0, row))
                        return ins

                    for q_ in range(NQ2):
                        if q_ >= 2:
                            d = [exp_ops[q_ - 2]]
                        elif hist.get("pexp") is not None:
                            d = [hist["pexp"][q_]]
                        else:
                            d = qk_ops + v_ops + [b2_]
                        qk_mms[q_] = s.add("tensor",
                                           lambda e, q=q_, f=em_qk: f(e, q), deps=d)

                        ex_d = [qk_mms[q_], b2_]
                        if q_ >= 4:
                            ex_d += [pv_mms[q_ - 4], den_mms[(q_ - 4) // 2]]
                        elif hist.get("ppv") is not None:
                            ex_d += [hist["ppv"][q_], hist["pdb"][q_ // 2]]
                        exp_ops[q_] = s.add("scalar", lambda e, q=q_: e.activation(
                            ptb[:, (q % 4) * 1024:(q % 4) * 1024 + 1024],
                            pssc[:, (q % 2) * 1024:(q % 2) * 1024 + 1024],
                            AF.Exp, scale=1.0 / 32.0, bias=cbias[:, 0:1]), deps=ex_d)

                        if q_ >= 1:
                            d = [exp_ops[q_ - 1]]
                            if q_ - 1 == 0 and hist.get("raw_copy") is not None:
                                d.append(hist["raw_copy"])
                            pv_mms[q_ - 1] = s.add("tensor",
                                lambda e, q=q_ - 1, f=em_pv: f(e, q), deps=d)
                            if (q_ - 1) % 2 == 1:
                                bi = (q_ - 1) // 2
                                dd_ = [exp_ops[q_ - 1], exp_ops[q_ - 2], o_m3]
                                if bi == 0 and hist.get("last_rr") is not None:
                                    dd_.append(hist["last_rr"])
                                if bi == 0 and hist.get("rr_used") is not None:
                                    dd_.append(hist.get("rr_used"))
                                den_mms[bi] = s.add("tensor",
                                    lambda e, qa=q_ - 2, qb=q_ - 1, f=em_den:
                                    f(e, qa, qb), deps=dd_)
                        if q_ == 8 and pending is not None:
                            emit_ep_late_half(pending[0], pending[1], 0)
                        if q_ == 20 and pending is not None:
                            done_u = pending
                            emit_ep_late_half(pending[0], pending[1], 1)
                            pending = None
                            if done_u == (0, 7):
                                snd3a = s.add("sync", lambda e: e.dma_start(
                                    out=cc3_snd_a.ap(),
                                    in_=attn_oT[:, 0:2 * TFULL]),
                                    deps=list(attn_ops), dma=True, sem="io")
                                cc3a = s.add("gpsimd", lambda e: e.collective_compute(
                                    "AllGather", ALU.bypass, replica_groups=RG,
                                    ins=[cc3_snd_a.ap().opt()],
                                    outs=[cc3_rcv_a.ap().opt()]),
                                    deps=[snd3a, cc2], coll=True, sem="cc")
                            elif done_u == (1, 3):
                                snd3b1 = s.add("sync", lambda e: e.dma_start(
                                    out=cc3_snd_b1.ap().rearrange(
                                        "p (j t) -> p j t", j=2),
                                    in_=attn_oT.ap().rearrange(
                                        "p (j t) -> p j t", j=4)[:, 2:4, 0:T]),
                                    deps=attn_ops[16:24], dma=True, sem="io")
                                cc3b1 = s.add("gpsimd", lambda e: e.collective_compute(
                                    "AllGather", ALU.bypass, replica_groups=RG,
                                    ins=[cc3_snd_b1.ap().opt()],
                                    outs=[cc3_rcv_b1.ap().opt()]),
                                    deps=[snd3b1, cc3a], coll=True, sem="cc")
                    pv_mms[NQ2 - 1] = s.add("tensor",
                        lambda e, q=NQ2 - 1, f=em_pv: f(e, q),
                        deps=[exp_ops[NQ2 - 1]])
                    den_mms[(NQ2 - 1) // 2] = s.add("tensor",
                        lambda e, qa=NQ2 - 2, qb=NQ2 - 1, f=em_den: f(e, qa, qb),
                        deps=[exp_ops[NQ2 - 1], exp_ops[NQ2 - 2], o_m3])
                    hist["pexp"] = (exp_ops[NQ2 - 2], exp_ops[NQ2 - 1])
                    hist["ppv"] = [pv_mms[NQ2 - 4 + i] for i in range(4)]
                    hist["pdb"] = [den_mms[(NQ2 - 4) // 2], den_mms[(NQ2 - 2) // 2]]
                    hist["last_pv"] = pv_mms[NQ2 - 1]
                    hist["last_den"] = den_mms[(NQ2 - 1) // 2]
                    emit_ep_early(p, qt)
                    pending = (p, qt)
                emit_ep_late(*pending)
                s.emit()

            if BUILD_UPTO == 3:
                _tap_out(nc, sems, attn_oT.ap().bitcast(F32), OUT, attn_ops, ntiles=8)
                return nc

            s = Section(nc, sems)
            snd3b2 = s.add("sync", lambda e: e.dma_start(
                out=cc3_snd_b2.ap().rearrange("p (j t) -> p j t", j=2),
                in_=attn_oT.ap().rearrange("p (j t) -> p j t", j=4)[:, 2:4, T:2 * T]),
                deps=attn_ops, dma=True, sem="io")
            cc3b2 = s.add("gpsimd", lambda e: e.collective_compute(
                "AllGather", ALU.bypass, replica_groups=RG,
                ins=[cc3_snd_b2.ap().opt()], outs=[cc3_rcv_b2.ap().opt()]),
                deps=[snd3b2, cc3b1], coll=True, sem="cc")
            s.emit()

        with contextlib.ExitStack() as _s7x:
                ec7 = _s7x.enter_context
                attn_all = ec7(nc.sbuf_tensor("attn_all", [128, NCC * T], BF16))
                mbufL = ec7(nc.sbuf_tensor("mbufL", [128, 2 * T], BF16))
                mbufR = ec7(nc.sbuf_tensor("mbufR", [128, 2 * T], BF16))
                ow_sb = ec7(nc.sbuf_tensor("ow_sb", [128, 2 * NCC * C], BF16))
                ob_sb = ec7(nc.sbuf_tensor("ob_sb", [128, 16], F32))
                o_ev2 = ec7(nc.sbuf_tensor("o_ev2", [128, 8 * 512], F32))
                wsb = ec7(nc.sbuf_tensor("wsb", [128, C], F32))
                fin = ec7(nc.sbuf_tensor("fin", [128, 2 * C], F32))
                psO2 = ec7(nc.psum_tensor("psO2", [128, 2 * 512], F32))
                psTk = ec7(nc.psum_tensor("psTk", [128, 2 * C], F32))
                _dummy = 1
                s = Section(nc, sems)
                ow_d0 = s.add("gpsimd", lambda e: e.dma_start(
                    out=ow_sb[:, 0:NCC * C].rearrange("p (ic m) -> p ic m", ic=NCC),
                    in_=owT0.rearrange("(ic p) m -> p ic m", p=128)),
                    deps=[snd3b2], dma=True, sem="w")
                ow_d1 = s.add("gpsimd", lambda e: e.dma_start(
                    out=ow_sb[:, NCC * C:2 * NCC * C].rearrange("p (ic m) -> p ic m", ic=NCC),
                    in_=owT1.rearrange("(ic p) m -> p ic m", p=128)),
                    deps=[snd3b2], dma=True, sem="w")
                ob_d = s.add("gpsimd", lambda e: e.dma_start(
                    out=ob_sb[:, :], in_=ob_cols), deps=[snd3b2], dma=True, sem="w")
                mrg = []
                mrg_by = {}
                for idx, hh in enumerate((0, 1, 4, 5, 2, 3, 6, 7)):
                    r, j = hh // 4, hh % 4
                    pp = idx % 2
                    ms = "io" if pp == 0 else "io2"
                    if j < 2:
                        src_l = cc3_rcv_a.ap()[r][:, j * TFULL:j * TFULL + T]
                        src_r = cc3_rcv_a.ap()[r][:, j * TFULL + T:j * TFULL + 2 * T]
                        dep_l, dep_r = cc3a, cc3a
                    else:
                        src_l = cc3_rcv_b1.ap()[r][:, (j - 2) * T:(j - 1) * T]
                        src_r = cc3_rcv_b2.ap()[r][:, (j - 2) * T:(j - 1) * T]
                        dep_l, dep_r = cc3b1, cc3b2
                    prevm = mrg[idx - 2] if idx >= 2 else None
                    dl = s.add("sync", lambda e, src_l=src_l, pp=pp: e.dma_start(
                        out=mbufL[:, pp * T:(pp + 1) * T], in_=src_l),
                        deps=[dep_l, snd3b2, prevm], dma=True, sem=ms)
                    dr = s.add("sync", lambda e, src_r=src_r, pp=pp: e.dma_start(
                        out=mbufR[:, pp * T:(pp + 1) * T], in_=src_r),
                        deps=[dep_r, snd3b2, prevm], dma=True, sem=ms)
                    m1 = s.add("vector", lambda e, hh=hh, pp=pp: e.tensor_sub(
                        attn_all[:, hh * T:(hh + 1) * T],
                        mbufR[:, pp * T:(pp + 1) * T], mbufL[:, pp * T:(pp + 1) * T]),
                        deps=[dl, dr])
                    m2 = s.add("vector", lambda e, hh=hh, pp=pp: e.scalar_tensor_tensor(
                        attn_all[:, hh * T:(hh + 1) * T],
                        attn_all[:, hh * T:(hh + 1) * T],
                        vp[:, VP_HM:VP_HM + 1], mbufL[:, pp * T:(pp + 1) * T],
                        op0=ALU.mult, op1=ALU.add), deps=[m1, o_vp])
                    mrg.append(m2)
                    mrg_by[hh] = m2
                rel = s.add("sync", lambda e: e.dma_start(
                    out=XR.ap().rearrange("p (n c) -> p n c", c=C),
                    in_=r1_spill.rearrange("(n p) c -> p n c", p=128)),
                    deps=attn_ops, dma=True, sem="spl")
                prev_ev = {}
                prev_f = {}
                last_round_trbs = []
                out_ops = []
                ko = 0
                for eo in range(2):
                    for qt2 in range(2):
                        ev_blocks = []
                        round_trbs = []
                        for oc in range(NCC):
                            sl = ko % 2

                            def em_omm(ics, first, last, e=None, eo=eo, qt2=qt2,
                                       oc=oc, sl=sl):
                                for k_, ic in enumerate(ics):
                                    ins = e.matmul(
                                        psO2[:, sl * 512:sl * 512 + 512],
                                        ow_sb[:, eo * NCC * C + ic * C + oc * 128:
                                              eo * NCC * C + ic * C + oc * 128 + 128],
                                        attn_all[:, ic * T + eo * C + qt2 * 512:
                                                 ic * T + eo * C + (qt2 + 1) * 512],
                                        start=(first and k_ == 0),
                                        stop=(last and k_ == len(ics) - 1))
                                return ins
                            ow_d = ow_d0 if eo == 0 else ow_d1
                            d = [mrg_by[0], mrg_by[1], mrg_by[4], mrg_by[5], ow_d]
                            if ko >= 2:
                                d.append(prev_ev[ko - 2])
                            mma = s.add("tensor", lambda e, f=em_omm: f(
                                (0, 1, 4, 5), True, False, e), deps=d)
                            mmb = s.add("tensor", lambda e, f=em_omm: f(
                                (2, 3, 6, 7), False, True, e),
                                deps=[mma, mrg_by[2], mrg_by[3], mrg_by[6], mrg_by[7]])
                            ev = s.add("vector", lambda e, oc=oc, eo=eo, sl=sl: e.tensor_scalar(
                                o_ev2[:, oc * 512:(oc + 1) * 512],
                                psO2[:, sl * 512:sl * 512 + 512],
                                ob_sb[:, eo * 8 + oc:eo * 8 + oc + 1], None, op0=ALU.add),
                                deps=[mmb, ob_d] + last_round_trbs)
                            prev_ev[ko] = ev
                            ev_blocks.append(ev)
                            ko += 1
                        for blk_ in range(4):
                            ti = eo * 8 + qt2 * 4 + blk_
                            slt = ti % 2

                            def em_trb(e, blk_=blk_, slt=slt):
                                for oc in range(NCC):
                                    ins = e.transpose(
                                        psTk[:, slt * C + oc * 128:slt * C + oc * 128 + 128],
                                        o_ev2[:, oc * 512 + blk_ * 128:oc * 512 + blk_ * 128 + 128],
                                        ident_f[:, :])
                                return ins
                            d = ev_blocks + [o_idf]
                            if ti >= 2:
                                d.append(prev_f[ti - 2])
                            trb = s.add("tensor", em_trb, deps=d)
                            round_trbs.append(trb)
                            wop = s.add("scalar", lambda e, ti=ti: e.activation(
                                wsb[:, :], XR[:, ti * C:(ti + 1) * C], AF.Identity,
                                bias=stat[:, ST_NMR + ti:ST_NMR + ti + 1],
                                scale=stat[:, ST_W1 + ti:ST_W1 + ti + 1]),
                                deps=[rel, oW, oN2, prev_f.get(ti - 1)])
                            fo = s.add("vector", lambda e, ti=ti, slt=slt: e.tensor_add(
                                fin[:, (ti % 2) * C:(ti % 2) * C + C],
                                wsb[:, :], psTk[:, slt * C:slt * C + C]),
                                deps=[wop, trb] +
                                ([out_ops[ti - 2]] if ti >= 2 else []))
                            prev_f[ti] = fo
                            oo = s.add("sync", lambda e, ti=ti: e.dma_start(
                                out=OUT[ti * 128:(ti + 1) * 128, :],
                                in_=fin[:, (ti % 2) * C:(ti % 2) * C + C]),
                                deps=[fo], dma=True, sem="io")
                            out_ops.append(oo)
                        last_round_trbs = round_trbs
                s.add("sync", lambda e: e.nop(), deps=out_ops)
                s.emit()
    return nc


def _tap_out(nc, sems, XR, OUT, deps, ntiles=NT):
    s = Section(nc, sems)
    outs = []
    for i in range(ntiles):
        outs.append(s.add("sync", lambda e, i=i: e.dma_start(
            out=OUT[i * 128:(i + 1) * 128, :], in_=XR[:, i * C:(i + 1) * C]),
            deps=deps, dma=True, sem="io"))
    s.add("sync", lambda e: e.nop(), deps=outs)
    s.emit()


_NC_CACHE = {}


def kernel(L0_attn_x, L0_attn_time_decay, L0_attn_time_first, L0_attn_time_mix,
           L0_attn_Wk, L0_attn_Wv, L0_attn_Wr, L0_attn_Wo,
           L0_ffn_router_prob, L0_ffn_norm_w, L0_ffn_norm_b,
           L0_ffn_q_w, L0_ffn_q_b, L0_ffn_k_w, L0_ffn_k_b,
           L0_ffn_v_w, L0_ffn_v_b, L0_ffn_o_w, L0_ffn_o_b):
    x = np.ascontiguousarray(np.asarray(L0_attn_x, dtype=np.float32))
    B, Tf, D = x.shape
    f32 = np.float32

    def packcols(v):
        return np.ascontiguousarray(np.asarray(v, f32).reshape(NCC, 128).T)

    wkT = np.ascontiguousarray(np.asarray(L0_attn_Wk, f32).T)
    wvT = np.ascontiguousarray(np.asarray(L0_attn_Wv, f32).T)
    wrT = np.ascontiguousarray(np.asarray(L0_attn_Wr, f32).T)
    woT = np.ascontiguousarray(np.asarray(L0_attn_Wo, f32).T)
    qwTf = np.asarray(L0_ffn_q_w, f32).T
    kwTf = np.asarray(L0_ffn_k_w, f32).T
    vwTf = np.asarray(L0_ffn_v_w, f32).T
    owTf = np.asarray(L0_ffn_o_w, f32).T

    key = (BUILD_UPTO, len(RG_GROUPS))
    if key not in _NC_CACHE:
        _NC_CACHE[key] = build_nc()
    nc = _NC_CACHE[key]

    in_maps = []
    for c in range(8):
        b, h = c // 2, c % 2
        xe = np.zeros((T + 1, C), f32)
        if h == 1:
            xe[0] = x[b, T - 1]
        xe[1:] = x[b, h * T:(h + 1) * T]
        ow0 = np.zeros((C, C), f32)
        ow1 = np.zeros((C, C), f32)
        m0 = C >> (2 * h)
        m1 = C >> (2 * h + 1)
        ow0[:m0, :m0] = owTf[:m0, :m0]
        ow1[:m1, :m1] = owTf[:m1, :m1]
        obc = np.zeros((2, C), f32)
        obc[0, :m0] = np.asarray(L0_ffn_o_b, f32)[:m0]
        obc[1, :m1] = np.asarray(L0_ffn_o_b, f32)[:m1]
        ob_cols = np.zeros((128, 16), f32)
        ob_cols[:, 0:8] = obc[0].reshape(NCC, 128).T
        ob_cols[:, 8:16] = obc[1].reshape(NCC, 128).T
        veca = np.zeros((128, 49), f32)
        veca[:, VP_TD:VP_TD + 8] = packcols(L0_attn_time_decay)
        veca[:, VP_TF:VP_TF + 8] = packcols(L0_attn_time_first)
        veca[:, VP_TM:VP_TM + 8] = packcols(L0_attn_time_mix)
        veca[:, VP_NW:VP_NW + 8] = packcols(L0_ffn_norm_w)
        veca[:, VP_NB:VP_NB + 8] = packcols(L0_ffn_norm_b)
        qb = np.asarray(L0_ffn_q_b, f32)[h * 512:(h + 1) * 512]
        kb = np.asarray(L0_ffn_k_b, f32)[h * 512:(h + 1) * 512]
        veca[:, VP_QB:VP_QB + 4] = qb.reshape(4, 128).T
        veca[:, VP_KB:VP_KB + 4] = kb.reshape(4, 128).T
        veca[:, VP_HM] = float(h)
        in_maps.append(dict(
            x_ext=xe,
            wkT=wkT, wvT=wvT, wrT=wrT, woT=woT,
            qwT=np.ascontiguousarray(qwTf[:, h * 512:(h + 1) * 512]),
            kwT=np.ascontiguousarray(kwTf[:, h * 512:(h + 1) * 512]),
            vwT=np.ascontiguousarray(vwTf[:, h * 512:(h + 1) * 512]),
            owT0=ow0, owT1=ow1,
            veca=veca,
            vb_row=np.ascontiguousarray(
                np.asarray(L0_ffn_v_b, f32)[h * 512:(h + 1) * 512].reshape(1, 512)),
            ob_cols=ob_cols,
        ))

    res = run_bass_kernel_spmd(nc, in_maps, core_ids=list(range(8)))
    out = np.empty((B, Tf, D), np.float32)
    for c in range(8):
        b, h = c // 2, c % 2
        out[b, h * T:(h + 1) * T] = res.results[c]["OUT"]
    return out


# version 48
# speedup vs baseline: 1.0619x; 1.0139x over previous
import contextlib
import numpy as np
import concourse.bass as bass
import concourse.mybir as mybir
from concourse.bass_utils import run_bass_kernel_spmd

F32 = mybir.dt.float32
BF16 = mybir.dt.bfloat16
F16 = mybir.dt.float16
AF = mybir.ActivationFunctionType
ALU = mybir.AluOpType
AX = mybir.AxisListType

T = 2048
TFULL = 4096
NT = 16
C = 1024
NCC = 8
TEC = 2052
EPS = 1e-5
K_CLAMP = 60.0
K_EPS = 1e-16

BUILD_UPTO = 99
S2_LEVEL = 4
RG_GROUPS = [[0, 1], [2, 3], [4, 5], [6, 7]]

ST_TM1, ST_E, ST_LAM, ST_FW, ST_NEGE, ST_N2047E = 0, 8, 16, 24, 32, 40
ST_MEAN1, ST_VAR1, ST_RSTD1, ST_NMR, ST_SY, ST_BY, ST_W1 = 48, 64, 80, 96, 112, 128, 144
ST_COLS = 160
VP_TD, VP_TF, VP_TM, VP_NW, VP_NB, VP_QB, VP_KB, VP_HM = 0, 8, 16, 24, 32, 40, 44, 48


class Op:
    __slots__ = ("eng", "sem", "emit", "deps", "inc_k", "val", "needed", "sec",
                 "async_cls", "dwaits")

    def __init__(self, eng, sem, emit, deps, inc_k, sec, async_cls):
        self.eng = eng
        self.sem = sem
        self.emit = emit
        self.deps = deps
        self.dwaits = {}
        self.inc_k = inc_k
        self.val = None
        self.needed = False
        self.sec = sec
        self.async_cls = async_cls


class Section:

    ENGS = ("tensor", "vector", "scalar", "gpsimd", "sync")
    COUNTER = [0]

    def __init__(self, nc, sems):
        self.nc = nc
        self.sems = sems
        self.ops = {e: [] for e in self.ENGS}
        Section.COUNTER[0] += 1
        self.sec_id = Section.COUNTER[0]

    def add(self, eng, emit, deps=(), dma=False, coll=False, sem=None):
        async_cls = dma or coll
        if async_cls:
            skey = sem or "io"
            inc_k = 16 if dma else 1
        else:
            skey = eng
            inc_k = 1
        op = Op(eng, skey, emit, [], inc_k, self.sec_id, async_cls)
        for d in deps:
            if d is None:
                continue
            if d.async_cls:
                op.dwaits[d.sem] = self.sems[d.sem].n
            elif d.sec == self.sec_id:
                d.needed = True
                op.deps.append(d)
        if async_cls:
            if self.sems[skey].n > 0:
                op.dwaits[skey] = max(op.dwaits.get(skey, 0), self.sems[skey].n)
            self.sems[skey].n += inc_k
            op.val = self.sems[skey].n
        self.ops[eng].append(op)
        return op

    def emit(self):
        for e in self.ENGS:
            for op in self.ops[e]:
                if not op.async_cls and op.needed:
                    self.sems[e].n += 1
                    op.val = self.sems[e].n
        with self.nc.Block() as blk:
            for ename in self.ENGS:
                if not self.ops[ename]:
                    continue

                def make_body(ename):
                    def body(eng):
                        for op in self.ops[ename]:
                            waits = dict(op.dwaits)
                            for d in op.deps:
                                if d.val is None:
                                    raise RuntimeError("dep without inc")
                                waits[d.sem] = max(waits.get(d.sem, 0), d.val)
                            for de, dv in waits.items():
                                if dv > 0:
                                    eng.wait_ge(self.sems[de].h, dv)
                            ins = op.emit(eng)
                            if op.val is not None:
                                ins.then_inc(self.sems[op.sem].h, op.inc_k)
                    return body

                getattr(blk, ename)(make_body(ename))


class Sem:
    __slots__ = ("h", "n")

    def __init__(self, h):
        self.h = h
        self.n = 0


def build_nc():
    nc = bass.Bass("TRN2", target_bir_lowering=False)

    x_ext = nc.dram_tensor("x_ext", [T + 1, C], F32, kind="ExternalInput").ap()
    wkT = nc.dram_tensor("wkT", [C, C], F32, kind="ExternalInput").ap()
    wvT = nc.dram_tensor("wvT", [C, C], F32, kind="ExternalInput").ap()
    wrT = nc.dram_tensor("wrT", [C, C], F32, kind="ExternalInput").ap()
    woT = nc.dram_tensor("woT", [C, C], F32, kind="ExternalInput").ap()
    qwT = nc.dram_tensor("qwT", [C, 512], F32, kind="ExternalInput").ap()
    kwT = nc.dram_tensor("kwT", [C, 512], F32, kind="ExternalInput").ap()
    vwT = nc.dram_tensor("vwT", [C, 512], F32, kind="ExternalInput").ap()
    owT0 = nc.dram_tensor("owT0", [C, C], F32, kind="ExternalInput").ap()
    owT1 = nc.dram_tensor("owT1", [C, C], F32, kind="ExternalInput").ap()
    veca = nc.dram_tensor("veca", [128, 49], F32, kind="ExternalInput").ap()
    vb_row = nc.dram_tensor("vb_row", [1, 512], F32, kind="ExternalInput").ap()
    ob_cols = nc.dram_tensor("ob_cols", [128, 16], F32, kind="ExternalInput").ap()
    OUT = nc.dram_tensor("OUT", [T, C], F32, kind="ExternalOutput").ap()

    r1_spill = nc.dram_tensor("r1_spill", [T, C], F32).ap()
    bnd_dram = nc.dram_tensor("bnd_dram", [C], BF16).ap()
    cc1_snd = [nc.dram_tensor(f"cc1s{i}", [128, 2], F32) for i in range(NCC)]
    cc1_rcv = [nc.dram_tensor(f"cc1r{i}", [2, 128, 2], F32) for i in range(NCC)]
    cc2_snd = [nc.dram_tensor(f"cc2s{a}", [128, NCC * (T // 2)], BF16)
               for a in range(2)]
    cc2_rcv = [nc.dram_tensor(f"cc2r{a}", [2, 128, NCC * (T // 2)], BF16)
               for a in range(2)]
    cc3_snd_a = nc.dram_tensor("cc3sa", [128, 2 * TFULL], BF16)
    cc3_rcv_a = nc.dram_tensor("cc3ra", [2, 128, 2 * TFULL], BF16)
    cc3_snd_b1 = nc.dram_tensor("cc3sb1", [128, 2 * T], BF16)
    cc3_rcv_b1 = nc.dram_tensor("cc3rb1", [2, 128, 2 * T], BF16)
    cc3_snd_b2 = nc.dram_tensor("cc3sb2", [128, 2 * T], BF16)
    cc3_rcv_b2 = nc.dram_tensor("cc3rb2", [2, 128, 2 * T], BF16)
    rr_dram = [nc.dram_tensor(f"rrd{h}", [1, 512], F32) for h in range(2)]
    RG = RG_GROUPS

    with contextlib.ExitStack() as _top:
        ec = _top.enter_context
        XR = ec(nc.sbuf_tensor("XR", [128, NT * C], F32))
        vp = ec(nc.sbuf_tensor("vp", [128, 52], F32))
        stat = ec(nc.sbuf_tensor("stat", [128, ST_COLS], F32))
        ident_b = ec(nc.sbuf_tensor("ident_b", [128, 128], BF16))
        ident_f = ec(nc.sbuf_tensor("ident_f", [128, 128], F32))
        ones_r1 = ec(nc.sbuf_tensor("ones_r1", [1, 128], BF16))
        ones_r1f = ec(nc.sbuf_tensor("ones_r1f", [1, 128], F32))
        ones_c1 = ec(nc.sbuf_tensor("ones_c1", [128, 1], BF16))
        ones128 = ec(nc.sbuf_tensor("ones128", [128, 128], BF16))
        carry = ec(nc.sbuf_tensor("carry", [128, 24], F32))
        st6 = ec(nc.sbuf_tensor("st6", [128, 12], F32))
        mv = ec(nc.sbuf_tensor("mv", [128, 4], F32))
        cbias = ec(nc.sbuf_tensor("cbias", [128, 2], F32))
        cst = ec(nc.sbuf_tensor("cst", [128, 1], F32))
        nc.const_aps.register(F32, EPS, cst[:, 0:1])
        sems = {n: Sem(ec(nc.semaphore("q_" + n)))
                for n in ("tensor", "vector", "scalar", "gpsimd", "sync",
                          "ld", "ld2", "w", "io", "io2", "cc", "cc1d", "spl", "rb",
                          "x0", "x1", "x2", "x3", "x4", "x5", "x6", "x7")}

        XRb = XR.ap().bitcast(BF16)

        def QT(j):
            return XRb[:, j * TFULL:(j + 1) * TFULL]

        def KT(j):
            return XRb[:, 16384 + j * TFULL:16384 + (j + 1) * TFULL]

        s = Section(nc, sems)
        o_m1 = s.add("gpsimd", lambda e: e.memset(ones_r1[:, :], 1.0))
        o_m2 = s.add("gpsimd", lambda e: e.memset(ones_r1f[:, :], 1.0))
        o_m3 = s.add("gpsimd", lambda e: e.memset(ones_c1[:, :], 1.0))
        o_m4 = s.add("gpsimd", lambda e: e.memset(ones128[:, :], 1.0))
        o_m5 = s.add("gpsimd", lambda e: e.memset(carry[:, :], 0.0))
        o_m6 = s.add("gpsimd", lambda e: e.memset(cst[:, :], EPS))
        o_vp = s.add("sync", lambda e: e.dma_start(out=vp[:, 0:49], in_=veca), dma=True, sem="ld")
        xload = []
        for i in range(NT):
            xload.append(s.add("sync", lambda e, i=i: e.dma_start(
                out=XR[:, i * C:(i + 1) * C],
                in_=x_ext[1 + i * 128:1 + (i + 1) * 128, :]), dma=True,
                sem="ld" if i % 2 == 0 else "ld2"))
        o_id = s.add("gpsimd", lambda e: e.affine_select(
            ident_b[:, :], ones128[:, :], [[1, 128]],
            compare_op=ALU.is_equal, fill=0.0, base=0, channel_multiplier=-1),
            deps=[o_m4])
        o_idf = s.add("vector", lambda e: e.tensor_copy(ident_f[:, :], ident_b[:, :]),
                      deps=[o_id])
        o_tm1 = s.add("vector", lambda e: e.tensor_scalar(
            stat[:, ST_TM1:ST_TM1 + 8], vp[:, VP_TM:VP_TM + 8], -1.0, 1.0,
            op0=ALU.mult, op1=ALU.add), deps=[o_vp])
        o_e = s.add("scalar", lambda e: e.activation(
            stat[:, ST_E:ST_E + 8], vp[:, VP_TD:VP_TD + 8], AF.Exp), deps=[o_vp])
        o_lam = s.add("scalar", lambda e: e.activation(
            stat[:, ST_LAM:ST_LAM + 8], stat[:, ST_E:ST_E + 8], AF.Exp, scale=-1.0),
            deps=[o_e])
        o_fw = s.add("scalar", lambda e: e.activation(
            stat[:, ST_FW:ST_FW + 8], vp[:, VP_TF:VP_TF + 8], AF.Exp), deps=[o_vp])
        o_nege = s.add("vector", lambda e: e.tensor_scalar_mul(
            stat[:, ST_NEGE:ST_NEGE + 8], stat[:, ST_E:ST_E + 8], -1.0), deps=[o_e])
        o_n2047 = s.add("vector", lambda e: e.tensor_scalar_mul(
            stat[:, ST_N2047E:ST_N2047E + 8], stat[:, ST_E:ST_E + 8], -2047.0),
            deps=[o_e])

        last_coll = [None]
        with contextlib.ExitStack() as _sx0:
            xmT = _sx0.enter_context(nc.sbuf_tensor("xmT", [128, NCC * T], BF16))
            rwkvT = _sx0.enter_context(nc.sbuf_tensor("rwkvT", [128, NCC * T], BF16))
            with contextlib.ExitStack() as _s1x:
                ec1 = _s1x.enter_context
                xnT = ec1(nc.sbuf_tensor("xnT", [128, NCC * TEC], BF16))
                xn_tile = ec1(nc.sbuf_tensor("xn_tile", [128, 2 * C], BF16))
                xb_row = ec1(nc.sbuf_tensor("xb_row", [1, C], F32))
                xb_st = ec1(nc.sbuf_tensor("xb_st", [1, 12], F32))
                xb_mv = ec1(nc.sbuf_tensor("xb_mv", [1, 6], F32))
                xb_bf = ec1(nc.sbuf_tensor("xb_bf", [1, C], BF16))
                ptr = ec1(nc.psum_tensor("ptr", [128, 2 * C], BF16))
                _dummy = 1
                stats_done = []
                prev = None
                for i in range(NT):
                    d = [xload[i]] + ([prev] if prev else [])
                    a = s.add("vector", lambda e, i=i: e.bn_stats(
                        st6[:, 0:6], XR[:, i * C:i * C + 512]), deps=d)
                    b = s.add("vector", lambda e, i=i: e.bn_stats(
                        st6[:, 6:12], XR[:, i * C + 512:i * C + 1024]), deps=[a])
                    g = s.add("vector", lambda e: e.bn_aggr(mv[:, 0:2], st6[:, :]),
                              deps=[a, b])
                    cm = s.add("vector", lambda e, i=i: e.tensor_copy(
                        stat[:, ST_MEAN1 + i:ST_MEAN1 + i + 1], mv[:, 0:1]), deps=[g])
                    cv = s.add("vector", lambda e, i=i: e.tensor_copy(
                        stat[:, ST_VAR1 + i:ST_VAR1 + i + 1], mv[:, 1:2]), deps=[g])
                    prev = cv
                    stats_done.extend([cm, cv])
                o_ln = s.add("scalar", lambda e: e.activation(
                    stat[:, ST_RSTD1:ST_RSTD1 + 16], stat[:, ST_VAR1:ST_VAR1 + 16],
                    AF.Ln, bias=EPS), deps=stats_done)
                o_rstd = s.add("scalar", lambda e: e.activation(
                    stat[:, ST_RSTD1:ST_RSTD1 + 16], stat[:, ST_RSTD1:ST_RSTD1 + 16],
                    AF.Exp, scale=-0.5), deps=[o_ln])
                o_nmr0 = s.add("vector", lambda e: e.tensor_mul(
                    stat[:, ST_NMR:ST_NMR + 16], stat[:, ST_MEAN1:ST_MEAN1 + 16],
                    stat[:, ST_RSTD1:ST_RSTD1 + 16]), deps=[o_rstd])
                o_nmr = s.add("vector", lambda e: e.tensor_scalar_mul(
                    stat[:, ST_NMR:ST_NMR + 16], stat[:, ST_NMR:ST_NMR + 16], -1.0),
                    deps=[o_nmr0])
                o_xb = s.add("sync", lambda e: e.dma_start(
                    out=xb_row[:, :], in_=x_ext[0:1, :]), dma=True, sem="ld")
                a = s.add("vector", lambda e: e.bn_stats(xb_st[:, 0:6], xb_row[:, 0:512]),
                          deps=[o_xb])
                b = s.add("vector", lambda e: e.bn_stats(xb_st[:, 6:12], xb_row[:, 512:1024]),
                          deps=[a])
                g = s.add("vector", lambda e: e.bn_aggr(xb_mv[:, 0:2], xb_st[:, :]),
                          deps=[a, b])
                l1 = s.add("scalar", lambda e: e.activation(
                    xb_mv[:, 2:3], xb_mv[:, 1:2], AF.Ln, bias=EPS), deps=[g])
                l2 = s.add("scalar", lambda e: e.activation(
                    xb_mv[:, 3:4], xb_mv[:, 2:3], AF.Exp, scale=-0.5), deps=[l1])
                l3 = s.add("vector", lambda e: e.tensor_mul(
                    xb_mv[:, 4:5], xb_mv[:, 0:1], xb_mv[:, 3:4]), deps=[l2, g])
                l4 = s.add("vector", lambda e: e.tensor_scalar_mul(
                    xb_mv[:, 4:5], xb_mv[:, 4:5], -1.0), deps=[l3])
                l5 = s.add("scalar", lambda e: e.activation(
                    xb_bf[:, :], xb_row[:, :], AF.Identity,
                    bias=xb_mv[:, 4:5], scale=xb_mv[:, 3:4]), deps=[l4])
                l6 = s.add("sync", lambda e: e.dma_start(
                    out=bnd_dram.rearrange("(o c) -> o c", o=1), in_=xb_bf[0:1, :]),
                    deps=[l5], dma=True, sem="ld")
                l7s = []
                for cc in range(NCC):
                    l7s.append(s.add("sync", lambda e, cc=cc: e.dma_start(
                        out=xnT[:, cc * TEC:cc * TEC + 1],
                        in_=bnd_dram[cc * 128:(cc + 1) * 128].rearrange("(p o) -> p o", o=1)),
                        deps=[l6], dma=True, sem="ld"))
                l7 = l7s[-1]
                ap_ops = [None] * NT
                tr_ops = [None] * NT
                cp_ops = [None] * NT
                for i in range(NT):
                    d = [o_rstd, o_nmr, xload[i]]
                    if i >= 2:
                        d.append(tr_ops[i - 2])
                    ap_ops[i] = s.add("scalar", lambda e, i=i: e.activation(
                        xn_tile[:, (i % 2) * C:(i % 2) * C + C],
                        XR[:, i * C:(i + 1) * C], AF.Identity,
                        bias=stat[:, ST_NMR + i:ST_NMR + i + 1],
                        scale=stat[:, ST_RSTD1 + i:ST_RSTD1 + i + 1]), deps=d)

                    def em_tr(e, i=i):
                        for cc in range(NCC):
                            ins = e.transpose(
                                ptr[:, (i % 2) * C + cc * 128:(i % 2) * C + cc * 128 + 128],
                                xn_tile[:, (i % 2) * C + cc * 128:(i % 2) * C + cc * 128 + 128],
                                ident_b[:, :])
                        return ins
                    d = [ap_ops[i], o_id]
                    if i >= 2:
                        d.append(cp_ops[i - 2])
                    tr_ops[i] = s.add("tensor", em_tr, deps=d)
                    cp_ops[i] = s.add("vector", lambda e, i=i: e.tensor_copy(
                        xnT.ap().rearrange("p (cc t) -> p cc t", cc=NCC)[:, :, 1 + i * 128:1 + (i + 1) * 128],
                        ptr.ap().rearrange("p (b cc t) -> p b cc t", b=2, cc=NCC)[:, i % 2]),
                        deps=[tr_ops[i]])
                mix1 = []
                for cc in range(NCC):
                    mix1.append(s.add("vector", lambda e, cc=cc: e.tensor_scalar_mul(
                        xmT[:, cc * T:(cc + 1) * T],
                        xnT[:, cc * TEC:cc * TEC + T],
                        stat[:, ST_TM1 + cc:ST_TM1 + cc + 1]),
                        deps=[o_tm1] + l7s + cp_ops))
                mix_done = []
                for cc in range(NCC):
                    mix_done.append(s.add("vector", lambda e, cc=cc: e.scalar_tensor_tensor(
                        xmT[:, cc * T:(cc + 1) * T],
                        xnT[:, cc * TEC + 1:cc * TEC + 1 + T],
                        vp[:, VP_TM + cc:VP_TM + cc + 1],
                        xmT[:, cc * T:(cc + 1) * T],
                        op0=ALU.mult, op1=ALU.add), deps=[mix1[cc]]))
                s.emit()

            if BUILD_UPTO == 12:
                _tap_out(nc, sems, XR, OUT, [])
                return nc
            with contextlib.ExitStack() as _s2x:
                ec2 = _s2x.enter_context
                wbuf = ec2(nc.sbuf_tensor("wbuf", [128, 2 * 2 * C], BF16))
                sl1 = ec2(nc.sbuf_tensor("sl1", [128, 2 * T], F32))
                sl2 = ec2(nc.sbuf_tensor("sl2", [128, 2 * T], F32))
                sl3 = ec2(nc.sbuf_tensor("sl3", [128, 2 * T], F32))
                sl6 = ec2(nc.sbuf_tensor("sl6", [128, 2 * T], F32))
                lamb = ec2(nc.sbuf_tensor("lamb", [128, T], BF16))
                psA = ec2(nc.psum_tensor("psA", [128, 2048], F32))
                psB = ec2(nc.psum_tensor("psB", [128, 2048], F32))
                _dummy = 1
                s = Section(nc, sems)
                rwkv_ops = [None] * NCC
                prev = {}
                E60 = float(np.exp(60.0))

                def emit_cc(oc):
                    if ("ccargs", oc) not in prev:
                        return
                    (snd, CM, xs) = prev.pop(("ccargs", oc))
                    cc = s.add("gpsimd", lambda e, oc=oc: e.collective_compute(
                        "AllGather", ALU.bypass, replica_groups=RG,
                        ins=[cc1_snd[oc].ap().opt()],
                        outs=[cc1_rcv[oc].ap().opt()]),
                        deps=[snd], coll=True, sem=xs)
                    last_coll[0] = cc
                    rcv = s.add("sync", lambda e, oc=oc, CM=CM: e.dma_start(
                        out=carry[:, CM:CM + 2], in_=cc1_rcv[oc].ap()[0]),
                        deps=[cc, prev.get(("cmask", oc - 2))], dma=True, sem=xs)
                    cmask = s.add("vector", lambda e, CM=CM: e.tensor_scalar_mul(
                        carry[:, CM + 2:CM + 4], carry[:, CM:CM + 2],
                        vp[:, VP_HM:VP_HM + 1]),
                        deps=[rcv, o_vp])
                    prev[("cmask", oc)] = cmask

                def emit_tail(oc):
                    if ("tailargs", oc) not in prev:
                        return
                    (s1, s2, s3, s6, lam0, CM,
                     w1b, w2e, er, ck) = prev.pop(("tailargs", oc))
                    cmask = prev[("cmask", oc)]
                    c1 = s.add("vector", lambda e, s2=s2, CM=CM:
                               e.scalar_tensor_tensor(
                                   s2, lamb[:, :], carry[:, CM + 2:CM + 3], s2,
                                   op0=ALU.mult, op1=ALU.add),
                               deps=[w1b, lam0, cmask])
                    c2 = s.add("vector", lambda e, s1=s1, CM=CM:
                               e.scalar_tensor_tensor(
                                   s1, lamb[:, :], carry[:, CM + 3:CM + 4], s1,
                                   op0=ALU.mult, op1=ALU.add),
                               deps=[w2e, lam0, cmask, c1])
                    prev[("lamused", oc)] = c2
                    dd = s.add("vector", lambda e, s1=s1, s3=s3, s6=s6:
                               e.scalar_tensor_tensor(
                                   s3, s6, 1.0, s1, op0=ALU.add, op1=ALU.mult),
                               deps=[er, c2])
                    prev[("dd", oc)] = dd
                    prev[("t2", oc)] = (s2, s3, c1, ck, dd)

                def emit_tail2a(oc):
                    if ("t2", oc) not in prev:
                        return
                    (s2, s3, c1, ck, dd) = prev[("t2", oc)]
                    lnd = s.add("scalar", lambda e, s3=s3: e.activation(s3, s3, AF.Ln),
                                deps=[dd])
                    invd = s.add("scalar", lambda e, s3=s3: e.activation(
                        s3, s3, AF.Exp, scale=-1.0), deps=[lnd])
                    prev[("invd", oc)] = invd

                def emit_tail2b(oc):
                    if ("t2", oc) not in prev:
                        return
                    (s2, s3, c1, ck, dd) = prev.pop(("t2", oc))
                    invd = prev.pop(("invd", oc))
                    rw = s.add("vector", lambda e, s2=s2, s3=s3, oc=oc: e.tensor_mul(
                        rwkvT[:, oc * T:(oc + 1) * T], s2, s3),
                        deps=[invd, c1, ck])
                    prev[("rw", oc)] = rw
                    rwkv_ops[oc] = rw

                for oc in range(NCC):
                    par = oc % 2
                    s1 = sl1[:, par * T:par * T + T]
                    s2 = sl2[:, par * T:par * T + T]
                    s3 = sl3[:, par * T:par * T + T]
                    s6 = sl6[:, par * T:par * T + T]
                    trash = rwkvT[:, oc * T:(oc + 1) * T]
                    ecol = stat[:, ST_E + oc:ST_E + oc + 1]
                    negecol = stat[:, ST_NEGE + oc:ST_NEGE + oc + 1]
                    n2047col = stat[:, ST_N2047E + oc:ST_N2047E + oc + 1]
                    fwcol = stat[:, ST_FW + oc:ST_FW + oc + 1]
                    lamcol = stat[:, ST_LAM + oc:ST_LAM + oc + 1]
                    base = (oc % 2) * 2 * C
                    CM = 16 + 4 * par

                    wdeps = [prev.get(("rmm", oc - 2))] if oc >= 2 else []
                    wk_d = s.add("gpsimd", lambda e, oc=oc, base=base: e.dma_start(
                        out=wbuf[:, base:base + C].rearrange("p (ic m) -> p ic m", ic=NCC),
                        in_=wkT[:, oc * 128:(oc + 1) * 128]
                        .rearrange("(ic p) m -> p ic m", p=128)), deps=wdeps, dma=True, sem="w")
                    wv_d = s.add("gpsimd", lambda e, oc=oc, base=base: e.dma_start(
                        out=wbuf[:, base + C:base + 2 * C].rearrange("p (ic m) -> p ic m", ic=NCC),
                        in_=wvT[:, oc * 128:(oc + 1) * 128]
                        .rearrange("(ic p) m -> p ic m", p=128)), deps=wdeps, dma=True, sem="w")
                    o_iota = s.add("gpsimd", lambda e, s3=s3: e.iota(
                        s3, [[1, T]], base=0, channel_multiplier=0,
                        allow_small_or_imprecise_dtypes=True),
                        deps=[prev.get(("rw", oc - 2))])


                    def emit_mms_h(e, ps, woff, h, base=base, po=0):
                        for qt in range(2 * h, 2 * h + 2):
                            for ic in range(NCC):
                                ins = e.matmul(
                                    ps[:, po + (qt - 2 * h) * 512:po + (qt - 2 * h + 1) * 512],
                                    wbuf[:, base + woff + ic * 128:base + woff + ic * 128 + 128],
                                    xmT[:, ic * T + qt * 512:ic * T + (qt + 1) * 512],
                                    start=(ic == 0), stop=(ic == NCC - 1))
                        return ins

                    kmm_h = [None, None]
                    vmm_h = [None, None]
                    rmm_h = [None, None]
                    ek_h = [None, None]
                    vev_h = [None, None]
                    er_h = [None, None]
                    for h in (0, 1):
                        d = [wk_d] + mix_done
                        pk = prev.get(("ek", oc - 1, h))
                        if pk is not None:
                            d.append(pk)
                        kmm_h[h] = s.add("tensor",
                                         lambda e, h=h, f=emit_mms_h:
                                         f(e, psA, 0, h, po=h * 1024),
                                         deps=d)
                        ek_h[h] = s.add("scalar", lambda e, s1=s1, h=h: e.activation(
                            s1[:, h * 1024:(h + 1) * 1024],
                            psA[:, h * 1024:(h + 1) * 1024], AF.Exp),
                            deps=[kmm_h[h], prev.get(("dd", oc - 2))])
                        prev[("ek", oc, h)] = ek_h[h]
                    wr_d = s.add("gpsimd", lambda e, oc=oc, base=base: e.dma_start(
                        out=wbuf[:, base:base + C].rearrange("p (ic m) -> p ic m", ic=NCC),
                        in_=wrT[:, oc * 128:(oc + 1) * 128]
                        .rearrange("(ic p) m -> p ic m", p=128)),
                        deps=[kmm_h[1]], dma=True, sem="w")
                    emit_cc(oc - 1)
                    emit_tail(oc - 1)
                    for h in (0, 1):
                        d = [wv_d]
                        pv = prev.get(("er", oc - 1, h))
                        if pv is not None:
                            d.append(pv)
                        vmm_h[h] = s.add("tensor",
                                         lambda e, h=h, f=emit_mms_h:
                                         f(e, psB, C, h, po=h * 1024),
                                         deps=d)
                        vev_h[h] = s.add("scalar", lambda e, s2=s2, h=h: e.activation(
                            s2[:, h * 1024:(h + 1) * 1024],
                            psB[:, h * 1024:(h + 1) * 1024], AF.Identity),
                            deps=[vmm_h[h], prev.get(("rw", oc - 2))])
                        prev[("vev", oc, h)] = vev_h[h]
                    for h in (0, 1):
                        d = [wr_d, vev_h[h]]
                        rmm_h[h] = s.add("tensor",
                                         lambda e, h=h, f=emit_mms_h:
                                         f(e, psB, 0, h, po=h * 1024),
                                         deps=d)
                        er_h[h] = s.add("scalar", lambda e, s6=s6, h=h: e.activation(
                            s6[:, h * 1024:(h + 1) * 1024],
                            psB[:, h * 1024:(h + 1) * 1024], AF.Exp, scale=-1.0),
                            deps=[rmm_h[h], prev.get(("dd", oc - 2))])
                        prev[("er", oc, h)] = er_h[h]
                    ek = ek_h[1]
                    vev = vev_h[1]
                    er = er_h[1]
                    prev[("er", oc)] = er
                    prev[("vev", oc)] = vev
                    prev[("rmm", oc)] = rmm_h[1]
                    lam0 = s.add("scalar", lambda e, negecol=negecol, s3=s3: e.activation(
                        lamb[:, :], s3, AF.Exp, scale=negecol),
                        deps=[o_iota, o_nege, prev.get(("lamused", oc - 1))])
                    emit_tail2a(oc - 1)
                    clp = s.add("vector", lambda e, s1=s1: e.tensor_scalar_min(s1, s1, E60),
                                deps=[ek, prev.get(("dd", oc - 2))])
                    kv = s.add("vector", lambda e, s1=s1, s2=s2: e.tensor_mul(s2, s1, s2),
                               deps=[clp, vev])
                    emit_tail2b(oc - 1)
                    lrev = s.add("scalar", lambda e, s3=s3, ecol=ecol, n2047col=n2047col:
                                 e.activation(s3, s3, AF.Exp,
                                              scale=ecol, bias=n2047col),
                                 deps=[o_iota, o_e, o_n2047, lam0])
                    ckv = s.add("vector", lambda e, s2=s2, s3=s3, trash=trash, oc=oc:
                                e.scalar_tensor_tensor(
                                    trash, s2, 1.0, s3, op0=ALU.mult, op1=ALU.mult,
                                    accum_out=carry[:, 2 * oc:2 * oc + 1]),
                                deps=[kv, lrev, o_m5])
                    ck = s.add("vector", lambda e, s1=s1, s3=s3, trash=trash, oc=oc:
                               e.scalar_tensor_tensor(
                                   trash, s1, 1.0, s3, op0=ALU.mult, op1=ALU.mult,
                                   accum_out=carry[:, 2 * oc + 1:2 * oc + 2]),
                               deps=[ckv])
                    xs = f"x{oc}"
                    snd = s.add("sync", lambda e, oc=oc: e.dma_start(
                        out=cc1_snd[oc].ap(), in_=carry[:, 2 * oc:2 * oc + 2]),
                        deps=[ck], dma=True, sem=xs)
                    prev[("ccargs", oc)] = (snd, CM, xs)
                    sckv = s.add("vector", lambda e, s2=s2, s3=s3, lamcol=lamcol:
                                 e.tensor_tensor_scan(s3, lamcol.broadcast_to([128, T]),
                                                      s2, 0.0, op0=ALU.mult, op1=ALU.add),
                                 deps=[kv, ckv, ck, o_lam, lrev])
                    w1 = s.add("vector", lambda e, s2=s2, s3=s3, fwcol=fwcol:
                               e.scalar_tensor_tensor(
                                   s2[:, 1:T], s2[:, 1:T], fwcol, s3[:, 0:T - 1],
                                   op0=ALU.mult, op1=ALU.add), deps=[sckv, ck])
                    w1b = s.add("vector", lambda e, s2=s2, fwcol=fwcol:
                                e.tensor_scalar_mul(s2[:, 0:1], s2[:, 0:1], fwcol),
                                deps=[w1])
                    sck = s.add("vector", lambda e, s1=s1, s3=s3, lamcol=lamcol:
                                e.tensor_tensor_scan(s3, lamcol.broadcast_to([128, T]),
                                                     s1, 0.0, op0=ALU.mult, op1=ALU.add),
                                deps=[w1, ck])
                    w2 = s.add("vector", lambda e, s1=s1, s3=s3, fwcol=fwcol:
                               e.scalar_tensor_tensor(
                                   s1[:, 1:T], s1[:, 1:T], fwcol, s3[:, 0:T - 1],
                                   op0=ALU.mult, op1=ALU.add), deps=[sck])
                    w2b = s.add("vector", lambda e, s1=s1, fwcol=fwcol:
                                e.tensor_scalar_mul(s1[:, 0:1], s1[:, 0:1], fwcol),
                                deps=[w2])
                    w2e = s.add("vector", lambda e, s1=s1: e.tensor_scalar_add(
                        s1, s1, K_EPS), deps=[w2b])
                    prev[("tailargs", oc)] = (s1, s2, s3, s6, lam0, CM,
                                              w1b, w2e, er, ck)
                emit_cc(NCC - 1)
                emit_tail(NCC - 1)
                emit_tail2a(NCC - 1)
                emit_tail2b(NCC - 1)
                s.emit()

            if BUILD_UPTO == 13:
                _tap_out(nc, sems, XR, OUT, rwkv_ops)
                return nc
            with contextlib.ExitStack() as _s3x:
                ec3 = _s3x.enter_context
                wo_sb = ec3(nc.sbuf_tensor("wo_sb", [128, NCC * C], BF16))
                o_ev = ec3(nc.sbuf_tensor("o_ev", [128, 2 * 512], F32))
                psO = ec3(nc.psum_tensor("psO", [128, 2 * 512], F32))
                psT = ec3(nc.psum_tensor("psT", [128, 2 * 512], F32))
                _dummy = 1
                s = Section(nc, sems)
                wo_d = s.add("gpsimd", lambda e: e.dma_start(
                    out=wo_sb.ap().rearrange("p (ic m) -> p ic m", ic=NCC),
                    in_=woT.rearrange("(ic p) m -> p ic m", p=128)), dma=True, sem="w")
                r1_ops = []
                prev_ev = {}
                prev_tr = {}
                prev_add = {}
                k = 0
                for oc in range(NCC):
                    for qt in range(4):
                        sl = k % 2

                        def em_owmm(e, oc=oc, qt=qt, sl=sl):
                            for ic in range(NCC):
                                ins = e.matmul(
                                    psO[:, sl * 512:sl * 512 + 512],
                                    wo_sb[:, ic * C + oc * 128:ic * C + oc * 128 + 128],
                                    rwkvT[:, ic * T + qt * 512:ic * T + (qt + 1) * 512],
                                    start=(ic == 0), stop=(ic == NCC - 1))
                            return ins
                        d = [wo_d] + rwkv_ops
                        if k >= 2:
                            d.append(prev_ev[k - 2])
                        mm = s.add("tensor", em_owmm, deps=d)
                        ev = s.add("vector", lambda e, sl=sl: e.tensor_copy(
                            o_ev[:, sl * 512:sl * 512 + 512], psO[:, sl * 512:sl * 512 + 512]),
                            deps=[mm] + ([prev_tr[k - 2]] if k >= 2 else []))
                        prev_ev[k] = ev

                        def em_tr(e, oc=oc, qt=qt, sl=sl):
                            for j in range(4):
                                ins = e.transpose(
                                    psT[:, sl * 512 + j * 128:sl * 512 + j * 128 + 128],
                                    o_ev[:, sl * 512 + j * 128:sl * 512 + j * 128 + 128],
                                    ident_f[:, :])
                            return ins
                        tr = s.add("tensor", em_tr,
                                   deps=[ev, o_idf] + ([prev_add[k - 2]] if k >= 2 else []))
                        prev_tr[k] = tr

                        def em_add(e, oc=oc, qt=qt, sl=sl):
                            return e.tensor_add(
                                XR.ap().rearrange("p (n c) -> p n c", n=NT)[
                                    :, 4 * qt:4 * qt + 4, oc * 128:oc * 128 + 128],
                                XR.ap().rearrange("p (n c) -> p n c", n=NT)[
                                    :, 4 * qt:4 * qt + 4, oc * 128:oc * 128 + 128],
                                psT.ap().rearrange("p (b j t) -> p (b j) t", b=2, j=4)[
                                    :, sl * 4:sl * 4 + 4, :])
                        ad = s.add("vector", em_add, deps=[tr])
                        prev_add[k] = ad
                        r1_ops.append(ad)
                        k += 1
                spill = s.add("sync", lambda e: e.dma_start(
                    out=r1_spill.rearrange("(n p) c -> p n c", p=128),
                    in_=XR.ap().rearrange("p (n c) -> p n c", c=C)),
                    deps=list(r1_ops), dma=True, sem="spl")

                xn2o = ec3(nc.sbuf_tensor("xn2o", [128, NCC * T], BF16))
                x2_tile = ec3(nc.sbuf_tensor("x2_tile", [128, 2 * C], BF16))
                ptr2 = ec3(nc.psum_tensor("ptr2", [128, 2 * C], BF16))

                stats2 = []
                prevv = None
                for i in range(NT):
                    d = [r1_ops[7 * 4 + i // 4]] + ([prevv] if prevv else [])
                    a = s.add("vector", lambda e, i=i: e.bn_stats(
                        st6[:, 0:6], XR[:, i * C:i * C + 512]), deps=d)
                    b = s.add("vector", lambda e, i=i: e.bn_stats(
                        st6[:, 6:12], XR[:, i * C + 512:i * C + 1024]), deps=[a])
                    g = s.add("vector", lambda e: e.bn_aggr(mv[:, 0:2], st6[:, :]), deps=[a, b])
                    cm = s.add("vector", lambda e, i=i: e.tensor_copy(
                        stat[:, ST_MEAN1 + i:ST_MEAN1 + i + 1], mv[:, 0:1]), deps=[g])
                    cv = s.add("vector", lambda e, i=i: e.tensor_copy(
                        stat[:, ST_VAR1 + i:ST_VAR1 + i + 1], mv[:, 1:2]), deps=[g])
                    prevv = cv
                    stats2.extend([cm, cv])
                LNU, IU, V2 = ST_SY, ST_BY, ST_W1
                o1 = s.add("scalar", lambda e: e.activation(
                    stat[:, LNU:LNU + 16], stat[:, ST_VAR1:ST_VAR1 + 16], AF.Ln, bias=EPS),
                    deps=stats2)
                o2 = s.add("scalar", lambda e: e.activation(
                    stat[:, ST_RSTD1:ST_RSTD1 + 16], stat[:, LNU:LNU + 16], AF.Exp, scale=-0.5),
                    deps=[o1])
                o3 = s.add("scalar", lambda e: e.activation(
                    stat[:, IU:IU + 16], stat[:, LNU:LNU + 16], AF.Exp, scale=-1.0), deps=[o1])
                o4 = s.add("vector", lambda e: e.tensor_mul(
                    stat[:, V2:V2 + 16], stat[:, ST_VAR1:ST_VAR1 + 16], stat[:, IU:IU + 16]),
                    deps=[o3])
                o5 = s.add("scalar", lambda e: e.activation(
                    stat[:, V2:V2 + 16], stat[:, V2:V2 + 16], AF.Ln, bias=EPS), deps=[o4])
                o6 = s.add("scalar", lambda e: e.activation(
                    stat[:, V2:V2 + 16], stat[:, V2:V2 + 16], AF.Exp, scale=-0.5), deps=[o5])
                o7 = s.add("vector", lambda e: e.tensor_mul(
                    stat[:, ST_SY:ST_SY + 16], stat[:, ST_RSTD1:ST_RSTD1 + 16],
                    stat[:, V2:V2 + 16]), deps=[o6, o2, o3])
                o8 = s.add("vector", lambda e: e.tensor_mul(
                    stat[:, ST_BY:ST_BY + 16], stat[:, ST_MEAN1:ST_MEAN1 + 16],
                    stat[:, ST_SY:ST_SY + 16]), deps=[o7])
                o9 = s.add("vector", lambda e: e.tensor_scalar_mul(
                    stat[:, ST_BY:ST_BY + 16], stat[:, ST_BY:ST_BY + 16], -1.0), deps=[o8])
                oW = s.add("vector", lambda e: e.tensor_scalar(
                    stat[:, ST_W1:ST_W1 + 16], stat[:, ST_RSTD1:ST_RSTD1 + 16], 1.0, 1.0,
                    op0=ALU.mult, op1=ALU.add), deps=[o8, o6])
                oN = s.add("vector", lambda e: e.tensor_mul(
                    stat[:, ST_NMR:ST_NMR + 16], stat[:, ST_MEAN1:ST_MEAN1 + 16],
                    stat[:, ST_RSTD1:ST_RSTD1 + 16]), deps=[o7])
                oN2 = s.add("vector", lambda e: e.tensor_scalar_mul(
                    stat[:, ST_NMR:ST_NMR + 16], stat[:, ST_NMR:ST_NMR + 16], -1.0), deps=[oN])
                tr2 = [None] * NT
                cp2 = [None] * NT
                ap2 = [None] * NT
                snd2h = {}
                cc2h = {}
                for i in range(NT):
                    d = [o7, o9]
                    if i >= 2:
                        d.append(tr2[i - 2])
                    ap2[i] = s.add("scalar", lambda e, i=i: e.activation(
                        x2_tile[:, (i % 2) * C:(i % 2) * C + C],
                        XR[:, i * C:(i + 1) * C], AF.Identity,
                        bias=stat[:, ST_BY + i:ST_BY + i + 1],
                        scale=stat[:, ST_SY + i:ST_SY + i + 1]), deps=d)

                    def em_tr2(e, i=i):
                        for cc in range(NCC):
                            ins = e.transpose(
                                ptr2[:, (i % 2) * C + cc * 128:(i % 2) * C + cc * 128 + 128],
                                x2_tile[:, (i % 2) * C + cc * 128:(i % 2) * C + cc * 128 + 128],
                                ident_b[:, :])
                        return ins
                    d = [ap2[i]]
                    if i >= 2:
                        d.append(cp2[i - 2])
                    tr2[i] = s.add("tensor", em_tr2, deps=d)
                    cp2[i] = s.add("vector", lambda e, i=i: e.tensor_copy(
                        xn2o.ap().rearrange("p (cc t) -> p cc t", cc=NCC)[:, :, i * 128:(i + 1) * 128],
                        ptr2.ap().rearrange("p (b cc t) -> p b cc t", b=2, cc=NCC)[:, i % 2]),
                        deps=[tr2[i]])
                    if i in (7, 15):
                        hh = i // 8
                        affh = []
                        for cc in range(NCC):
                            affh.append(s.add("vector", lambda e, cc=cc, hh=hh: e.tensor_scalar(
                                xn2o[:, cc * T + hh * 1024:cc * T + hh * 1024 + 1024],
                                xn2o[:, cc * T + hh * 1024:cc * T + hh * 1024 + 1024],
                                vp[:, VP_NW + cc:VP_NW + cc + 1],
                                vp[:, VP_NB + cc:VP_NB + cc + 1],
                                op0=ALU.mult, op1=ALU.add),
                                deps=[o_vp] + cp2[hh * 8:hh * 8 + 8]))
                        snd2h[hh] = s.add("sync", lambda e, hh=hh: e.dma_start(
                            out=cc2_snd[hh].ap().rearrange("p (cc t) -> p cc t", cc=NCC),
                            in_=xn2o.ap().rearrange("p (cc t) -> p cc t", cc=NCC)[
                                :, :, hh * 1024:hh * 1024 + 1024]),
                            deps=affh, dma=True, sem="io")
                        cc2h[hh] = s.add("gpsimd", lambda e, hh=hh: e.collective_compute(
                            "AllGather", ALU.bypass, replica_groups=RG,
                            ins=[cc2_snd[hh].ap().opt()], outs=[cc2_rcv[hh].ap().opt()]),
                            deps=[snd2h[hh], last_coll[0] if hh == 0 else cc2h[0]],
                            coll=True, sem="cc")
                cc2 = cc2h[1]
                s.emit()


        if BUILD_UPTO == 1:
            _tap_out(nc, sems, XR, OUT, r1_ops)
            return nc

        qk_ops = []
        v_ops = []
        with contextlib.ExitStack() as _s5x:
            ec5 = _s5x.enter_context
            Vt = ec5(nc.sbuf_tensor("Vt", [128, 32 * 512], BF16))
            qmkm = ec5(nc.sbuf_tensor("qmkm", [128, 20], F32))
            stt_row = ec5(nc.sbuf_tensor("stt_row", [1, 4], F32))
            _dummy = 1
            with contextlib.ExitStack() as _s5y:
                ec5y = _s5y.enter_context
                xn2f = ec5y(nc.sbuf_tensor("xn2f", [128, NCC * TFULL], BF16))
                pw_sb = ec5y(nc.sbuf_tensor("pw_sb", [128, 2 * 8 * 512], BF16))
                vw_sb = ec5y(nc.sbuf_tensor("vw_sb", [128, 8 * 512], BF16))
                vb_sb = ec5y(nc.sbuf_tensor("vb_sb", [1, 512], BF16))
                _dummy = 1
                s = Section(nc, sems)
                wq_d = s.add("gpsimd", lambda e: e.dma_start(
                    out=pw_sb[:, 0:8 * 512].rearrange("p (ic m) -> p ic m", ic=NCC),
                    in_=qwT.rearrange("(ic p) m -> p ic m", p=128)), dma=True, sem="w")
                wk_d = s.add("gpsimd", lambda e: e.dma_start(
                    out=pw_sb[:, 8 * 512:16 * 512].rearrange("p (ic m) -> p ic m", ic=NCC),
                    in_=kwT.rearrange("(ic p) m -> p ic m", p=128)), dma=True, sem="w")
                wv_d = s.add("gpsimd", lambda e: e.dma_start(
                    out=vw_sb.ap().rearrange("p (ic m) -> p ic m", ic=NCC),
                    in_=vwT.rearrange("(ic p) m -> p ic m", p=128)), dma=True, sem="w")
                vb_d = s.add("gpsimd", lambda e: e.dma_start(
                    out=vb_sb[:, :], in_=vb_row), dma=True, sem="w")
                rcvh = {0: [], 1: []}
                for a in range(2):
                    for r in range(2):
                        dd = [cc2h[a], spill]
                        if a == 0:
                            dd.append(snd2h[1])
                        rcvh[a].append(s.add("sync", lambda e, a=a, r=r: e.dma_start(
                            out=xn2f.ap().rearrange("p (cc t) -> p cc t", cc=NCC)[
                                :, :, r * T + a * 1024:r * T + a * 1024 + 1024],
                            in_=cc2_rcv[a].ap()[r].rearrange("p (cc t) -> p cc t", cc=NCC)),
                            deps=dd, dma=True, sem="io" if r == 0 else "io2"))

                with contextlib.ExitStack() as _s5p:
                    psP = _s5p.enter_context(nc.psum_tensor("psP", [128, 2 * 512], F32))
                    prev_ev = {}
                    red_ops = []
                    k5 = 0
                    units = []
                    for half in range(2):
                        qts = (0, 1, 4, 5) if half == 0 else (2, 3, 6, 7)
                        tcs = (list(range(0, 8)) + list(range(16, 24)) if half == 0
                               else list(range(8, 16)) + list(range(24, 32)))
                        for which, woff, dst, bcol in (("q", 0, QT, VP_QB),
                                                       ("k", 8 * 512, KT, VP_KB)):
                            for j in range(4):
                                for qt in qts:
                                    units.append(("qk", half, which, woff, dst, bcol, j, qt))
                                if half == 1:
                                    units.append(("red", which, j))
                        for tc in tcs:
                            units.append(("v", half, tc))
                    qk_by = {}
                    for u in units:
                        if u[0] == "red":
                            _, which, j = u
                            rbase = 4 if which == "q" else 12
                            xbase = (0 if which == "q" else 16384) + j * 4096
                            red_ops.append(s.add("vector", lambda e, rbase=rbase, j=j, xbase=xbase:
                                e.tensor_reduce(
                                    qmkm[:, rbase + j:rbase + j + 1],
                                    XRb[:, xbase:xbase + 4096],
                                    axis=AX.X, op=ALU.max, apply_absolute_value=True),
                                deps=qk_by[(which, j)]))
                            continue
                        sl = k5 % 2
                        if u[0] == "qk":
                            _, half, which, woff, dst, bcol, j, qt = u
                            nic = NCC >> (qt // 2)

                            def em_p(e, j=j, qt=qt, nic=nic, sl=sl, woff=woff):
                                for ic in range(nic):
                                    ins = e.matmul(
                                        psP[:, sl * 512:sl * 512 + 512],
                                        pw_sb[:, woff + ic * 512 + j * 128:woff + ic * 512 + j * 128 + 128],
                                        xn2f[:, ic * TFULL + qt * 512:ic * TFULL + (qt + 1) * 512],
                                        start=(ic == 0), stop=(ic == nic - 1))
                                return ins
                            d = rcvh[half] + [wq_d if which == "q" else wk_d, spill]
                            if k5 >= 2:
                                d.append(prev_ev[k5 - 2])
                            mm = s.add("tensor", em_p, deps=d)
                            ev = s.add("vector", lambda e, j=j, qt=qt, sl=sl, dst=dst, bcol=bcol: e.tensor_scalar(
                                dst(j)[:, qt * 512:(qt + 1) * 512],
                                psP[:, sl * 512:sl * 512 + 512],
                                vp[:, bcol + j:bcol + j + 1], None, op0=ALU.add),
                                deps=[mm, o_vp])
                            qk_ops.append(ev)
                            qk_by.setdefault((which, j), []).append(ev)
                        else:
                            _, half, tc = u
                            nic = NCC >> (tc // 8)

                            def em_v(e, tc=tc, nic=nic, sl=sl):
                                for ic in range(nic):
                                    e.matmul(
                                        psP[:, sl * 512:sl * 512 + 512],
                                        xn2f[:, ic * TFULL + tc * 128:ic * TFULL + tc * 128 + 128],
                                        vw_sb[:, ic * 512:(ic + 1) * 512],
                                        start=(ic == 0), stop=False)
                                ins = e.matmul(psP[:, sl * 512:sl * 512 + 512],
                                               ones_r1[:, :], vb_sb[:, :],
                                               start=False, stop=True)
                                return ins
                            d = rcvh[half] + [wv_d, vb_d, o_m1]
                            if k5 >= 2:
                                d.append(prev_ev[k5 - 2])
                            mm = s.add("tensor", em_v, deps=d)
                            ev = s.add("vector", lambda e, tc=tc, sl=sl: e.tensor_copy(
                                Vt[:, tc * 512:(tc + 1) * 512], psP[:, sl * 512:sl * 512 + 512]),
                                deps=[mm])
                            v_ops.append(ev)
                        prev_ev[k5] = ev
                        k5 += 1
                    g1 = s.add("vector", lambda e: e.tensor_reduce(
                        qmkm[:, 0:1], qmkm[:, 4:8], axis=AX.X, op=ALU.max),
                        deps=qk_ops + red_ops)
                    g2 = s.add("vector", lambda e: e.tensor_reduce(
                        qmkm[:, 1:2], qmkm[:, 12:16], axis=AX.X, op=ALU.max),
                        deps=qk_ops + red_ops + [g1])
                    with nc.psum_tensor("psst", [128, 132], F32) as psst:
                        t1_ = s.add("tensor", lambda e: e.transpose(
                            psst[0:2, 0:128], qmkm[:, 0:2], ident_f[:, :]),
                            deps=[g1, g2, o_idf])
                        r1_ = s.add("vector", lambda e: e.tensor_reduce(
                            qmkm[0:2, 0:1], psst[0:2, 0:128], axis=AX.X, op=ALU.max),
                            deps=[t1_, g2])
                        t2_ = s.add("tensor", lambda e: e.transpose(
                            psst[0:1, 128:130], qmkm[0:2, 0:1], ident_f[0:2, 0:2]),
                            deps=[r1_])
                        m0_ = s.add("vector", lambda e: e.tensor_copy(
                            stt_row[0:1, 2:4], psst[0:1, 128:130]), deps=[t2_])
                        m1_ = s.add("vector", lambda e: e.tensor_mul(
                            stt_row[0:1, 0:1], stt_row[0:1, 2:3], stt_row[0:1, 3:4]),
                            deps=[m0_])
                        m2_ = s.add("vector", lambda e: e.tensor_scalar_mul(
                            stt_row[0:1, 0:1], stt_row[0:1, 0:1], -4.0), deps=[m1_])
                        b1_ = s.add("tensor", lambda e: e.matmul(
                            psst[:, 130:131], ones_r1f[:, :], stt_row[0:1, 0:1],
                            start=True, stop=True), deps=[m2_, o_m2])
                        b2_ = s.add("vector", lambda e: e.tensor_copy(
                            cbias[:, 0:1], psst[:, 130:131]), deps=[b1_])
                    s.emit()

            attn_oT = ec5(nc.sbuf_tensor("attn_oT", [128, 4 * TFULL], BF16))

            if BUILD_UPTO == 2:
                _tap_out(nc, sems, XR, OUT, qk_ops + v_ops + [b2_])
                return nc

            with contextlib.ExitStack() as _s6x:
                ec6 = _s6x.enter_context
                recipb = ec6(nc.sbuf_tensor("recipb", [128, 2 * 512], F32))
                rrow = ec6(nc.sbuf_tensor("rrow", [1, 2048], F32))
                ptb = ec6(nc.sbuf_tensor("ptb", [128, 2 * 2048], BF16))
                praw = ec6(nc.sbuf_tensor("praw", [128, 2 * 512], F32))
                pssc = ec6(nc.psum_tensor("pssc", [128, 2048], F32))
                psPV = ec6(nc.psum_tensor("psPV", [128, 2 * 512], F32))
                psDB = ec6(nc.psum_tensor("psDB", [128, 2 * 512], F32))
                _dummy = 1
                s = Section(nc, sems)
                attn_ops = []
                hist = {}
                NQ = 16

                def emit_ep_early(p, qt):
                    den_last = hist["last_den"]
                    pv_last = hist["last_pv"]
                    cr = s.add("vector", lambda e: e.tensor_copy(
                        praw[:, :], psPV[:, :]), deps=[pv_last, hist.get("raw_used")])
                    hist["raw_copy"] = cr
                    cps = []
                    for half, r0, r1_ in ((0, 0, 64), (1, 32, 96)):
                        ca = s.add("vector", lambda e, r0=r0, half=half: e.tensor_copy(
                            rrow[0:1, half * 1024:half * 1024 + 512],
                            psDB[r0:r0 + 1, 0:512]), deps=[den_last, hist.get("rr_used")])
                        cb = s.add("vector", lambda e, r1_=r1_, half=half: e.tensor_add(
                            rrow[0:1, half * 1024:half * 1024 + 512],
                            rrow[0:1, half * 1024:half * 1024 + 512],
                            psDB[r1_:r1_ + 1, 0:512]), deps=[ca])
                        cps.append(cb)
                    hist["last_rr"] = cps[-1]
                    hist["den_cp"] = cps

                def emit_ep_late_half(p, qt, half):
                    jj = 2 * p + half
                    rr1 = s.add("scalar", lambda e, half=half: e.activation(
                        rrow[0:1, half * 1024:half * 1024 + 512],
                        rrow[0:1, half * 1024:half * 1024 + 512], AF.Ln),
                        deps=[hist["den_cp"][half]])
                    rr2 = s.add("scalar", lambda e, half=half: e.activation(
                        rrow[0:1, half * 1024:half * 1024 + 512],
                        rrow[0:1, half * 1024:half * 1024 + 512],
                        AF.Exp, scale=-1.0), deps=[rr1])
                    d1 = s.add("sync", lambda e, half=half: e.dma_start(
                        out=rr_dram[half].ap(),
                        in_=rrow[0:1, half * 1024:half * 1024 + 512]),
                        deps=[rr2], dma=True, sem="rb")
                    d2 = s.add("sync", lambda e, half=half: e.dma_start(
                        out=recipb[:, half * 512:half * 512 + 512],
                        in_=rr_dram[half].ap().broadcast_to([128, 512])),
                        deps=[d1, hist.get(("rb_used", half))], dma=True, sem="rb")
                    ev = s.add("vector", lambda e, jj=jj, qt=qt, half=half: e.tensor_mul(
                        attn_oT[:, jj * TFULL + qt * 512:jj * TFULL + (qt + 1) * 512],
                        praw[:, half * 512:half * 512 + 512],
                        recipb[:, half * 512:half * 512 + 512]),
                        deps=[d2, hist["raw_copy"]])
                    attn_ops.append(ev)
                    hist[("rb_used", half)] = ev
                    if half == 1:
                        hist["raw_used"] = ev
                        hist["rr_used"] = d1

                def emit_ep_late(p, qt):
                    emit_ep_late_half(p, qt, 0)
                    emit_ep_late_half(p, qt, 1)

                pending = None
                NQ2 = 32
                units = [(p, qt) for p in range(2) for qt in range(8)]
                for p, qt in units:
                    j0, j1 = 2 * p, 2 * p + 1
                    exp_ops = [None] * NQ2
                    qk_mms = [None] * NQ2
                    pv_mms = [None] * NQ2
                    den_mms = {}

                    def em_qk(e, q, j0=j0, j1=j1, qt=qt):
                        for i, jj in enumerate((j0, j1)):
                            ins = e.matmul(
                                pssc[:, (q % 2) * 1024 + i * 512:
                                     (q % 2) * 1024 + (i + 1) * 512],
                                KT(jj)[:, q * 128:(q + 1) * 128],
                                QT(jj)[:, qt * 512:(qt + 1) * 512],
                                start=True, stop=True)
                        return ins

                    def em_pv(e, q, j0=j0, j1=j1):
                        for i, jj in enumerate((j0, j1)):
                            ins = e.matmul(
                                psPV[:, i * 512:(i + 1) * 512],
                                Vt[:, q * 512 + jj * 128:q * 512 + jj * 128 + 128],
                                ptb[:, (q % 4) * 1024 + i * 512:
                                    (q % 4) * 1024 + (i + 1) * 512],
                                start=(q == 0), stop=(q == NQ2 - 1))
                        return ins

                    def em_den(e, qa, qb):
                        first = (qa == 0)
                        last = (qb == NQ2 - 1)
                        for row, kcx, i in ((0, qa, 0), (32, qa, 1),
                                            (64, qb, 0), (96, qb, 1)):
                            ins = e.matmul(
                                psDB[row:row + 1, 0:512], ones_c1[:, :],
                                ptb[:, (kcx % 4) * 1024 + i * 512:
                                    (kcx % 4) * 1024 + (i + 1) * 512],
                                start=first, stop=last, tile_position=(0, row))
                        return ins

                    for q_ in range(NQ2):
                        if q_ >= 2:
                            d = [exp_ops[q_ - 2]]
                        elif hist.get("pexp") is not None:
                            d = [hist["pexp"][q_]]
                        else:
                            d = qk_ops + v_ops + [b2_]
                        qk_mms[q_] = s.add("tensor",
                                           lambda e, q=q_, f=em_qk: f(e, q), deps=d)

                        ex_d = [qk_mms[q_], b2_]
                        if q_ >= 4:
                            ex_d += [pv_mms[q_ - 4], den_mms[(q_ - 4) // 2]]
                        elif hist.get("ppv") is not None:
                            ex_d += [hist["ppv"][q_], hist["pdb"][q_ // 2]]
                        exp_ops[q_] = s.add("scalar", lambda e, q=q_: e.activation(
                            ptb[:, (q % 4) * 1024:(q % 4) * 1024 + 1024],
                            pssc[:, (q % 2) * 1024:(q % 2) * 1024 + 1024],
                            AF.Exp, scale=1.0 / 32.0, bias=cbias[:, 0:1]), deps=ex_d)

                        if q_ >= 1:
                            d = [exp_ops[q_ - 1]]
                            if q_ - 1 == 0 and hist.get("raw_copy") is not None:
                                d.append(hist["raw_copy"])
                            pv_mms[q_ - 1] = s.add("tensor",
                                lambda e, q=q_ - 1, f=em_pv: f(e, q), deps=d)
                            if (q_ - 1) % 2 == 1:
                                bi = (q_ - 1) // 2
                                dd_ = [exp_ops[q_ - 1], exp_ops[q_ - 2], o_m3]
                                if bi == 0 and hist.get("last_rr") is not None:
                                    dd_.append(hist["last_rr"])
                                if bi == 0 and hist.get("rr_used") is not None:
                                    dd_.append(hist.get("rr_used"))
                                den_mms[bi] = s.add("tensor",
                                    lambda e, qa=q_ - 2, qb=q_ - 1, f=em_den:
                                    f(e, qa, qb), deps=dd_)
                        if q_ == 8 and pending is not None:
                            emit_ep_late_half(pending[0], pending[1], 0)
                        if q_ == 20 and pending is not None:
                            done_u = pending
                            emit_ep_late_half(pending[0], pending[1], 1)
                            pending = None
                            if done_u == (0, 7):
                                snd3a = s.add("sync", lambda e: e.dma_start(
                                    out=cc3_snd_a.ap(),
                                    in_=attn_oT[:, 0:2 * TFULL]),
                                    deps=list(attn_ops), dma=True, sem="io")
                                cc3a = s.add("gpsimd", lambda e: e.collective_compute(
                                    "AllGather", ALU.bypass, replica_groups=RG,
                                    ins=[cc3_snd_a.ap().opt()],
                                    outs=[cc3_rcv_a.ap().opt()]),
                                    deps=[snd3a, cc2], coll=True, sem="cc")
                            elif done_u == (1, 3):
                                snd3b1 = s.add("sync", lambda e: e.dma_start(
                                    out=cc3_snd_b1.ap().rearrange(
                                        "p (j t) -> p j t", j=2),
                                    in_=attn_oT.ap().rearrange(
                                        "p (j t) -> p j t", j=4)[:, 2:4, 0:T]),
                                    deps=attn_ops[16:24], dma=True, sem="io")
                                cc3b1 = s.add("gpsimd", lambda e: e.collective_compute(
                                    "AllGather", ALU.bypass, replica_groups=RG,
                                    ins=[cc3_snd_b1.ap().opt()],
                                    outs=[cc3_rcv_b1.ap().opt()]),
                                    deps=[snd3b1, cc3a], coll=True, sem="cc")
                    pv_mms[NQ2 - 1] = s.add("tensor",
                        lambda e, q=NQ2 - 1, f=em_pv: f(e, q),
                        deps=[exp_ops[NQ2 - 1]])
                    den_mms[(NQ2 - 1) // 2] = s.add("tensor",
                        lambda e, qa=NQ2 - 2, qb=NQ2 - 1, f=em_den: f(e, qa, qb),
                        deps=[exp_ops[NQ2 - 1], exp_ops[NQ2 - 2], o_m3])
                    hist["pexp"] = (exp_ops[NQ2 - 2], exp_ops[NQ2 - 1])
                    hist["ppv"] = [pv_mms[NQ2 - 4 + i] for i in range(4)]
                    hist["pdb"] = [den_mms[(NQ2 - 4) // 2], den_mms[(NQ2 - 2) // 2]]
                    hist["last_pv"] = pv_mms[NQ2 - 1]
                    hist["last_den"] = den_mms[(NQ2 - 1) // 2]
                    emit_ep_early(p, qt)
                    pending = (p, qt)
                emit_ep_late(*pending)
                s.emit()

            if BUILD_UPTO == 3:
                _tap_out(nc, sems, attn_oT.ap().bitcast(F32), OUT, attn_ops, ntiles=8)
                return nc

            s = Section(nc, sems)
            snd3b2 = s.add("sync", lambda e: e.dma_start(
                out=cc3_snd_b2.ap().rearrange("p (j t) -> p j t", j=2),
                in_=attn_oT.ap().rearrange("p (j t) -> p j t", j=4)[:, 2:4, T:2 * T]),
                deps=attn_ops, dma=True, sem="io")
            cc3b2 = s.add("gpsimd", lambda e: e.collective_compute(
                "AllGather", ALU.bypass, replica_groups=RG,
                ins=[cc3_snd_b2.ap().opt()], outs=[cc3_rcv_b2.ap().opt()]),
                deps=[snd3b2, cc3b1], coll=True, sem="cc")
            s.emit()

        with contextlib.ExitStack() as _s7x:
                ec7 = _s7x.enter_context
                attn_all = ec7(nc.sbuf_tensor("attn_all", [128, NCC * T], BF16))
                mbufL = ec7(nc.sbuf_tensor("mbufL", [128, 2 * T], BF16))
                mbufR = ec7(nc.sbuf_tensor("mbufR", [128, 2 * T], BF16))
                ow_sb = ec7(nc.sbuf_tensor("ow_sb", [128, 2 * NCC * C], BF16))
                ob_sb = ec7(nc.sbuf_tensor("ob_sb", [128, 16], F32))
                o_ev2 = ec7(nc.sbuf_tensor("o_ev2", [128, 8 * 512], F32))
                wsb = ec7(nc.sbuf_tensor("wsb", [128, C], F32))
                fin = ec7(nc.sbuf_tensor("fin", [128, 2 * C], F32))
                psO2 = ec7(nc.psum_tensor("psO2", [128, 2 * 512], F32))
                psTk = ec7(nc.psum_tensor("psTk", [128, 2 * C], F32))
                _dummy = 1
                s = Section(nc, sems)
                ow_d0 = s.add("gpsimd", lambda e: e.dma_start(
                    out=ow_sb[:, 0:NCC * C].rearrange("p (ic m) -> p ic m", ic=NCC),
                    in_=owT0.rearrange("(ic p) m -> p ic m", p=128)),
                    deps=[snd3b2], dma=True, sem="w")
                ow_d1 = s.add("gpsimd", lambda e: e.dma_start(
                    out=ow_sb[:, NCC * C:2 * NCC * C].rearrange("p (ic m) -> p ic m", ic=NCC),
                    in_=owT1.rearrange("(ic p) m -> p ic m", p=128)),
                    deps=[snd3b2], dma=True, sem="w")
                ob_d = s.add("gpsimd", lambda e: e.dma_start(
                    out=ob_sb[:, :], in_=ob_cols), deps=[snd3b2], dma=True, sem="w")
                mrg = []
                mrg_by = {}
                for idx, hh in enumerate((0, 1, 4, 5, 2, 3, 6, 7)):
                    r, j = hh // 4, hh % 4
                    pp = idx % 2
                    ms = "io" if pp == 0 else "io2"
                    if j < 2:
                        src_l = cc3_rcv_a.ap()[r][:, j * TFULL:j * TFULL + T]
                        src_r = cc3_rcv_a.ap()[r][:, j * TFULL + T:j * TFULL + 2 * T]
                        dep_l, dep_r = cc3a, cc3a
                    else:
                        src_l = cc3_rcv_b1.ap()[r][:, (j - 2) * T:(j - 1) * T]
                        src_r = cc3_rcv_b2.ap()[r][:, (j - 2) * T:(j - 1) * T]
                        dep_l, dep_r = cc3b1, cc3b2
                    prevm = mrg[idx - 2] if idx >= 2 else None
                    dl = s.add("sync", lambda e, src_l=src_l, pp=pp: e.dma_start(
                        out=mbufL[:, pp * T:(pp + 1) * T], in_=src_l),
                        deps=[dep_l, snd3b2, prevm], dma=True, sem=ms)
                    dr = s.add("sync", lambda e, src_r=src_r, pp=pp: e.dma_start(
                        out=mbufR[:, pp * T:(pp + 1) * T], in_=src_r),
                        deps=[dep_r, snd3b2, prevm], dma=True, sem=ms)
                    m1 = s.add("vector", lambda e, hh=hh, pp=pp: e.tensor_sub(
                        attn_all[:, hh * T:(hh + 1) * T],
                        mbufR[:, pp * T:(pp + 1) * T], mbufL[:, pp * T:(pp + 1) * T]),
                        deps=[dl, dr])
                    m2 = s.add("vector", lambda e, hh=hh, pp=pp: e.scalar_tensor_tensor(
                        attn_all[:, hh * T:(hh + 1) * T],
                        attn_all[:, hh * T:(hh + 1) * T],
                        vp[:, VP_HM:VP_HM + 1], mbufL[:, pp * T:(pp + 1) * T],
                        op0=ALU.mult, op1=ALU.add), deps=[m1, o_vp])
                    mrg.append(m2)
                    mrg_by[hh] = m2
                rel = s.add("sync", lambda e: e.dma_start(
                    out=XR.ap().rearrange("p (n c) -> p n c", c=C),
                    in_=r1_spill.rearrange("(n p) c -> p n c", p=128)),
                    deps=attn_ops, dma=True, sem="spl")
                prev_ev = {}
                prev_f = {}
                last_round_trbs = []
                out_ops = []
                ko = 0
                for eo in range(2):
                    for qt2 in range(2):
                        ev_blocks = []
                        round_trbs = []
                        for oc in range(NCC):
                            sl = ko % 2

                            def em_omm(ics, first, last, e=None, eo=eo, qt2=qt2,
                                       oc=oc, sl=sl):
                                for k_, ic in enumerate(ics):
                                    ins = e.matmul(
                                        psO2[:, sl * 512:sl * 512 + 512],
                                        ow_sb[:, eo * NCC * C + ic * C + oc * 128:
                                              eo * NCC * C + ic * C + oc * 128 + 128],
                                        attn_all[:, ic * T + eo * C + qt2 * 512:
                                                 ic * T + eo * C + (qt2 + 1) * 512],
                                        start=(first and k_ == 0),
                                        stop=(last and k_ == len(ics) - 1))
                                return ins
                            ow_d = ow_d0 if eo == 0 else ow_d1
                            d = [mrg_by[0], mrg_by[1], mrg_by[4], mrg_by[5], ow_d]
                            if ko >= 2:
                                d.append(prev_ev[ko - 2])
                            mma = s.add("tensor", lambda e, f=em_omm: f(
                                (0, 1, 4, 5), True, False, e), deps=d)
                            mmb = s.add("tensor", lambda e, f=em_omm: f(
                                (2, 3, 6, 7), False, True, e),
                                deps=[mma, mrg_by[2], mrg_by[3], mrg_by[6], mrg_by[7]])
                            ev = s.add("vector", lambda e, oc=oc, eo=eo, sl=sl: e.tensor_scalar(
                                o_ev2[:, oc * 512:(oc + 1) * 512],
                                psO2[:, sl * 512:sl * 512 + 512],
                                ob_sb[:, eo * 8 + oc:eo * 8 + oc + 1], None, op0=ALU.add),
                                deps=[mmb, ob_d] + last_round_trbs)
                            prev_ev[ko] = ev
                            ev_blocks.append(ev)
                            ko += 1
                        for blk_ in range(4):
                            ti = eo * 8 + qt2 * 4 + blk_
                            slt = ti % 2

                            def em_trb(e, blk_=blk_, slt=slt):
                                for oc in range(NCC):
                                    ins = e.transpose(
                                        psTk[:, slt * C + oc * 128:slt * C + oc * 128 + 128],
                                        o_ev2[:, oc * 512 + blk_ * 128:oc * 512 + blk_ * 128 + 128],
                                        ident_f[:, :])
                                return ins
                            d = ev_blocks + [o_idf]
                            if ti >= 2:
                                d.append(prev_f[ti - 2])
                            trb = s.add("tensor", em_trb, deps=d)
                            round_trbs.append(trb)
                            wop = s.add("scalar", lambda e, ti=ti: e.activation(
                                wsb[:, :], XR[:, ti * C:(ti + 1) * C], AF.Identity,
                                bias=stat[:, ST_NMR + ti:ST_NMR + ti + 1],
                                scale=stat[:, ST_W1 + ti:ST_W1 + ti + 1]),
                                deps=[rel, oW, oN2, prev_f.get(ti - 1)])
                            fo = s.add("vector", lambda e, ti=ti, slt=slt: e.tensor_add(
                                fin[:, (ti % 2) * C:(ti % 2) * C + C],
                                wsb[:, :], psTk[:, slt * C:slt * C + C]),
                                deps=[wop, trb] +
                                ([out_ops[ti - 2]] if ti >= 2 else []))
                            prev_f[ti] = fo
                            oo = s.add("sync", lambda e, ti=ti: e.dma_start(
                                out=OUT[ti * 128:(ti + 1) * 128, :],
                                in_=fin[:, (ti % 2) * C:(ti % 2) * C + C]),
                                deps=[fo], dma=True, sem="io")
                            out_ops.append(oo)
                        last_round_trbs = round_trbs
                s.add("sync", lambda e: e.nop(), deps=out_ops)
                s.emit()
    return nc


def _tap_out(nc, sems, XR, OUT, deps, ntiles=NT):
    s = Section(nc, sems)
    outs = []
    for i in range(ntiles):
        outs.append(s.add("sync", lambda e, i=i: e.dma_start(
            out=OUT[i * 128:(i + 1) * 128, :], in_=XR[:, i * C:(i + 1) * C]),
            deps=deps, dma=True, sem="io"))
    s.add("sync", lambda e: e.nop(), deps=outs)
    s.emit()


_NC_CACHE = {}


def kernel(L0_attn_x, L0_attn_time_decay, L0_attn_time_first, L0_attn_time_mix,
           L0_attn_Wk, L0_attn_Wv, L0_attn_Wr, L0_attn_Wo,
           L0_ffn_router_prob, L0_ffn_norm_w, L0_ffn_norm_b,
           L0_ffn_q_w, L0_ffn_q_b, L0_ffn_k_w, L0_ffn_k_b,
           L0_ffn_v_w, L0_ffn_v_b, L0_ffn_o_w, L0_ffn_o_b):
    x = np.ascontiguousarray(np.asarray(L0_attn_x, dtype=np.float32))
    B, Tf, D = x.shape
    f32 = np.float32

    def packcols(v):
        return np.ascontiguousarray(np.asarray(v, f32).reshape(NCC, 128).T)

    wkT = np.ascontiguousarray(np.asarray(L0_attn_Wk, f32).T)
    wvT = np.ascontiguousarray(np.asarray(L0_attn_Wv, f32).T)
    wrT = np.ascontiguousarray(np.asarray(L0_attn_Wr, f32).T)
    woT = np.ascontiguousarray(np.asarray(L0_attn_Wo, f32).T)
    qwTf = np.asarray(L0_ffn_q_w, f32).T
    kwTf = np.asarray(L0_ffn_k_w, f32).T
    vwTf = np.asarray(L0_ffn_v_w, f32).T
    owTf = np.asarray(L0_ffn_o_w, f32).T

    key = (BUILD_UPTO, len(RG_GROUPS))
    if key not in _NC_CACHE:
        _NC_CACHE[key] = build_nc()
    nc = _NC_CACHE[key]

    in_maps = []
    for c in range(8):
        b, h = c // 2, c % 2
        xe = np.zeros((T + 1, C), f32)
        if h == 1:
            xe[0] = x[b, T - 1]
        xe[1:] = x[b, h * T:(h + 1) * T]
        ow0 = np.zeros((C, C), f32)
        ow1 = np.zeros((C, C), f32)
        m0 = C >> (2 * h)
        m1 = C >> (2 * h + 1)
        ow0[:m0, :m0] = owTf[:m0, :m0]
        ow1[:m1, :m1] = owTf[:m1, :m1]
        obc = np.zeros((2, C), f32)
        obc[0, :m0] = np.asarray(L0_ffn_o_b, f32)[:m0]
        obc[1, :m1] = np.asarray(L0_ffn_o_b, f32)[:m1]
        ob_cols = np.zeros((128, 16), f32)
        ob_cols[:, 0:8] = obc[0].reshape(NCC, 128).T
        ob_cols[:, 8:16] = obc[1].reshape(NCC, 128).T
        veca = np.zeros((128, 49), f32)
        veca[:, VP_TD:VP_TD + 8] = packcols(L0_attn_time_decay)
        veca[:, VP_TF:VP_TF + 8] = packcols(L0_attn_time_first)
        veca[:, VP_TM:VP_TM + 8] = packcols(L0_attn_time_mix)
        veca[:, VP_NW:VP_NW + 8] = packcols(L0_ffn_norm_w)
        veca[:, VP_NB:VP_NB + 8] = packcols(L0_ffn_norm_b)
        qb = np.asarray(L0_ffn_q_b, f32)[h * 512:(h + 1) * 512]
        kb = np.asarray(L0_ffn_k_b, f32)[h * 512:(h + 1) * 512]
        veca[:, VP_QB:VP_QB + 4] = qb.reshape(4, 128).T
        veca[:, VP_KB:VP_KB + 4] = kb.reshape(4, 128).T
        veca[:, VP_HM] = float(h)
        in_maps.append(dict(
            x_ext=xe,
            wkT=wkT, wvT=wvT, wrT=wrT, woT=woT,
            qwT=np.ascontiguousarray(qwTf[:, h * 512:(h + 1) * 512]),
            kwT=np.ascontiguousarray(kwTf[:, h * 512:(h + 1) * 512]),
            vwT=np.ascontiguousarray(vwTf[:, h * 512:(h + 1) * 512]),
            owT0=ow0, owT1=ow1,
            veca=veca,
            vb_row=np.ascontiguousarray(
                np.asarray(L0_ffn_v_b, f32)[h * 512:(h + 1) * 512].reshape(1, 512)),
            ob_cols=ob_cols,
        ))

    res = run_bass_kernel_spmd(nc, in_maps, core_ids=list(range(8)))
    out = np.empty((B, Tf, D), np.float32)
    for c in range(8):
        b, h = c // 2, c % 2
        out[b, h * T:(h + 1) * T] = res.results[c]["OUT"]
    return out
